# version 08
# baseline (speedup 1.0000x reference)
import contextlib
import numpy as np
import ml_dtypes
import concourse.bass as bass
import concourse.mybir as mybir
from concourse.bass_utils import run_bass_kernel_spmd

F32 = mybir.dt.float32
BF16 = mybir.dt.bfloat16
AF = mybir.ActivationFunctionType
ALU = mybir.AluOpType

NCORES = 8
BPC = 2
S = 4096
T = BPC * S
D = 1024
KT = D // 128
H = 16
HD = 64
LN_EPS = 1e-5
BF = ml_dtypes.bfloat16
SIM_NO_COLLECTIVE = False


def set_dims(s):
    global S, T
    S = s
    T = BPC * s


class Prog:
    def __init__(self, nc, es):
        self.nc = nc
        self.es = es
        self.es_sem = es
        self.c = {}
        self.sems = {}

    def sem(self, name):
        if name not in self.sems:
            self.sems[name] = self.es_sem.enter_context(self.nc.semaphore(name))
            self.c[name] = 0
        return self.sems[name]

    def inc(self, name, k):
        self.c[name] += k
        return self.sems[name]

    def wge(self, e, name, val=None):
        if val is None:
            val = self.c[name]
        if val > 0:
            e.wait_ge(self.sems[name], val)

    def bar(self, e, name, inst):
        inst.then_inc(self.inc(name, 1), 1)
        self.wge(e, name)

    def sb(self, name, shape, dt=F32):
        return self.es.enter_context(self.nc.sbuf_tensor(name, shape, dt))

    def ps(self, name, shape, dt=F32):
        return self.es.enter_context(self.nc.psum_tensor(name, shape, dt))


def host_consts():
    c = {}
    c["c_ones1"] = np.ones((1, 128), dtype=BF)
    c["c_ident"] = np.eye(128, dtype=BF)
    ese = np.zeros((8, 8 * 128), dtype=BF)
    for e in range(8):
        ese[e, e * 128:(e + 1) * 128] = 1
    c["c_esel"] = ese
    c["c_ones1f"] = np.ones((1, 128), dtype=np.float32)
    ii = np.arange(128)
    c["c_dgrid"] = np.where(ii[None, :] >= ii[:, None], ii[None, :] - ii[:, None], 0).astype(np.float32)
    c["c_mgridf"] = (ii[None, :] >= ii[:, None]).astype(np.float32)
    c["c_ramp"] = np.arange(130, dtype=np.float32)[None, :]
    c["c_rrcol"] = (127.0 - ii).astype(np.float32)[:, None]
    c["c_identf"] = np.eye(128, dtype=np.float32)
    c["c_onescol"] = np.ones((128, 1), dtype=BF)
    c["c_onescolf"] = np.ones((128, 1), dtype=np.float32)
    bsl = np.zeros((128, 4), dtype=BF)
    bsl[:, 0] = 1
    bsl[:, 3] = 1
    c["c_bsel"] = bsl
    sel = np.zeros((16, 8 * 128), dtype=BF)
    for p in range(8):
        sel[2 * p, p * 128: p * 128 + 64] = 1
        sel[2 * p + 1, p * 128 + 64: p * 128 + 128] = 1
    c["c_pairsel"] = sel
    return c


def declare_consts(nc):
    return {
        "ones1": nc.dram_tensor("c_ones1", [1, 128], BF16, kind="ExternalInput").ap(),
        "pairsel": nc.dram_tensor("c_pairsel", [16, 8 * 128], BF16, kind="ExternalInput").ap(),
        "ident": nc.dram_tensor("c_ident", [128, 128], BF16, kind="ExternalInput").ap(),
        "esel": nc.dram_tensor("c_esel", [8, 8 * 128], BF16, kind="ExternalInput").ap(),
        "ones1f": nc.dram_tensor("c_ones1f", [1, 128], F32, kind="ExternalInput").ap(),
        "dgrid": nc.dram_tensor("c_dgrid", [128, 128], F32, kind="ExternalInput").ap(),
        "mgridf": nc.dram_tensor("c_mgridf", [128, 128], F32, kind="ExternalInput").ap(),
        "ramp": nc.dram_tensor("c_ramp", [1, 130], F32, kind="ExternalInput").ap(),
        "rrcol": nc.dram_tensor("c_rrcol", [128, 1], F32, kind="ExternalInput").ap(),
        "identf": nc.dram_tensor("c_identf", [128, 128], F32, kind="ExternalInput").ap(),
        "onescol": nc.dram_tensor("c_onescol", [128, 1], BF16, kind="ExternalInput").ap(),
        "onescolf": nc.dram_tensor("c_onescolf", [128, 1], F32, kind="ExternalInput").ap(),
        "bsel": nc.dram_tensor("c_bsel", [128, 4], BF16, kind="ExternalInput").ap(),
    }


def emit_ln_store(p, blk, lnb, xpre, x_out, tok0):
    stat, mv, mneg, rstd, xout_tok, epscol = lnb

    @blk.vector
    def _(e):
        p.wge(e, "va")
        p.bar(e, "va", e.bn_stats(stat[:, 0:6], xpre[:, 0:512]))
        p.bar(e, "va", e.bn_stats(stat[:, 6:12], xpre[:, 512:1024]))
        p.bar(e, "va", e.bn_aggr(mv[:, :], stat[:, :]))
        e.tensor_scalar(mneg[:, :], mv[:, 0:1], -1.0, None, ALU.mult).then_inc(p.inc("va", 1), 1)

    @blk.scalar
    def _(e):
        p.wge(e, "va")
        p.bar(e, "sa", e.activation(rstd[:, :], mv[:, 1:2], AF.Ln, bias=epscol[:, :]))
        e.activation(rstd[:, :], rstd[:, :], AF.Exp, scale=-0.5).then_inc(p.inc("sa", 1), 1)

    @blk.vector
    def _(e):
        p.wge(e, "sa")
        e.tensor_scalar(xout_tok[:, :], xpre[:, :], mneg[:, :], rstd[:, :],
                        ALU.add, ALU.mult).then_inc(p.inc("va", 1), 1)

    @blk.sync
    def _(e):
        p.wge(e, "va")
        e.dma_start(x_out[tok0:tok0 + 128, :], xout_tok[:, :]).then_inc(p.inc("os", 16), 16)



def emit_l0_attn(p, io, dbg=None):
    nc = p.nc
    cs = io["consts"]
    blk = p.es.enter_context(nc.Block())

    wq = [p.sb(f"l0a_wq{k}", [128, D], BF16) for k in range(KT)]
    wk = [p.sb(f"l0a_wk{k}", [128, D], BF16) for k in range(KT)]
    wv = [p.sb(f"l0a_wv{k}", [128, D], BF16) for k in range(KT)]
    wo = [p.sb(f"l0a_wo{k}", [128, D], BF16) for k in range(KT)]
    brow = p.sb("l0a_brow", [1, 4 * D], BF16)
    ones1 = p.sb("l0a_ones1", [1, 128], BF16)
    pairsel = p.sb("l0a_psel", [16, 8 * 128], BF16)
    onescol = p.sb("l0a_onescol", [128, 1], BF16)

    STRIP = min(2048, S)
    NSTRIP = T // STRIP
    NTILE = STRIP // 128
    TPB = S // 128
    xf = [[p.sb(f"l0a_xf{b}_{k}", [128, STRIP], BF16) for k in range(KT)] for b in range(2)]

    q_tok = [p.sb(f"l0a_qtok{i}", [128, D], BF16) for i in range(2)]
    k_tok = [p.sb(f"l0a_ktok{i}", [128, D], BF16) for i in range(2)]
    v_tok = [p.sb(f"l0a_vtok{i}", [128, D], BF16) for i in range(2)]
    xm_ = p.sb("l0a_xm", [128, 512], F32)
    ee = [p.sb(f"l0a_ee{i}", [128, 512], F32) for i in range(2)]
    rl = [p.sb(f"l0a_rl{i}", [128, 512], F32) for i in range(2)]

    kv_sb = p.sb("l0a_kvsb", [128, 8 * 64], BF16)
    kvacc = p.sb("l0a_kvacc", [128, 8 * 64], F32)
    ksacc = p.sb("l0a_ksacc", [128, 8], F32)
    ks_sb = [p.sb(f"l0a_kssb{k}", [128, 16], BF16) for k in range(KT)]
    qf = [[p.sb(f"l0a_qf{b}_{k}", [128, 512], BF16) for k in range(KT)] for b in range(2)]
    att = [p.sb(f"l0a_att{k}", [128, 512], BF16) for k in range(KT)]
    rz_bf = p.sb("l0a_rzbf", [16, 512], BF16)
    rz_f = p.sb("l0a_rzf", [16, 512], F32)
    rzb_sb = [p.sb(f"l0a_rzbsb{i}", [128, 512], BF16) for i in range(2)]
    x0_tok = p.sb("l0a_x0tok", [128, D], BF16)
    xpre = p.sb("l0a_xpre", [128, D], F32)
    lnb = (p.sb("l0a_stat", [128, 12], F32), p.sb("l0a_mv", [128, 2], F32),
           p.sb("l0a_mneg", [128, 1], F32), p.sb("l0a_rstd", [128, 1], F32),
           p.sb("l0a_xouttok", [128, D], BF16), p.sb("l0a_eps", [128, 1], F32))

    bank = [p.ps(f"l0a_bank{i}", [128, 512], F32) for i in range(8)]
    pkv = bank[6]
    pks = bank[7]

    for sname in ["ld", "xa0", "xa1", "qa0", "qa1", "xo", "os", "ta", "va", "sa", "ga", "so0", "so1"]:
        p.sem(sname)

    @blk.gpsimd
    def _(e):
        for j in range(T // 128):
            e.dma_start(io["xbf"][j * 128:(j + 1) * 128, :],
                        io["x_in"][j * 128:(j + 1) * 128, :]).then_inc(p.inc("ga", 16), 16)
        for i, w in enumerate(["Wq", "Wk", "Wv", "Wo"]):
            for j in range(D // 128):
                e.dma_start(io["wsc"][i * D + j * 128:i * D + (j + 1) * 128, :],
                            io[w][j * 128:(j + 1) * 128, :]).then_inc(p.inc("ga", 16), 16)
        for i, b in enumerate(["bq", "bk", "bv", "bo"]):
            e.dma_start(brow[:1, i * D:(i + 1) * D], io[b][None, :]).then_inc(p.inc("ga", 16), 16)

    ga_all = p.c["ga"]

    @blk.sync
    def _(e):
        e.dma_start(ones1[:1, :], cs["ones1"][:, :]).then_inc(p.inc("ld", 16), 16)
        e.dma_start(pairsel[:, :], cs["pairsel"][:, :]).then_inc(p.inc("ld", 16), 16)

    @blk.scalar
    def _(e):
        p.wge(e, "ga", ga_all)
        for k in range(KT):
            for wi, wt in enumerate([wq, wk, wv, wo]):
                e.dma_start(wt[k][:, :], io["wsc"][wi * D:(wi + 1) * D, k * 128:(k + 1) * 128],
                            transpose=True).then_inc(p.inc("ld", 16), 16)

    ld_all = p.c["ld"]

    @blk.vector
    def _(e):
        e.memset(lnb[5][:, :], LN_EPS).then_inc(p.inc("va", 1), 1)
        e.memset(onescol[:, :], 1.0).then_inc(p.inc("va", 1), 1)
        for k in range(KT):
            e.memset(ks_sb[k][:, :], 0.0).then_inc(p.inc("va", 1), 1)

    va_init = p.c["va"]

    def proj_chunk(e, bi, xf_s, ti, wt, n, brow_off):
        for k in range(KT):
            e.matmul(bank[bi][:, :], xf_s[k][:, ti * 128:(ti + 1) * 128],
                     wt[k][:, n * 512:(n + 1) * 512], start=(k == 0), stop=False)
        return e.matmul(bank[bi][:, :], ones1[:1, :],
                        brow[:1, brow_off + n * 512: brow_off + (n + 1) * 512],
                        start=False, stop=True)

    strip_loaded = {}
    strip_consumed = {}
    sa_free = {}
    va_adds = {}
    kv_accum = {}
    sa_vtok = {}
    ta_kv = {}
    so_q = {}

    for b in range(BPC):
        for g in range(b * TPB, (b + 1) * TPB):
            s, ti = divmod(g, NTILE)
            par = g % 2
            if ti == 0:
                sb_ = s % 2
                @blk.scalar
                def _(e, s=s, sb_=sb_):
                    if s == 0:
                        p.wge(e, "ga", ga_all)
                    if s >= 2:
                        p.wge(e, "ta", strip_consumed[s - 2])
                    for k in range(KT):
                        e.dma_start(xf[sb_][k][:, :],
                                    io["xbf"][s * STRIP:(s + 1) * STRIP, k * 128:(k + 1) * 128],
                                    transpose=True).then_inc(p.inc(f"xa{s % 2}", 16), 16)
                strip_loaded[s] = (f"xa{s % 2}", p.c[f"xa{s % 2}"])

                if dbg is not None and isinstance(dbg, dict) and "xf" in dbg and s == 0:
                    @blk.sync
                    def _(e, sb_=sb_):
                        p.wge(e, "xa0", strip_loaded[0][1])
                        for k in range(KT):
                            e.dma_start(dbg["xf"][k, :, :], xf[sb_][k][:, :]).then_inc(p.inc("os", 16), 16)

            @blk.tensor
            def _(e, s=s, ti=ti, g=g):
                if g == 0:
                    p.wge(e, "ld", ld_all)
                    p.wge(e, "va", va_init)
                if g == TPB:
                    p.wge(e, "va")
                    p.wge(e, "sa")
                p.wge(e, strip_loaded[s][0], strip_loaded[s][1])
                if g >= 1:
                    p.wge(e, "sa", sa_free[g - 1])
                proj_chunk(e, 0, xf[s % 2], ti, wq, 0, 0)
                proj_chunk(e, 1, xf[s % 2], ti, wq, 1, 0)
                proj_chunk(e, 2, xf[s % 2], ti, wk, 0, D)
                proj_chunk(e, 3, xf[s % 2], ti, wk, 1, D)
                proj_chunk(e, 4, xf[s % 2], ti, wv, 0, 2 * D)
                proj_chunk(e, 5, xf[s % 2], ti, wv, 1, 2 * D).then_inc(p.inc("ta", 1), 1)

            ta_proj = p.c["ta"]
            va_base = p.c["va"]
            sa_ci = []

            @blk.scalar
            def _(e, ta_proj=ta_proj, g=g, par=par, va_base=va_base):
                p.wge(e, "ta", ta_proj)
                p.wge(e, "va", va_base)
                if g >= 2:
                    p.wge(e, "ta", ta_kv[g - 2])
                for ci, bi in enumerate([0, 1, 2, 3]):
                    if ci >= 2:
                        p.wge(e, "va", va_base + ci - 1)
                    p.wge(e, "sa")
                    p.bar(e, "sa", e.activation(xm_[:, :], bank[bi][:, :], AF.Relu, scale=-1.0))
                    e.activation(ee[ci % 2][:, :], xm_[:, :], AF.Exp, scale=-1.0).then_inc(p.inc("sa", 1), 1)
                    e.activation(rl[ci % 2][:, :], bank[bi][:, :], AF.Relu).then_inc(p.inc("sa", 1), 1)
                    sa_ci.append(p.c["sa"])
                e.activation(v_tok[par][:, 0:512], bank[4][:, :], AF.Copy).then_inc(p.inc("sa", 1), 1)
                e.activation(v_tok[par][:, 512:1024], bank[5][:, :], AF.Copy).then_inc(p.inc("sa", 1), 1)

            sa_free[g] = p.c["sa"]
            sa_vtok[g] = p.c["sa"]

            @blk.vector
            def _(e, g=g, par=par, sa_ci=sa_ci):
                if g >= 2:
                    p.wge(e, "ta", ta_kv[g - 2])
                    p.wge(e, so_q[g - 2][0], so_q[g - 2][1])
                for ci in range(4):
                    p.wge(e, "sa", sa_ci[ci])
                    dst = q_tok[par] if ci < 2 else k_tok[par]
                    n = ci % 2
                    e.tensor_tensor(dst[:, n * 512:(n + 1) * 512], ee[ci % 2][:, :],
                                    rl[ci % 2][:, :], ALU.add).then_inc(p.inc("va", 1), 1)

            va_adds[g] = p.c["va"]

            @blk.tensor
            def _(e, g=g, par=par):
                p.wge(e, "va", va_adds[g])
                p.wge(e, "sa", sa_vtok[g])
                if g >= 1:
                    p.wge(e, "va", kv_accum[g - 1])
                inst = None
                for h in range(H):
                    pp, hi = divmod(h, 2)
                    e.matmul(pkv[hi * 64:(hi + 1) * 64, pp * 64:(pp + 1) * 64],
                             k_tok[par][:, h * 64:(h + 1) * 64],
                             v_tok[par][:, h * 64:(h + 1) * 64],
                             start=True, stop=True, skip_group_check=True)
                    inst = e.matmul(pks[hi * 64:(hi + 1) * 64, pp:pp + 1],
                                    k_tok[par][:, h * 64:(h + 1) * 64],
                                    onescol[:, :], start=True, stop=True,
                                    skip_group_check=True)
                inst.then_inc(p.inc("ta", 1), 1)

            ta_kv[g] = p.c["ta"]
            if ti == NTILE - 1:
                strip_consumed[s] = p.c["ta"]

            @blk.vector
            def _(e, g=g, first=(g % TPB == 0)):
                p.wge(e, "ta", ta_kv[g])
                if first:
                    e.tensor_copy(kvacc[:, :], pkv[:, :]).then_inc(p.inc("va", 1), 1)
                    e.tensor_copy(ksacc[:, :], pks[:, 0:8]).then_inc(p.inc("va", 1), 1)
                else:
                    e.tensor_tensor(kvacc[:, :], kvacc[:, :], pkv[:, :], ALU.add).then_inc(p.inc("va", 1), 1)
                    e.tensor_tensor(ksacc[:, :], ksacc[:, :], pks[:, 0:8], ALU.add).then_inc(p.inc("va", 1), 1)

            kv_accum[g] = p.c["va"]

            @blk.sync
            def _(e, g=g, par=par):
                p.wge(e, "va", va_adds[g])
                e.dma_start(io["qsc"][g * 128:(g + 1) * 128, :], q_tok[par][:, :]).then_inc(p.inc(f"so{par}", 16), 16)

            so_q[g] = (f"so{par}", p.c[f"so{par}"])

        @blk.vector
        def _(e):
            p.wge(e, "va", kv_accum[(b + 1) * TPB - 1])
            e.tensor_copy(kv_sb[:, :], kvacc[:, :]).then_inc(p.inc("va", 1), 1)
            for k in range(KT):
                e.tensor_copy(ks_sb[k][0:64, 2 * k:2 * k + 1], ksacc[0:64, k:k + 1]).then_inc(p.inc("va", 1), 1)
            for k in range(KT):
                e.tensor_copy(ks_sb[k][64:128, 2 * k + 1:2 * k + 2], ksacc[64:128, k:k + 1]).then_inc(p.inc("va", 1), 1)

        va_kvsb = p.c["va"]

        if dbg is not None:
            dbgkv = dbg["kv"] if isinstance(dbg, dict) else dbg

            @blk.sync
            def _(e, b=b, va_kvsb=va_kvsb, dbgkv=dbgkv):
                p.wge(e, "va", va_kvsb)
                e.dma_start(dbgkv[b, :, 0:512], kvacc[:, :]).then_inc(p.inc("os", 16), 16)
                e.dma_start(dbgkv[b, :, 512:520], ksacc[:, :]).then_inc(p.inc("os", 16), 16)

        NCH = S // 512
        rz_done = {}
        copy_done = {}
        att_done = {}
        qf_used = {}
        wo_evac = 0
        for c in range(NCH):
            cpar = c % 2
            tok0 = b * S + c * 512

            @blk.scalar
            def _(e, b=b, c=c, cpar=cpar, tok0=tok0):
                p.wge(e, "so0")
                p.wge(e, "so1")
                if c >= 2:
                    p.wge(e, "ta", qf_used[c - 2])
                for k in range(KT):
                    e.dma_start(qf[cpar][k][:, :],
                                io["qsc"][tok0:tok0 + 512, k * 128:(k + 1) * 128],
                                transpose=True).then_inc(p.inc(f"qa{cpar}", 16), 16)

            qa_now = (f"qa{cpar}", p.c[f"qa{cpar}"])

            @blk.tensor
            def _(e, c=c, cpar=cpar, qa_now=qa_now):
                p.wge(e, qa_now[0], qa_now[1])
                p.wge(e, "va", va_kvsb)
                p.wge(e, "sa")
                if c >= 1:
                    p.wge(e, "va", rz_done[c - 1])
                for k in range(KT - 1):
                    e.matmul(bank[0][0:16, :], ks_sb[k][:, :], qf[cpar][k][:, :],
                             start=(k == 0), stop=False)
                e.matmul(bank[0][0:16, :], ks_sb[KT - 1][:, :], qf[cpar][KT - 1][:, :],
                         start=False, stop=True).then_inc(p.inc("ta", 1), 1)

            ta_z = p.c["ta"]

            @blk.vector
            def _(e, ta_z=ta_z):
                p.wge(e, "ta", ta_z)
                p.bar(e, "va", e.tensor_scalar(rz_f[:, :], bank[0][0:16, :], 1e-6, None, ALU.add))
                p.bar(e, "va", e.reciprocal(rz_f[:, :], rz_f[:, :]))
                e.tensor_copy(rz_bf[:, :], rz_f[:, :]).then_inc(p.inc("va", 1), 1)

            rz_done[c] = p.c["va"]
            va_rz = p.c["va"]

            for pp in range(8):
                pb_bank = bank[1]
                po_bank = bank[2 + (pp % 2)]

                @blk.tensor
                def _(e, pp=pp, c=c, va_rz=va_rz):
                    p.wge(e, "va", va_rz)
                    if (c, pp - 1) in copy_done:
                        p.wge(e, "va", copy_done[(c, pp - 1)])
                    elif c >= 1:
                        p.wge(e, "va", copy_done[(c - 1, 7)])
                    e.matmul(pb_bank[:, :], pairsel[:, pp * 128:(pp + 1) * 128], rz_bf[:, :],
                             start=True, stop=True).then_inc(p.inc("ta", 1), 1)

                ta_pb = p.c["ta"]

                @blk.vector
                def _(e, ta_pb=ta_pb, pp=pp):
                    p.wge(e, "ta", ta_pb)
                    e.tensor_copy(rzb_sb[pp % 2][:, :], pb_bank[:, :]).then_inc(p.inc("va", 1), 1)

                copy_done[(c, pp)] = p.c["va"]
                va_rzb = p.c["va"]

                @blk.tensor
                def _(e, pp=pp, c=c, cpar=cpar, va_rzb=va_rzb):
                    p.wge(e, "va", va_rzb)
                    if (c, pp - 2) in att_done:
                        p.wge(e, "va", att_done[(c, pp - 2)])
                    elif c >= 1:
                        p.wge(e, "va", att_done[(c - 1, 6 + pp)])
                    e.matmul(po_bank[0:64, :], kv_sb[0:64, pp * 64:(pp + 1) * 64],
                             qf[cpar][pp][0:64, :], start=True, stop=True)
                    e.matmul(po_bank[64:128, :], kv_sb[64:128, pp * 64:(pp + 1) * 64],
                             qf[cpar][pp][64:128, :], start=True, stop=True).then_inc(p.inc("ta", 1), 1)

                ta_po = p.c["ta"]

                @blk.vector
                def _(e, pp=pp, ta_po=ta_po):
                    p.wge(e, "ta", ta_po)
                    e.tensor_tensor(att[pp][:, :], po_bank[:, :], rzb_sb[pp % 2][:, :],
                                    ALU.mult).then_inc(p.inc("va", 1), 1)

                att_done[(c, pp)] = p.c["va"]

            va_att = p.c["va"]
            qf_used[c] = p.c["ta"]

            for j in range(4):
                tt0 = tok0 + j * 128

                @blk.sync
                def _(e, tt0=tt0, wo_evac=wo_evac):
                    if wo_evac:
                        p.wge(e, "va", wo_evac)
                    e.dma_start(x0_tok[:, :], io["xbf"][tt0:tt0 + 128, :]).then_inc(p.inc("xo", 16), 16)

                ld_x0 = p.c["xo"]

                @blk.tensor
                def _(e, j=j, va_att=va_att, wo_evac=wo_evac):
                    p.wge(e, "va", va_att)
                    if wo_evac:
                        p.wge(e, "va", wo_evac)
                    for n in range(2):
                        for k in range(KT):
                            e.matmul(bank[4 + n][:, :], att[k][:, j * 128:(j + 1) * 128],
                                     wo[k][:, n * 512:(n + 1) * 512], start=(k == 0), stop=False)
                        inst = e.matmul(bank[4 + n][:, :], ones1[:1, :],
                                        brow[:1, 3 * D + n * 512: 3 * D + (n + 1) * 512],
                                        start=False, stop=True)
                    inst.then_inc(p.inc("ta", 1), 1)

                ta_wo = p.c["ta"]

                @blk.vector
                def _(e, ta_wo=ta_wo, ld_x0=ld_x0):
                    p.wge(e, "ta", ta_wo)
                    p.wge(e, "xo", ld_x0)
                    p.wge(e, "os")
                    e.tensor_tensor(xpre[:, 0:512], x0_tok[:, 0:512], bank[4][:, :],
                                    ALU.add).then_inc(p.inc("va", 1), 1)
                    e.tensor_tensor(xpre[:, 512:1024], x0_tok[:, 512:1024], bank[5][:, :],
                                    ALU.add).then_inc(p.inc("va", 1), 1)

                wo_evac = p.c["va"]
                emit_ln_store(p, blk, lnb, xpre, io["x_out"], tt0)

    @blk.sync
    def _(e):
        p.wge(e, "so0")
        p.wge(e, "so1")
        p.wge(e, "os")
    return



def emit_l0_ffn(p, io):
    nc = p.nc
    cs = io["consts"]
    blk = p.es.enter_context(nc.Block())
    E, HF = 8, 512

    w1 = [p.sb(f"l0f_w1_{e}", [128, KT * HF], BF16) for e in range(E)]
    w2 = [p.sb(f"l0f_w2_{e}", [128, 4 * D], BF16) for e in range(E)]
    ws1 = p.sb("l0f_ws1", [128, KT * HF], BF16)
    ws2 = p.sb("l0f_ws2", [128, 4 * D], BF16)
    gw = p.sb("l0f_gw", [128, KT * 8], BF16)
    b1c = p.sb("l0f_b1c", [128, E * 4], F32)
    bs1c = p.sb("l0f_bs1c", [128, 4], F32)
    b2m = p.sb("l0f_b2m", [8, D], BF16)
    bs2r = p.sb("l0f_bs2r", [1, D], BF16)
    ones1 = p.sb("l0f_ones1", [1, 128], BF16)
    esel = p.sb("l0f_esel", [8, 8 * 128], BF16)
    rs_bf = p.sb("l0f_rsbf", [1, 1], BF16)
    rs_col = p.sb("l0f_rs", [128, 1], F32)
    idn = p.sb("l0f_idn", [128, 128], BF16)

    STRIP = min(1024, S)
    NSTRIP = T // STRIP
    NTILE = STRIP // 128
    xf = [[p.sb(f"l0f_xf{b}_{k}", [128, STRIP], BF16) for k in range(KT)] for b in range(2)]

    sc_sb = p.sb("l0f_scsb", [8, 128], BF16)
    s_tok = p.sb("l0f_stok", [128, 8], F32)
    m1 = p.sb("l0f_m1", [128, 1], F32)
    m2 = p.sb("l0f_m2", [128, 1], F32)
    msk1 = p.sb("l0f_msk1", [128, 8], F32)
    msk2 = p.sb("l0f_msk2", [128, 8], F32)
    smsk = p.sb("l0f_smsk", [128, 8], F32)
    r12 = p.sb("l0f_r12", [128, 1], F32)
    g_tok = p.sb("l0f_gtok", [128, 8], BF16)
    gT = p.sb("l0f_gT", [8, 128], BF16)
    gb_sb = [p.sb(f"l0f_gb{i}", [128, 128], BF16) for i in range(2)]
    h_sb = [p.sb(f"l0f_hsb{i}", [128, 4 * 128], BF16) for i in range(2)]
    x0_tok = p.sb("l0f_x0tok", [128, D], BF16)
    xpre = p.sb("l0f_xpre", [128, D], F32)
    lnb = (p.sb("l0f_stat", [128, 12], F32), p.sb("l0f_mv", [128, 2], F32),
           p.sb("l0f_mneg", [128, 1], F32), p.sb("l0f_rstd", [128, 1], F32),
           p.sb("l0f_xouttok", [128, D], BF16), p.sb("l0f_eps", [128, 1], F32))

    acc = [p.ps(f"l0f_acc{i}", [128, 512], F32) for i in range(2)]
    hps = [p.ps(f"l0f_hps{i}", [128, 512], F32) for i in range(2)]
    gbp = [p.ps(f"l0f_gbp{i}", [128, 128], F32) for i in range(2)]
    scp = p.ps("l0f_scp", [128, 512], F32)
    trp = p.ps("l0f_trp", [128, 128], BF16)

    for sname in ["ld", "xa0", "xa1", "xo", "os", "ta", "va", "sa", "ga", "so"]:
        p.sem(sname)

    @blk.gpsimd
    def _(e):
        for ee in range(E):
            e.dma_start(w1[ee][:, :].rearrange("p (k h) -> p k h", k=KT),
                        io["W1"][ee, :, :].rearrange("(k p) h -> p k h", p=128)).then_inc(p.inc("ga", 16), 16)
            e.dma_start(w2[ee][:, :].rearrange("p (k d) -> p k d", k=4),
                        io["W2"][ee, :, :].rearrange("(k p) d -> p k d", p=128)).then_inc(p.inc("ga", 16), 16)
        e.dma_start(ws1[:, :].rearrange("p (k h) -> p k h", k=KT),
                    io["Ws1"].rearrange("(k p) h -> p k h", p=128)).then_inc(p.inc("ga", 16), 16)
        e.dma_start(ws2[:, :].rearrange("p (k d) -> p k d", k=4),
                    io["Ws2"].rearrange("(k p) d -> p k d", p=128)).then_inc(p.inc("ga", 16), 16)
        e.dma_start(gw[:, :].rearrange("p (k ee) -> p k ee", k=KT),
                    io["gate_W"].rearrange("(k p) ee -> p k ee", p=128)).then_inc(p.inc("ga", 16), 16)
        e.dma_start(b2m[:, :], io["b2"][:, :]).then_inc(p.inc("ga", 16), 16)
        e.dma_start(bs2r[:1, :], io["bs2"][None, :]).then_inc(p.inc("ga", 16), 16)
        e.dma_start(rs_bf[:1, :], io["route_scale"][None, :]).then_inc(p.inc("ga", 16), 16)

    ga_all = p.c["ga"]

    @blk.sync
    def _(e):
        e.dma_start(ones1[:1, :], cs["ones1"][:, :]).then_inc(p.inc("ld", 16), 16)
        e.dma_start(esel[:, :], cs["esel"][:, :]).then_inc(p.inc("ld", 16), 16)
        e.dma_start(idn[:, :], cs["ident"][:, :]).then_inc(p.inc("ld", 16), 16)
        for ee in range(E):
            for m in range(4):
                e.dma_start(b1c[:, ee * 4 + m:ee * 4 + m + 1],
                            io["b1"][ee, m * 128:(m + 1) * 128].rearrange("(q o) -> q o", o=1)).then_inc(p.inc("ld", 16), 16)
        for m in range(4):
            e.dma_start(bs1c[:, m:m + 1],
                        io["bs1"][m * 128:(m + 1) * 128].rearrange("(q o) -> q o", o=1)).then_inc(p.inc("ld", 16), 16)

    ld_all = p.c["ld"]

    @blk.vector
    def _(e):
        e.memset(lnb[5][:, :], LN_EPS).then_inc(p.inc("va", 1), 1)

    @blk.tensor
    def _(e):
        p.wge(e, "ga", ga_all)
        p.wge(e, "ld", ld_all)
        e.matmul(gbp[0][:, 0:1], ones1[:1, :], rs_bf[:1, :], start=True, stop=True).then_inc(p.inc("ta", 1), 1)

    @blk.vector
    def _(e):
        p.wge(e, "ta")
        e.tensor_copy(rs_col[:, :], gbp[0][:, 0:1]).then_inc(p.inc("va", 1), 1)

    va_init = p.c["va"]

    strip_loaded = {}
    strip_consumed = {}

    sig_done = 0
    gt_evac = 0
    gb_evac = {}
    h_used = {}
    h_evac = {}
    acc_read = 0

    for g in range(T // 128):
        s, ti = divmod(g, NTILE)
        if ti == 0:
            sb_ = s % 2
            @blk.scalar
            def _(e, s=s, sb_=sb_):
                if s >= 2:
                    p.wge(e, "ta", strip_consumed[s - 2])
                for k in range(KT):
                    e.dma_start(xf[sb_][k][:, :],
                                io["x_in"][s * STRIP:(s + 1) * STRIP, k * 128:(k + 1) * 128],
                                transpose=True).then_inc(p.inc(f"xa{s % 2}", 16), 16)
            strip_loaded[s] = (f"xa{s % 2}", p.c[f"xa{s % 2}"])

        xs = xf[s % 2]
        tsl = slice(ti * 128, (ti + 1) * 128)

        @blk.tensor
        def _(e, g=g, s=s, tsl=tsl, sig_done=sig_done):
            if g == 0:
                p.wge(e, "va", va_init)
            p.wge(e, strip_loaded[s][0], strip_loaded[s][1])
            if g >= 1:
                p.wge(e, "sa", sig_done)
            inst = None
            for k in range(KT):
                inst = e.matmul(scp[0:8, 0:128], gw[:, k * 8:(k + 1) * 8], xs[k][:, tsl],
                                start=(k == 0), stop=(k == KT - 1))
            inst.then_inc(p.inc("ta", 1), 1)

        ta_sc = p.c["ta"]

        @blk.vector
        def _(e, ta_sc=ta_sc, g=g, gt_evac=gt_evac):
            p.wge(e, "ta", ta_sc)
            e.tensor_copy(sc_sb[:, :], scp[0:8, 0:128]).then_inc(p.inc("va", 1), 1)

        va_sc = p.c["va"]

        @blk.tensor
        def _(e, va_sc=va_sc, g=g, gt_evac=gt_evac):
            p.wge(e, "va", va_sc)
            if g >= 1:
                p.wge(e, "va", gt_evac)
            e.transpose(trp[:, 0:8], sc_sb[:, :], idn[0:8, 0:8]).then_inc(p.inc("ta", 1), 1)

        ta_tr = p.c["ta"]

        @blk.scalar
        def _(e, ta_tr=ta_tr):
            p.wge(e, "ta", ta_tr)
            e.activation(s_tok[:, :], trp[:, 0:8], AF.Sigmoid).then_inc(p.inc("sa", 1), 1)

        sig_done = p.c["sa"]

        @blk.vector
        def _(e):
            p.wge(e, "sa", sig_done)
            p.bar(e, "va", e.tensor_scalar(s_tok[:, :], s_tok[:, :], rs_col[:, :], None, ALU.mult))
            p.bar(e, "va", e.reduce_max(m1[:, :], s_tok[:, :], axis=mybir.AxisListType.X))
            p.bar(e, "va", e.tensor_scalar(msk1[:, :], s_tok[:, :], m1[:, :], None, ALU.is_ge))
            p.bar(e, "va", e.scalar_tensor_tensor(smsk[:, :], msk1[:, :], -1e30, s_tok[:, :], ALU.mult, ALU.add))
            p.bar(e, "va", e.reduce_max(m2[:, :], smsk[:, :], axis=mybir.AxisListType.X))
            p.bar(e, "va", e.tensor_scalar(msk2[:, :], smsk[:, :], m2[:, :], None, ALU.is_ge))
            p.bar(e, "va", e.tensor_tensor(r12[:, :], m1[:, :], m2[:, :], ALU.add))
            p.bar(e, "va", e.reciprocal(r12[:, :], r12[:, :]))
            p.bar(e, "va", e.tensor_tensor(msk1[:, :], msk1[:, :], msk2[:, :], ALU.add))
            p.bar(e, "va", e.tensor_tensor(msk1[:, :], msk1[:, :], s_tok[:, :], ALU.mult))
            e.tensor_scalar(g_tok[:, :], msk1[:, :], r12[:, :], None, ALU.mult).then_inc(p.inc("va", 1), 1)

        va_g = p.c["va"]

        @blk.tensor
        def _(e, va_g=va_g):
            p.wge(e, "va", va_g)
            e.transpose(trp[0:8, 0:128], g_tok[:, :], idn[:, :]).then_inc(p.inc("ta", 1), 1)

        ta_gt = p.c["ta"]

        @blk.vector
        def _(e, ta_gt=ta_gt):
            p.wge(e, "ta", ta_gt)
            e.tensor_copy(gT[:, :], trp[0:8, 0:128]).then_inc(p.inc("va", 1), 1)

        gt_evac = p.c["va"]

        n_acc_mm = 0
        first_acc = True
        for ei in range(E + 1):
            par = ei % 2
            is_sh = (ei == E)
            wgt1 = ws1 if is_sh else w1[ei]
            wgt2 = ws2 if is_sh else w2[ei]

            @blk.tensor
            def _(e, ei=ei, par=par, tsl=tsl, wgt1=wgt1, g=g):
                key = (par,)
                if (g, ei - 2) in h_used or (g > 0 and ei < 2):
                    pass
                if h_evac.get(par, 0):
                    p.wge(e, "sa", h_evac[par][0])
                    p.wge(e, "va", h_evac[par][1])
                inst = None
                for m in range(4):
                    for k in range(KT):
                        inst = e.matmul(hps[par][:, m * 128:(m + 1) * 128],
                                        wgt1[:, k * HF + m * 128: k * HF + (m + 1) * 128],
                                        xs[k][:, tsl], start=(k == 0), stop=(k == KT - 1))
                inst.then_inc(p.inc("ta", 1), 1)

            ta_h = p.c["ta"]

            if not is_sh:
                gt_now = gt_evac
                @blk.tensor
                def _(e, ei=ei, par=par, gt_now=gt_now):
                    p.wge(e, "va", gt_now)
                    if gb_evac.get(par, 0):
                        p.wge(e, "va", gb_evac[par])
                    e.matmul(gbp[par][:, :], esel[:, ei * 128:(ei + 1) * 128], gT[:, :],
                             start=True, stop=True).then_inc(p.inc("ta", 1), 1)

                ta_gb = p.c["ta"]

                @blk.vector
                def _(e, par=par, ta_gb=ta_gb):
                    p.wge(e, "ta", ta_gb)
                    e.tensor_copy(gb_sb[par][:, :], gbp[par][:, :]).then_inc(p.inc("va", 1), 1)

                gb_evac[par] = p.c["va"]

            @blk.scalar
            def _(e, ei=ei, par=par, ta_h=ta_h, is_sh=is_sh):
                p.wge(e, "ta", ta_h)
                if h_used.get(par, 0):
                    p.wge(e, "ta", h_used[par])
                for m in range(4):
                    bias = bs1c[:, m:m + 1] if is_sh else b1c[:, ei * 4 + m:ei * 4 + m + 1]
                    e.activation(h_sb[par][:, m * 128:(m + 1) * 128],
                                 hps[par][:, m * 128:(m + 1) * 128], AF.Gelu,
                                 bias=bias).then_inc(p.inc("sa", 1), 1)

            sa_h = p.c["sa"]

            if not is_sh:
                @blk.vector
                def _(e, par=par, sa_h=sa_h):
                    p.wge(e, "sa", sa_h)
                    p.wge(e, "va", gb_evac[par])
                    for m in range(4):
                        e.tensor_tensor(h_sb[par][:, m * 128:(m + 1) * 128],
                                        h_sb[par][:, m * 128:(m + 1) * 128],
                                        gb_sb[par][:, :], ALU.mult).then_inc(p.inc("va", 1), 1)
                h_ready = ("va", p.c["va"])
            else:
                h_ready = ("sa", sa_h)

            h_evac[par] = (sa_h, p.c["va"])

            @blk.tensor
            def _(e, ei=ei, par=par, h_ready=h_ready, first_acc=first_acc, g=g, acc_read=acc_read):
                p.wge(e, h_ready[0], h_ready[1])
                if first_acc and g >= 1:
                    p.wge(e, "va", acc_read)
                inst = None
                for n in range(2):
                    for k2 in range(4):
                        inst = e.matmul(acc[n][:, :], h_sb[par][:, k2 * 128:(k2 + 1) * 128],
                                        wgt2[:, k2 * D + n * 512: k2 * D + n * 512 + 512],
                                        start=(first_acc and k2 == 0),
                                        stop=False)
                inst.then_inc(p.inc("ta", 1), 1)

            h_used[par] = p.c["ta"]
            first_acc = False

        @blk.tensor
        def _(e, tsl=tsl, gt_now=gt_evac):
            p.wge(e, "va", gt_now)
            inst = None
            for n in range(2):
                e.matmul(acc[n][:, :], gT[:, :], b2m[:, n * 512:(n + 1) * 512], start=False, stop=False)
                inst = e.matmul(acc[n][:, :], ones1[:1, :], bs2r[:1, n * 512:(n + 1) * 512],
                                start=False, stop=True)
            inst.then_inc(p.inc("ta", 1), 1)

        ta_acc = p.c["ta"]
        if ti == NTILE - 1:
            strip_consumed[s] = p.c["ta"]

        @blk.sync
        def _(e, g=g, acc_read=acc_read):
            if acc_read:
                p.wge(e, "va", acc_read)
            e.dma_start(x0_tok[:, :], io["x_in"][g * 128:(g + 1) * 128, :]).then_inc(p.inc("xo", 16), 16)

        ld_x0 = p.c["xo"]

        @blk.vector
        def _(e, ta_acc=ta_acc, ld_x0=ld_x0):
            p.wge(e, "ta", ta_acc)
            p.wge(e, "xo", ld_x0)
            p.wge(e, "os")
            e.tensor_tensor(xpre[:, 0:512], x0_tok[:, 0:512], acc[0][:, :], ALU.add).then_inc(p.inc("va", 1), 1)
            e.tensor_tensor(xpre[:, 512:1024], x0_tok[:, 512:1024], acc[1][:, :], ALU.add).then_inc(p.inc("va", 1), 1)

        acc_read = p.c["va"]
        emit_ln_store(p, blk, lnb, xpre, io["x_out"], g * 128)

    @blk.sync
    def _(e):
        p.wge(e, "os")
    return



def emit_l1_attn(p, io):
    nc = p.nc
    cs = io["consts"]
    blk = p.es.enter_context(nc.Block())
    C = 128
    TPB = S // C

    wk = [p.sb(f"l1a_wk{k}", [128, D], BF16) for k in range(KT)]
    wv = [p.sb(f"l1a_wv{k}", [128, D], BF16) for k in range(KT)]
    wr = [p.sb(f"l1a_wr{k}", [128, D], BF16) for k in range(KT)]
    wo = [p.sb(f"l1a_wo{k}", [128, D], BF16) for k in range(KT)]
    ones1 = p.sb("l1a_ones1", [1, 128], BF16)
    ones1f = p.sb("l1a_ones1f", [1, 128], F32)
    idn = p.sb("l1a_idn", [128, 128], BF16)
    idnf = p.sb("l1a_idnf", [128, 128], F32)
    dgrid = p.sb("l1a_dgrid", [128, 128], F32)
    mgrid = p.sb("l1a_mgrid", [128, 128], F32)
    ramp = p.sb("l1a_ramp", [1, 130], F32)
    rrcol = p.sb("l1a_rrcol", [128, 1], F32)

    STRIP = min(2048, S)
    NSTRIP = T // STRIP
    NTILE = STRIP // 128
    xf = [[p.sb(f"l1a_xf{b}_{k}", [128, STRIP], BF16) for k in range(KT)] for b in range(2)]

    lamv = p.sb("l1a_lamv", [16, 1], F32)
    lnl = p.sb("l1a_lnl", [16, 1], F32)
    lrow = p.sb("l1a_lrow", [1, 16], F32)
    lc = p.sb("l1a_lc", [128, 1], F32)
    rampl = p.sb("l1a_rampl", [1, 128], F32)
    lamtile = [p.sb(f"l1a_lt{t}", [128, 128], BF16) for t in range(KT)]
    ptile = [p.sb(f"l1a_pt{h}", [128, 128], BF16) for h in range(H)]
    lamrev = p.sb("l1a_lamrev", [128, H], F32)
    lam128 = p.sb("l1a_lam128", [128, H], F32)
    ucol = p.sb("l1a_ucol", [128, KT], F32)
    tmpP = p.sb("l1a_tmpP", [128, 128], F32)
    tcol = p.sb("l1a_tcol", [128, 1], F32)

    k_tok = p.sb("l1a_ktok", [128, D], BF16)
    v_tok = p.sb("l1a_vtok", [128, D], BF16)
    r_tok = p.sb("l1a_rtok", [128, D], BF16)
    kf = p.sb("l1a_kf", [128, KT * 128], BF16)
    rf = p.sb("l1a_rf", [128, KT * 128], BF16)
    khat = p.sb("l1a_khat", [128, 128], BF16)
    rtl = p.sb("l1a_rtl", [128, 128], BF16)
    abuf = [p.sb(f"l1a_ab{i}", [128, 128], BF16) for i in range(2)]
    att = p.sb("l1a_att", [128, KT * 128], BF16)
    s_f32 = p.sb("l1a_sf32", [128, 8 * 64], F32)
    s_bf = p.sb("l1a_sbf", [128, 8 * 64], BF16)
    x0_tok = p.sb("l1a_x0tok", [128, D], BF16)
    xpre = p.sb("l1a_xpre", [128, D], F32)
    lnb = (p.sb("l1a_stat", [128, 12], F32), p.sb("l1a_mv", [128, 2], F32),
           p.sb("l1a_mneg", [128, 1], F32), p.sb("l1a_rstd", [128, 1], F32),
           p.sb("l1a_xouttok", [128, D], BF16), p.sb("l1a_eps", [128, 1], F32))

    bank = [p.ps(f"l1a_bank{i}", [128, 512], F32) for i in range(6)]
    trp = p.ps("l1a_trp", [128, 128], BF16)

    for sname in ["ld", "xa0", "xa1", "xo", "os", "ta", "va", "sa", "ga"]:
        p.sem(sname)

    @blk.gpsimd
    def _(e):
        for wnm, wsc in [("Wk", "wksc"), ("Wv", "wvsc"), ("Wr", "wrsc"), ("Wo", "wosc")]:
            e.dma_start(io[wsc][:, :], io[wnm][:, :]).then_inc(p.inc("ga", 16), 16)

    ga_all = p.c["ga"]

    @blk.sync
    def _(e):
        e.dma_start(ones1[:1, :], cs["ones1"][:, :]).then_inc(p.inc("ld", 16), 16)
        e.dma_start(ones1f[:1, :], cs["ones1f"][:, :]).then_inc(p.inc("ld", 16), 16)
        e.dma_start(idn[:, :], cs["ident"][:, :]).then_inc(p.inc("ld", 16), 16)
        e.dma_start(idnf[:, :], cs["identf"][:, :]).then_inc(p.inc("ld", 16), 16)
        e.dma_start(dgrid[:, :], cs["dgrid"][:, :]).then_inc(p.inc("ld", 16), 16)
        e.dma_start(mgrid[:, :], cs["mgridf"][:, :]).then_inc(p.inc("ld", 16), 16)
        e.dma_start(ramp[:1, :], cs["ramp"][:, :]).then_inc(p.inc("ld", 16), 16)
        e.dma_start(rrcol[:, :], cs["rrcol"][:, :]).then_inc(p.inc("ld", 16), 16)
        e.dma_start(lamv[:, :], io["time_decay"][:, 0:1], allow_slow_non_contiguous=True).then_inc(p.inc("ld", 16), 16)
        for t in range(KT):
            e.dma_start(ucol[:, t:t + 1],
                        io["time_first"][2 * t:2 * t + 2, :].rearrange("a (b o) -> (a b) o", o=1)).then_inc(p.inc("ld", 16), 16)

    @blk.scalar
    def _(e):
        p.wge(e, "ga", ga_all)
        for k in range(KT):
            for wt, wsc in [(wk, "wksc"), (wv, "wvsc"), (wr, "wrsc"), (wo, "wosc")]:
                e.dma_start(wt[k][:, :], io[wsc][:, k * 128:(k + 1) * 128],
                            transpose=True).then_inc(p.inc("ld", 16), 16)

    ld_all = p.c["ld"]

    @blk.vector
    def _(e):
        e.memset(lnb[5][:, :], LN_EPS).then_inc(p.inc("va", 1), 1)

    @blk.scalar
    def _(e):
        p.wge(e, "ld", ld_all)
        e.activation(lnl[:, :], lamv[0:16, :], AF.Ln).then_inc(p.inc("sa", 1), 1)

    @blk.tensor
    def _(e):
        p.wge(e, "sa")
        e.transpose(bank[4][0:1, 0:16], lnl[0:16, :], idnf[0:16, 0:16]).then_inc(p.inc("ta", 1), 1)

    @blk.vector
    def _(e):
        p.wge(e, "ta")
        e.tensor_copy(lrow[:1, :], bank[4][0:1, 0:16]).then_inc(p.inc("va", 1), 1)

    for t in range(KT):
        for hi in range(2):
            h = 2 * t + hi
            @blk.vector
            def _(e, h=h):
                p.wge(e, "va")
                p.wge(e, "ta")
                e.tensor_scalar(rampl[:1, :], ramp[0:1, 1:129], lrow[0:1, h:h + 1],
                                None, ALU.mult).then_inc(p.inc("va", 1), 1)

            @blk.tensor
            def _(e, h=h, hi=hi):
                p.wge(e, "va")
                if not (h == 0):
                    p.wge(e, "sa")
                e.matmul(bank[4][hi * 64:(hi + 1) * 64, 0:128], ones1f[:1, 0:64], rampl[:1, :],
                         start=True, stop=True).then_inc(p.inc("ta", 1), 1)

        @blk.scalar
        def _(e, t=t):
            p.wge(e, "ta")
            e.activation(lamtile[t][:, :], bank[4][:, 0:128], AF.Exp).then_inc(p.inc("sa", 1), 1)

    for h in range(H):
        @blk.tensor
        def _(e, h=h):
            p.wge(e, "sa")
            p.wge(e, "va")
            e.matmul(bank[4][:, 128:129], ones1f[:1, :], lrow[0:1, h:h + 1],
                     start=True, stop=True).then_inc(p.inc("ta", 1), 1)

        @blk.vector
        def _(e, h=h):
            p.wge(e, "ta")
            e.tensor_copy(lc[:, :], bank[4][:, 128:129]).then_inc(p.inc("va", 1), 1)

        @blk.vector
        def _(e, h=h):
            p.wge(e, "va")
            p.bar(e, "va", e.tensor_scalar(tmpP[:, :], dgrid[:, :], lc[:, :], None, ALU.mult))
            p.bar(e, "va", e.tensor_scalar(tcol[:, :], rrcol[:, :], lc[:, :], None, ALU.mult))
            e.tensor_scalar(lam128[:, h:h + 1], lc[:, :], 128.0, None, ALU.mult).then_inc(p.inc("va", 1), 1)

        @blk.scalar
        def _(e, h=h):
            p.wge(e, "va")
            p.bar(e, "sa", e.activation(tmpP[:, :], tmpP[:, :], AF.Exp))
            p.bar(e, "sa", e.activation(lamrev[:, h:h + 1], tcol[:, :], AF.Exp))
            e.activation(lam128[:, h:h + 1], lam128[:, h:h + 1], AF.Exp).then_inc(p.inc("sa", 1), 1)

        @blk.vector
        def _(e, h=h):
            p.wge(e, "sa")
            e.tensor_tensor(ptile[h][:, :], tmpP[:, :], mgrid[:, :], ALU.mult).then_inc(p.inc("va", 1), 1)

    va_setup = p.c["va"]
    sa_setup = p.c["sa"]

    strip_loaded = {}
    strip_consumed = {}
    proj_evac_for = {}
    proj_evac = 0
    amask_done = {}
    abuf_used = {}
    khat_used = {}
    tm_state_prev = 0
    y_free = {}
    trp_free = ("va", 0)
    rtl_used = 0
    tok_used_ta = 0
    tok_used_va = 0
    kf_used = 0
    att_used = 0
    wo_free = 0

    for b in range(BPC):
        @blk.vector
        def _(e, b=b):
            p.wge(e, "ta")
            e.memset(s_f32[:, :], 0.0).then_inc(p.inc("va", 1), 1)
            e.memset(s_bf[:, :], 0.0).then_inc(p.inc("va", 1), 1)

        va_zero = p.c["va"]

        for cc in range(TPB):
            g = b * TPB + cc
            s_, ti = divmod(g, NTILE)
            if ti == 0:
                sb_ = s_ % 2
                @blk.scalar
                def _(e, s_=s_, sb_=sb_):
                    if s_ >= 2:
                        p.wge(e, "ta", strip_consumed[s_ - 2])
                    for k in range(KT):
                        e.dma_start(xf[sb_][k][:, :],
                                    io["x_in"][s_ * STRIP:(s_ + 1) * STRIP, k * 128:(k + 1) * 128],
                                    transpose=True).then_inc(p.inc(f"xa{s_ % 2}", 16), 16)
                strip_loaded[s_] = (f"xa{s_ % 2}", p.c[f"xa{s_ % 2}"])

            xs = xf[s_ % 2]
            tsl = slice(ti * 128, (ti + 1) * 128)

            for wi, (wt, dst) in enumerate([(wk, k_tok), (wv, v_tok), (wr, r_tok)]):
                for n in range(2):
                    bi = (wi * 2 + n) % 2
                    @blk.tensor
                    def _(e, g=g, s_=s_, tsl=tsl, wt=wt, n=n, bi=bi, wi=wi, va_zero=va_zero):
                        if g == 0 and wi == 0 and n == 0:
                            p.wge(e, "va", va_setup)
                            p.wge(e, "sa", sa_setup)
                        if wi == 0 and n == 0:
                            p.wge(e, "va", va_zero)
                            p.wge(e, strip_loaded[s_][0], strip_loaded[s_][1])
                        p.wge(e, "sa", proj_evac_for.get(bi, 0))
                        p.wge(e, "va", amask_done.get(bi, 0))
                        inst = None
                        for k in range(KT):
                            inst = e.matmul(bank[bi][:, :], xs[k][:, tsl],
                                            wt[k][:, n * 512:(n + 1) * 512],
                                            start=(k == 0), stop=(k == KT - 1))
                        inst.then_inc(p.inc("ta", 1), 1)

                    ta_pr = p.c["ta"]

                    @blk.scalar
                    def _(e, dst=dst, n=n, bi=bi, ta_pr=ta_pr, tu=(tok_used_ta, tok_used_va)):
                        p.wge(e, "ta", ta_pr)
                        p.wge(e, "ta", tu[0])
                        p.wge(e, "va", tu[1])
                        e.activation(dst[:, n * 512:(n + 1) * 512], bank[bi][:, :],
                                     AF.Copy).then_inc(p.inc("sa", 1), 1)

                    proj_evac_for[bi] = p.c["sa"]
            proj_evac = p.c["sa"]

            for t in range(KT):
                @blk.tensor
                def _(e, t=t, tf=trp_free, pe=proj_evac, ku=kf_used):
                    p.wge(e, "sa", pe)
                    p.wge(e, tf[0], tf[1])
                    e.transpose(trp[:, :], k_tok[:, t * 128:(t + 1) * 128], idn[:, :]).then_inc(p.inc("ta", 1), 1)

                ta_t1 = p.c["ta"]

                @blk.vector
                def _(e, t=t, ta_t1=ta_t1, ku=kf_used):
                    p.wge(e, "ta", ta_t1)
                    p.wge(e, "ta", ku)
                    e.tensor_copy(kf[:, t * 128:(t + 1) * 128], trp[:, :]).then_inc(p.inc("va", 1), 1)

                trp_free = ("va", p.c["va"])

                @blk.tensor
                def _(e, t=t, tf=trp_free):
                    p.wge(e, tf[0], tf[1])
                    e.transpose(trp[:, :], r_tok[:, t * 128:(t + 1) * 128], idn[:, :]).then_inc(p.inc("ta", 1), 1)

                ta_t2 = p.c["ta"]

                @blk.scalar
                def _(e, t=t, ta_t2=ta_t2):
                    p.wge(e, "ta", ta_t2)
                    p.wge(e, "va")
                    e.activation(tmpP[:, :], trp[:, :], AF.Sigmoid).then_inc(p.inc("sa", 1), 1)

                trp_free = ("sa", p.c["sa"])
                sa_sig = p.c["sa"]

                @blk.vector
                def _(e, t=t, sa_sig=sa_sig, ku=kf_used):
                    p.wge(e, "sa", sa_sig)
                    p.wge(e, "ta", ku)
                    e.tensor_scalar(rf[:, t * 128:(t + 1) * 128], tmpP[:, :], ucol[:, t:t + 1],
                                    None, ALU.mult).then_inc(p.inc("va", 1), 1)

            va_fm = p.c["va"]

            for pp in range(KT):
                @blk.vector
                def _(e, pp=pp, va_fm=va_fm, ru=rtl_used):
                    p.wge(e, "va", va_fm)
                    p.wge(e, "ta", ru)
                    e.tensor_tensor(rtl[:, :], rf[:, pp * 128:(pp + 1) * 128],
                                    lamtile[pp][:, :], ALU.mult).then_inc(p.inc("va", 1), 1)

                va_rtl = p.c["va"]
                ybank = bank[2 + pp % 2]

                for hi in range(2):
                    h = 2 * pp + hi
                    abi = h % 2

                    @blk.tensor
                    def _(e, pp=pp, hi=hi, h=h, abi=abi, va_fm=va_fm):
                        p.wge(e, "va", va_fm)
                        p.wge(e, "va", amask_done.get(abi, 0))
                        e.matmul(bank[abi][:, 0:128], kf[hi * 64:(hi + 1) * 64, pp * 128:(pp + 1) * 128],
                                 rf[hi * 64:(hi + 1) * 64, pp * 128:(pp + 1) * 128],
                                 start=True, stop=True).then_inc(p.inc("ta", 1), 1)

                    ta_ar = p.c["ta"]

                    @blk.vector
                    def _(e, h=h, abi=abi, ta_ar=ta_ar, au=abuf_used.get(abi, 0)):
                        p.wge(e, "ta", ta_ar)
                        p.wge(e, "ta", au)
                        e.tensor_tensor(abuf[abi][:, :], bank[abi][:, 0:128], ptile[h][:, :],
                                        ALU.mult).then_inc(p.inc("va", 1), 1)

                    amask_done[abi] = p.c["va"]
                    va_am = p.c["va"]

                    @blk.vector
                    def _(e, h=h, hi=hi, pe=proj_evac, ku=khat_used.get(hi, 0)):
                        p.wge(e, "sa", pe)
                        p.wge(e, "ta", ku)
                        e.tensor_scalar(khat[:, hi * 64:(hi + 1) * 64], k_tok[:, h * 64:(h + 1) * 64],
                                        lamrev[:, h:h + 1], None, ALU.mult).then_inc(p.inc("va", 1), 1)

                    va_kh = p.c["va"]

                    @blk.tensor
                    def _(e, pp=pp, hi=hi, h=h, abi=abi, va_am=va_am, va_rtl=va_rtl, va_kh=va_kh,
                          yf=y_free.get(pp % 2, 0), tf=tm_state_prev):
                        p.wge(e, "va", va_am)
                        p.wge(e, "va", va_rtl)
                        p.wge(e, "va", va_kh)
                        p.wge(e, "sa", yf)
                        p.wge(e, "va", tf)
                        e.matmul(ybank[hi * 64:(hi + 1) * 64, 0:128],
                                 s_bf[hi * 64:(hi + 1) * 64, pp * 64:(pp + 1) * 64],
                                 rtl[hi * 64:(hi + 1) * 64, :], start=True, stop=False)
                        e.matmul(ybank[hi * 64:(hi + 1) * 64, 0:128],
                                 v_tok[:, h * 64:(h + 1) * 64], abuf[abi][:, :],
                                 start=False, stop=True).then_inc(p.inc("ta", 1), 1)
                        e.matmul(bank[4][hi * 64:(hi + 1) * 64, (pp % 2) * 64:(pp % 2) * 64 + 64],
                                 khat[:, hi * 64:(hi + 1) * 64], v_tok[:, h * 64:(h + 1) * 64],
                                 start=True, stop=True, skip_group_check=True).then_inc(p.inc("ta", 1), 1)

                    abuf_used[abi] = p.c["ta"]
                    rtl_used = p.c["ta"]
                    khat_used[hi] = p.c["ta"]
                    ta_tm = p.c["ta"]

                    @blk.vector
                    def _(e, pp=pp, hi=hi, h=h, ta_tm=ta_tm):
                        p.wge(e, "ta", ta_tm)
                        r0, r1 = hi * 64, (hi + 1) * 64
                        c0, c1 = pp * 64, (pp + 1) * 64
                        p.bar(e, "va", e.scalar_tensor_tensor(
                            s_f32[r0:r1, c0:c1], s_f32[r0:r1, c0:c1], lam128[r0:r1, h:h + 1],
                            bank[4][r0:r1, (pp % 2) * 64:(pp % 2) * 64 + 64], ALU.mult, ALU.add))
                        e.tensor_copy(s_bf[r0:r1, c0:c1], s_f32[r0:r1, c0:c1]).then_inc(p.inc("va", 1), 1)

                    tm_state_prev = p.c["va"]

                @blk.scalar
                def _(e, pp=pp, au=att_used):
                    p.wge(e, "ta")
                    p.wge(e, "ta", au)
                    e.activation(att[:, pp * 128:(pp + 1) * 128], ybank[:, 0:128],
                                 AF.Copy).then_inc(p.inc("sa", 1), 1)

                y_free[pp % 2] = p.c["sa"]

            sa_att = p.c["sa"]

            @blk.sync
            def _(e, g=g, wf=wo_free):
                if wf:
                    p.wge(e, "va", wf)
                e.dma_start(x0_tok[:, :], io["x_in"][g * 128:(g + 1) * 128, :]).then_inc(p.inc("xo", 16), 16)

            xo_now = p.c["xo"]

            for n in range(2):
                @blk.tensor
                def _(e, n=n, sa_att=sa_att, wf=wo_free, vnow=p.c["va"]):
                    p.wge(e, "sa", sa_att)
                    if n == 0:
                        p.wge(e, "va", wf)
                    else:
                        p.wge(e, "va", vnow)
                    inst = None
                    for k in range(KT):
                        inst = e.matmul(bank[5][:, :], att[:, k * 128:(k + 1) * 128],
                                        wo[k][:, n * 512:(n + 1) * 512],
                                        start=(k == 0), stop=(k == KT - 1))
                    inst.then_inc(p.inc("ta", 1), 1)

                ta_wo = p.c["ta"]

                @blk.vector
                def _(e, n=n, ta_wo=ta_wo, xo_now=xo_now):
                    p.wge(e, "ta", ta_wo)
                    p.wge(e, "xo", xo_now)
                    p.wge(e, "os")
                    e.tensor_tensor(xpre[:, n * 512:(n + 1) * 512], x0_tok[:, n * 512:(n + 1) * 512],
                                    bank[5][:, :], ALU.add).then_inc(p.inc("va", 1), 1)

            wo_free = p.c["va"]
            tok_used_ta = p.c["ta"]
            tok_used_va = p.c["va"]
            kf_used = p.c["ta"]
            att_used = p.c["ta"]
            if ti == NTILE - 1:
                strip_consumed[s_] = p.c["ta"]
            emit_ln_store(p, blk, lnb, xpre, io["x_out"], g * 128)

    @blk.sync
    def _(e):
        p.wge(e, "os")
    return



def emit_l1_ffn(p, io):
    nc = p.nc
    cs = io["consts"]
    G3 = 3072
    NB = S // 8
    h0tok = p.sb("l1f_h0tok", [2, D], F32)

    for sname in ["ld", "xa0", "xa1", "xo0", "xo1", "xc", "xm", "ta", "va", "sa", "ga", "so0", "so1", "os",
                  "gq0", "gq1", "gh", "gt", "gv", "gs"]:
        p.sem(sname)

    esA = contextlib.ExitStack()
    with esA:
        sbA = lambda n, shp, dt=F32: esA.enter_context(nc.sbuf_tensor(n, shp, dt))
        psA = lambda n, shp, dt=F32: esA.enter_context(nc.psum_tensor(n, shp, dt))
        blk = esA.enter_context(nc.Block())
        wih = [sbA(f"l1f_wih{k}", [128, G3], BF16) for k in range(KT)]
        ctxw = [sbA(f"l1f_ctxw{k}", [128, D], BF16) for k in range(KT)]
        STRIP = min(2048, S)
        NSTRIP = T // STRIP
        NTILE = STRIP // 128
        xf = [[sbA(f"l1f_xfA{bb}_{k}", [128, STRIP], BF16) for k in range(KT)] for bb in range(2)]
        xw_tok = [sbA(f"l1f_xwtok{i}", [128, G3], BF16) for i in range(2)]
        x0A = [sbA(f"l1f_x0A{i}", [128, D], BF16) for i in range(2)]
        bsel = sbA("l1f_bsel", [128, 4], BF16)
        hm = sbA("l1f_hm", [2, D], BF16)
        hmT = sbA("l1f_hmT", [128, KT * 2], BF16)
        ctxbrow = sbA("l1f_ctxbrow", [1, D], BF16)
        onesA = sbA("l1f_onesA", [1, 128], BF16)
        idnA = sbA("l1f_idnA", [128, 128], BF16)
        bank = [psA(f"l1f_bankA{i}", [128, 512], F32) for i in range(4)]
        pmean = psA("l1f_pmean", [2, D], F32)
        ptrA = psA("l1f_ptrA", [128, 2], BF16)

        @blk.gpsimd
        def _(e):
            e.dma_start(io["wihsc"][:, :], io["gru_w_ih"][:, :]).then_inc(p.inc("ga", 16), 16)
            e.dma_start(io["whhsc"][:, :], io["gru_w_hh"][:, :]).then_inc(p.inc("ga", 16), 16)
            e.dma_start(io["ctxsc"][:, :], io["ctx_w"][:, :]).then_inc(p.inc("ga", 16), 16)
            e.dma_start(ctxbrow[:1, :], io["ctx_b"][None, :]).then_inc(p.inc("ga", 16), 16)

        ga_all = p.c["ga"]

        @blk.sync
        def _(e):
            e.dma_start(onesA[:1, :], cs["ones1"][:, :]).then_inc(p.inc("ld", 16), 16)
            e.dma_start(idnA[:, :], cs["ident"][:, :]).then_inc(p.inc("ld", 16), 16)
            e.dma_start(bsel[:, :], cs["bsel"][:, :]).then_inc(p.inc("ld", 16), 16)

        @blk.scalar
        def _(e):
            p.wge(e, "ga", ga_all)
            for k in range(KT):
                e.dma_start(wih[k][:, :], io["wihsc"][:, k * 128:(k + 1) * 128],
                            transpose=True).then_inc(p.inc("ld", 16), 16)
                e.dma_start(ctxw[k][:, :], io["ctxsc"][:, k * 128:(k + 1) * 128],
                            transpose=True).then_inc(p.inc("ld", 16), 16)

        ld_all = p.c["ld"]

        strip_loaded = {}
        strip_consumed = {}
        xw_stored = {}
        xw_evac = {}
        x0_used = {}
        for g in range(T // 128):
            s_, ti = divmod(g, NTILE)
            b = g // (S // 128)
            par = g % 2
            if ti == 0:
                sb_ = s_ % 2
                @blk.scalar
                def _(e, s_=s_, sb_=sb_):
                    if s_ >= 2:
                        p.wge(e, "ta", strip_consumed[s_ - 2])
                    for k in range(KT):
                        e.dma_start(xf[sb_][k][:, :],
                                    io["x_in"][s_ * STRIP:(s_ + 1) * STRIP, k * 128:(k + 1) * 128],
                                    transpose=True).then_inc(p.inc(f"xa{s_ % 2}", 16), 16)
                strip_loaded[s_] = (f"xa{s_ % 2}", p.c[f"xa{s_ % 2}"])

            xs = xf[s_ % 2]
            tsl = slice(ti * 128, (ti + 1) * 128)

            @blk.sync
            def _(e, g=g, par=par):
                if g >= 2:
                    p.wge(e, "ta", x0_used.get(g - 2, 0))
                e.dma_start(x0A[par][:, :], io["x_in"][g * 128:(g + 1) * 128, :]).then_inc(p.inc(f"xo{par}", 16), 16)

            xo_now = (f"xo{par}", p.c[f"xo{par}"])

            for n6 in range(6):
                bi = n6 % 3
                @blk.tensor
                def _(e, g=g, s_=s_, tsl=tsl, n6=n6, bi=bi):
                    if g == 0 and n6 == 0:
                        p.wge(e, "ld", ld_all)
                    if n6 == 0:
                        p.wge(e, strip_loaded[s_][0], strip_loaded[s_][1])
                    p.wge(e, "sa", xw_evac.get(bi, 0))
                    inst = None
                    for k in range(KT):
                        inst = e.matmul(bank[bi][:, :], xs[k][:, tsl],
                                        wih[k][:, n6 * 512:(n6 + 1) * 512],
                                        start=(k == 0), stop=(k == KT - 1))
                    inst.then_inc(p.inc("ta", 1), 1)

                ta_xw = p.c["ta"]

                @blk.scalar
                def _(e, g=g, n6=n6, bi=bi, par=par, ta_xw=ta_xw):
                    p.wge(e, "ta", ta_xw)
                    if g >= 2 and n6 == 0:
                        p.wge(e, xw_stored[g - 2][0], xw_stored[g - 2][1])
                    e.activation(xw_tok[par][:, n6 * 512:(n6 + 1) * 512], bank[bi][:, :],
                                 AF.Copy).then_inc(p.inc("sa", 1), 1)

                xw_evac[bi] = p.c["sa"]

            sa_xwtok = p.c["sa"]

            @blk.tensor
            def _(e, g=g, b=b, par=par, xo_now=xo_now, first=(g == 0), last=(g == T // 128 - 1)):
                p.wge(e, xo_now[0], xo_now[1])
                inst = None
                for n in range(2):
                    inst = e.matmul(pmean[:, n * 512:(n + 1) * 512], bsel[:, 2 * b:2 * b + 2],
                                    x0A[par][:, n * 512:(n + 1) * 512],
                                    start=first, stop=last, skip_group_check=True)
                inst.then_inc(p.inc("ta", 1), 1)

            x0_used[g] = p.c["ta"]

            @blk.sync
            def _(e, g=g, b=b, par=par, sa_xwtok=sa_xwtok):
                p.wge(e, "sa", sa_xwtok)
                s0 = (g % (S // 128)) * 128
                e.dma_start(io["xwsc"].rearrange("(s bb) c -> s bb c", bb=BPC)[s0:s0 + 128, b, :],
                            xw_tok[par][:, :]).then_inc(p.inc(f"so{par}", 16), 16)

            xw_stored[g] = (f"so{par}", p.c[f"so{par}"])
            if ti == NTILE - 1:
                strip_consumed[s_] = p.c["ta"]

        @blk.vector
        def _(e):
            p.wge(e, "ta")
            e.tensor_scalar(hm[:, :], pmean[:, :], 1.0 / S, None, ALU.mult).then_inc(p.inc("va", 1), 1)

        for k in range(KT):
            @blk.tensor
            def _(e, k=k):
                p.wge(e, "va")
                e.transpose(ptrA[:, :], hm[:, k * 128:(k + 1) * 128], idnA[0:2, 0:2]).then_inc(p.inc("ta", 1), 1)

            @blk.vector
            def _(e, k=k):
                p.wge(e, "ta")
                e.tensor_copy(hmT[:, k * 2:(k + 1) * 2], ptrA[:, :]).then_inc(p.inc("va", 1), 1)

        for n in range(2):
            @blk.tensor
            def _(e, n=n):
                p.wge(e, "va")
                for k in range(KT):
                    e.matmul(bank[3][0:2, :], hmT[:, k * 2:(k + 1) * 2],
                             ctxw[k][:, n * 512:(n + 1) * 512],
                             start=(k == 0), stop=False, skip_group_check=True)
                e.matmul(bank[3][0:2, :], onesA[:1, 0:2],
                         ctxbrow[:1, n * 512:(n + 1) * 512],
                         start=False, stop=True, skip_group_check=True).then_inc(p.inc("ta", 1), 1)

            @blk.vector
            def _(e, n=n):
                p.wge(e, "ta")
                e.tensor_copy(h0tok[:, n * 512:(n + 1) * 512], bank[3][0:2, :]).then_inc(p.inc("va", 1), 1)

    emit_l1_ffn_scan(p, io, h0tok)
    emit_l1_ffn_tail(p, io)
    return


def emit_l1_ffn_scan(p, io, h0tok):
    nc = p.nc
    cs = io["consts"]
    G3 = 3072
    NB = S // 8
    es = contextlib.ExitStack()
    with es:
        sb = lambda n, shp, dt=F32: es.enter_context(nc.sbuf_tensor(n, shp, dt))
        ps = lambda n, shp, dt=F32: es.enter_context(nc.psum_tensor(n, shp, dt))
        blk = es.enter_context(nc.Block())
        whh = [sb(f"l1f_whh{k}", [128, G3], BF16) for k in range(KT)]
        xwt = sb("l1f_xwt", [2, 16 * G3], BF16)
        hr = sb("l1f_hr", [2, 8 * D], BF16)
        hT = sb("l1f_hT", [128, 16], BF16)
        htok = sb("l1f_htok", [2, 2 * D], F32)
        rz = sb("l1f_rz", [2, 2048], F32)
        t3b = sb("l1f_t3", [2, D], F32)
        t4b = sb("l1f_t4", [2, D], F32)
        nnb = sb("l1f_nn", [2, D], F32)
        ddb = sb("l1f_dd", [2, D], F32)
        mmb = sb("l1f_mm", [2, D], F32)
        idnB = sb("l1f_idnB", [128, 128], BF16)
        h0bf = sb("l1f_h0bf", [2, D], BF16)
        phw = [ps(f"l1f_phw{i}", [2, 512], F32) for i in range(6)]
        ptrB = [ps(f"l1f_ptrB{i}", [128, 2], BF16) for i in range(2)]

        GVB, GTB, GSB, GHB = 512, 256, 256, 0
        NIT2 = NB // 2

        @blk.sync
        def _(e):
            e.dma_start(idnB[:, :], cs["ident"][:, :]).then_inc(p.inc("ld", 16), 16)

        @blk.scalar
        def _(e):
            p.wge(e, "ga")
            for k in range(KT):
                e.dma_start(whh[k][:, :], io["whhsc"][:, k * 128:(k + 1) * 128],
                            transpose=True).then_inc(p.inc("ld", 16), 16)

        ld_all = p.c["ld"]

        @blk.vector
        def _(e):
            p.wge(e, "va")
            p.bar(e, "gv", e.tensor_copy(htok[:, D:2 * D], h0tok[:, :]))
            e.tensor_copy(h0bf[:, :], h0tok[:, :]).then_inc(p.inc("gv", 1), 1)

        for k in range(KT):
            @blk.tensor
            def _(e, k=k):
                if k == 0:
                    p.wge(e, "ld", ld_all)
                p.wge(e, "gv")
                e.transpose(ptrB[0][:, :], h0bf[:, k * 128:(k + 1) * 128],
                            idnB[0:2, 0:2]).then_inc(p.inc("gt", 1), 1)

            @blk.vector
            def _(e, k=k):
                p.wge(e, "gt")
                e.tensor_copy(hT[:, 2 * k:2 * k + 2], ptrB[0][:, :]).then_inc(p.inc("gv", 1), 1)

        @blk.vector
        def _(e):
            p.wge(e, "gv")
            e.sem_inc(p.sems["gv"], GVB - p.c["gv"])
        p.c["gv"] = GVB

        @blk.scalar
        def _(e):
            p.wge(e, "gv", GVB)
            p.wge(e, "gs")
            e.sem_inc(p.sems["gs"], GSB - p.c["gs"])
        p.c["gs"] = GSB

        @blk.tensor
        def _(e):
            p.wge(e, "gt")
            e.sem_inc(p.sems["gt"], GTB - p.c["gt"])
        p.c["gt"] = GTB



        xwsc_h = io["xwsc_h"]
        hsc_h = io["hsc_h"]

        @blk.gpsimd
        def _(e):
            p.wge(e, "so0")
            p.wge(e, "so1")
            r1 = e.alloc_register("xwoff")
            r2 = e.alloc_register("hoff")
            for kb in range(2):
                e.dma_start(bass.AP(xwt, kb * 8 * G3, [[16 * G3, 2], [G3, 8], [1, G3]]),
                            bass.AP(xwsc_h, kb * 8 * 2 * G3, [[G3, 2], [2 * G3, 8], [1, G3]])).then_inc(p.inc(f"gq{kb}", 16), 16)
            with e.Fori(1, NIT2) as m:
                for sub in range(2):
                    e.wait_ge(p.sems["gs"], (m * 96 + sub * 48 - 48) + GSB)
                    e.reg_mul(r2, m, 2 * 8 * 2 * D)
                    e.reg_add(r2, r2, (sub - 2) * 8 * 2 * D)
                    e.dma_start(bass.AP(hsc_h, r2, [[D, 2], [2 * D, 8], [1, D]]),
                                bass.AP(hr, 0, [[8 * D, 2], [D, 8], [1, D]])).then_inc(p.inc("gh", 16), 16)
                    e.wait_ge(p.sems["gv"], (m * 240 + sub * 120 - 240) + 120 + GVB)
                    e.reg_mul(r1, m, 2 * 8 * 2 * G3)
                    e.reg_add(r1, r1, sub * 8 * 2 * G3)
                    e.dma_start(bass.AP(xwt, sub * 8 * G3, [[16 * G3, 2], [G3, 8], [1, G3]]),
                                bass.AP(xwsc_h, r1, [[G3, 2], [2 * G3, 8], [1, G3]])).then_inc(p.inc(f"gq{sub}", 16), 16)
            for kk in [NB - 2, NB - 1]:
                e.wait_ge(p.sems["gs"], kk * 48 + 48 + GSB)
                e.dma_start(bass.AP(hsc_h, kk * 8 * 2 * D, [[D, 2], [2 * D, 8], [1, D]]),
                            bass.AP(hr, 0, [[8 * D, 2], [D, 8], [1, D]])).then_inc(p.inc("gh", 16), 16)
            e.free_register(r1)
            e.free_register(r2)
            e.wait_ge(p.sems["gh"], NB * 16)

        p.c["gq0"] = 16 * (NB // 2)
        p.c["gq1"] = 16 * (NB // 2)
        p.c["gh"] = 16 * NB


        @blk.tensor
        def _(e):
            rw = e.alloc_register("t_rw")
            def wge_rt(sem, coef, const, it2):
                e.reg_mul(rw, it2, coef)
                e.reg_add(rw, rw, const)
                e.wait_ge(sem, rw)
            with e.Fori(0, NIT2) as it2:
                for jj in range(16):
                    t = None
                    kbslot = jj // 8
                    wge_rt(p.sems["gv"], 240, 15 * jj + GVB, it2)
                    wge_rt(p.sems[f"gq{jj // 8}"], 16, 16, it2)
                    wge_rt(p.sems["gs"], 96, 6 * jj + GSB, it2)
                    for n6 in range(6):
                        inst = None
                        for k in range(KT):
                            inst = e.matmul(phw[n6][:, :], hT[:, 2 * k:2 * k + 2],
                                            whh[k][:, n6 * 512:(n6 + 1) * 512],
                                            start=(k == 0), stop=(k == KT - 1 and n6 >= 4))
                        if n6 < 4:
                            inst = e.matmul(phw[n6][:, :], idnB[0:2, 0:2],
                                            bass.AP(xwt, kbslot * 8 * G3 + (jj % 8) * G3 + n6 * 512,
                                                    [[16 * G3, 2], [1, 512]]),
                                            start=False, stop=True)
                    inst.then_inc(p.sems["gt"], 1)
                    wge_rt(p.sems["gs"], 96, 6 * jj + 6 + GSB, it2)
                    for k in range(KT):
                        if k >= 2:
                            wge_rt(p.sems["gv"], 240, 15 * jj + 7 + k + GVB, it2)
                        e.transpose(ptrB[k % 2][:, :], bass.AP(hr, (jj % 8) * D + k * 128, [[8 * D, 2], [1, 128]]),
                                    idnB[0:2, 0:2]).then_inc(p.sems["gt"], 1)
            e.free_register(rw)

        p.c["gt"] = GTB + 9 * S

        @blk.scalar
        def _(e):
            rw = e.alloc_register("s_rw")
            def wge_rt(sem, coef, const, it2):
                e.reg_mul(rw, it2, coef)
                e.reg_add(rw, rw, const)
                e.wait_ge(sem, rw)
            with e.Fori(0, NIT2) as it2:
                for jj in range(16):
                    wge_rt(p.sems["gt"], 144, 9 * jj + 1 + GTB, it2)
                    for c in range(4):
                        e.activation(rz[:, c * 512:(c + 1) * 512], phw[c][:, :],
                                     AF.Sigmoid).then_inc(p.sems["gs"], 1)
                    wge_rt(p.sems["gv"], 240, 15 * jj + 4 + GVB, it2)
                    e.activation(nnb[:, :], t4b[:, :], AF.Tanh).then_inc(p.sems["gs"], 1)
                    wge_rt(p.sems["gv"], 240, 15 * jj + 7 + GVB, it2)
                    wge_rt(p.sems["gh"], 32, 16 * (jj // 8), it2)
                    e.activation(bass.AP(hr, (jj % 8) * D, [[8 * D, 2], [1, D]]),
                                 htok[:, (jj % 2) * D:(jj % 2) * D + D],
                                 AF.Copy).then_inc(p.sems["gs"], 1)
            e.free_register(rw)

        p.c["gs"] = GSB + 6 * S

        @blk.vector
        def _(e):
            rw = e.alloc_register("v_rw")
            def wge_rt(sem, coef, const, it2):
                e.reg_mul(rw, it2, coef)
                e.reg_add(rw, rw, const)
                e.wait_ge(sem, rw)
            with e.Fori(0, NIT2) as it2:
                for jj in range(16):
                    kbslot = jj // 8
                    cur = (jj % 2) * D
                    prv = ((jj + 1) % 2) * D
                    wge_rt(p.sems["gs"], 96, 6 * jj + 2 + GSB, it2)
                    e.tensor_tensor(t3b[:, 0:512], rz[:, 0:512], phw[4][:, :], ALU.mult).then_inc(p.sems["gv"], 1)
                    e.tensor_tensor(t3b[:, 512:1024], rz[:, 512:1024], phw[5][:, :], ALU.mult).then_inc(p.sems["gv"], 1)
                    wge_rt(p.sems["gv"], 240, 15 * jj + 1 + GVB, it2)
                    e.tensor_tensor(t4b[:, 0:512],
                                    bass.AP(xwt, kbslot * 8 * G3 + (jj % 8) * G3 + 2048, [[16 * G3, 2], [1, 512]]),
                                    t3b[:, 0:512], ALU.add).then_inc(p.sems["gv"], 1)
                    wge_rt(p.sems["gv"], 240, 15 * jj + 2 + GVB, it2)
                    e.tensor_tensor(t4b[:, 512:1024],
                                    bass.AP(xwt, kbslot * 8 * G3 + (jj % 8) * G3 + 2560, [[16 * G3, 2], [1, 512]]),
                                    t3b[:, 512:1024], ALU.add).then_inc(p.sems["gv"], 1)
                    wge_rt(p.sems["gs"], 96, 6 * jj + 5 + GSB, it2)
                    e.tensor_tensor(ddb[:, :], htok[:, prv:prv + D], nnb[:, :], ALU.subtract).then_inc(p.sems["gv"], 1)
                    wge_rt(p.sems["gv"], 240, 15 * jj + 5 + GVB, it2)
                    e.tensor_tensor(mmb[:, :], ddb[:, :], rz[:, 1024:2048], ALU.mult).then_inc(p.sems["gv"], 1)
                    wge_rt(p.sems["gv"], 240, 15 * jj + 6 + GVB, it2)
                    e.tensor_tensor(htok[:, cur:cur + D], mmb[:, :], nnb[:, :], ALU.add).then_inc(p.sems["gv"], 1)
                    for k in range(KT):
                        wge_rt(p.sems["gt"], 144, 9 * jj + 2 + k + GTB, it2)
                        e.tensor_copy(hT[:, 2 * k:2 * k + 2], ptrB[k % 2][:, :]).then_inc(p.sems["gv"], 1)
            e.free_register(rw)

        p.c["gv"] = GVB + 15 * S
    return

def emit_l1_ffn_tail(p, io):
    nc = p.nc
    cs = io["consts"]
    E, FF = 8, 2048
    NTPC = T // 128

    esC = contextlib.ExitStack()
    with esC:
        sb = lambda n, shp, dt=F32: esC.enter_context(nc.sbuf_tensor(n, shp, dt))
        ps = lambda n, shp, dt=F32: esC.enter_context(nc.psum_tensor(n, shp, dt))
        blk = esC.enter_context(nc.Block())
        exw = [sb(f"l1f_exw{k}", [128, D], BF16) for k in range(KT)]
        STRIP = min(2048, S)
        NSTRIP = T // STRIP
        NTILE = STRIP // 128
        xf = [[sb(f"l1f_xfC{bb}_{k}", [128, STRIP], BF16) for k in range(KT)] for bb in range(2)]
        ex_tok = sb("l1f_extok", [128, D], BF16)
        rt_tok = [sb(f"l1f_rttok{i}", [128, D], BF16) for i in range(2)]
        junk = sb("l1f_junk", [128, 128], F32)
        dot = sb("l1f_dot", [128, 8], F32)
        ssb = sb("l1f_ss", [128, 8], F32)
        seb = sb("l1f_se", [128, 8], F32)
        gn = sb("l1f_gn", [128, 64], F32)
        inv = sb("l1f_inv", [128, 8], F32)
        nr = sb("l1f_nr", [128, 8], F32)
        nrn = sb("l1f_nrn", [128, 8], F32)
        n2 = sb("l1f_n2", [128, 8], F32)
        q8 = sb("l1f_q8", [128, 8], F32)
        pentile = sb("l1f_pentile", [128, 1], F32)
        penacc = sb("l1f_penacc", [128, 1], F32)
        dist = sb("l1f_dist", [128, 8], F32)
        dldp = [sb(f"l1f_dld{i}", [128, 8], F32) for i in range(2)]
        tmp8 = sb("l1f_tmp8", [128, 8], F32)
        tmp1 = sb("l1f_tmp1c", [128, 1], F32)
        onesCf = sb("l1f_onesCf", [1, 128], F32)
        onescol = sb("l1f_onescolC", [128, 1], F32)
        pone = ps("l1f_pone", [128, 512], F32)
        pex = [ps(f"l1f_pex{i}", [128, 512], F32) for i in range(2)]

        @blk.gpsimd
        def _(e):
            e.dma_start(io["exprsc"][:, :], io["expr_w"][:, :]).then_inc(p.inc("ga", 16), 16)

        ga_now = p.c["ga"]

        @blk.sync
        def _(e):
            e.dma_start(onesCf[:1, :], cs["ones1f"][:, :]).then_inc(p.inc("ld", 16), 16)
            e.dma_start(onescol[:, :], cs["onescolf"][:, :]).then_inc(p.inc("ld", 16), 16)

        @blk.scalar
        def _(e):
            p.wge(e, "ga", ga_now)
            for k in range(KT):
                e.dma_start(exw[k][:, :], io["exprsc"][:, k * 128:(k + 1) * 128],
                            transpose=True).then_inc(p.inc("ld", 16), 16)

        ld_all = p.c["ld"]

        @blk.vector
        def _(e):
            e.memset(penacc[:, :], 0.0).then_inc(p.inc("va", 1), 1)

        va_pen = p.c["va"]

        strip_loaded = {}
        strip_consumed = {}
        rt_used = {}
        ex_evac = {}
        extok_used = 0
        dist_stored = 0
        for g in range(NTPC):
            s_, ti = divmod(g, NTILE)
            par = g % 2
            if ti == 0:
                sb_ = s_ % 2
                @blk.scalar
                def _(e, s_=s_, sb_=sb_):
                    if s_ >= 2:
                        p.wge(e, "ta", strip_consumed[s_ - 2])
                    for k in range(KT):
                        e.dma_start(xf[sb_][k][:, :],
                                    io["x_in"][s_ * STRIP:(s_ + 1) * STRIP, k * 128:(k + 1) * 128],
                                    transpose=True).then_inc(p.inc(f"xa{s_ % 2}", 16), 16)
                strip_loaded[s_] = (f"xa{s_ % 2}", p.c[f"xa{s_ % 2}"])

            xs = xf[s_ % 2]
            tsl = slice(ti * 128, (ti + 1) * 128)
            b = g // (S // 128)
            s0 = (g % (S // 128)) * 128

            @blk.sync
            def _(e, g=g, par=par, b=b, s0=s0):
                if g >= 2:
                    p.wge(e, "va", rt_used.get(g - 2, 0))
                e.dma_start(rt_tok[par][:, :],
                            io["hsc"].rearrange("(s bb) c -> s bb c", bb=BPC)[s0:s0 + 128, b, :]).then_inc(p.inc(f"xo{par}", 16), 16)

            xo_now = (f"xo{par}", p.c[f"xo{par}"])

            for n in range(2):
                @blk.tensor
                def _(e, g=g, s_=s_, tsl=tsl, n=n):
                    if g == 0 and n == 0:
                        p.wge(e, "ld", ld_all)
                        p.wge(e, "va", va_pen)
                    if n == 0:
                        p.wge(e, strip_loaded[s_][0], strip_loaded[s_][1])
                    p.wge(e, "sa", ex_evac.get(n, 0))
                    inst = None
                    for k in range(KT):
                        inst = e.matmul(pex[n][:, :], xs[k][:, tsl], exw[k][:, n * 512:(n + 1) * 512],
                                        start=(k == 0), stop=(k == KT - 1))
                    inst.then_inc(p.inc("ta", 1), 1)

                ta_ex = p.c["ta"]

                @blk.scalar
                def _(e, n=n, ta_ex=ta_ex):
                    p.wge(e, "ta", ta_ex)
                    p.wge(e, "va", extok_used)
                    e.activation(ex_tok[:, n * 512:(n + 1) * 512], pex[n][:, :],
                                 AF.Copy).then_inc(p.inc("sa", 1), 1)

                ex_evac[n] = p.c["sa"]

            sa_ex = p.c["sa"]

            @blk.vector
            def _(e, g=g, par=par, sa_ex=sa_ex, xo_now=xo_now):
                p.wge(e, "sa", sa_ex)
                p.wge(e, xo_now[0], xo_now[1])
                for ee in range(E):
                    sl = slice(ee * 128, (ee + 1) * 128)
                    p.bar(e, "va", e.tensor_tensor(junk[:, :], ex_tok[:, sl], rt_tok[par][:, sl], ALU.mult))
                    p.bar(e, "va", e.reduce_sum(dot[:, ee:ee + 1], junk[:, :], axis=mybir.AxisListType.X))
                    p.bar(e, "va", e.tensor_tensor(junk[:, :], rt_tok[par][:, sl], rt_tok[par][:, sl], ALU.mult))
                    p.bar(e, "va", e.reduce_sum(ssb[:, ee:ee + 1], junk[:, :], axis=mybir.AxisListType.X))
                    p.bar(e, "va", e.tensor_tensor(junk[:, :], ex_tok[:, sl], ex_tok[:, sl], ALU.mult))
                    p.bar(e, "va", e.reduce_sum(seb[:, ee:ee + 1], junk[:, :], axis=mybir.AxisListType.X))
                p.bar(e, "va", e.memset(gn[:, :], 0.0))
                for ee in range(E):
                    for ff in range(ee + 1, E):
                        p.bar(e, "va", e.tensor_tensor(junk[:, :], rt_tok[par][:, ee * 128:(ee + 1) * 128],
                                                       rt_tok[par][:, ff * 128:(ff + 1) * 128], ALU.mult))
                        p.bar(e, "va", e.reduce_sum(gn[:, ee * 8 + ff:ee * 8 + ff + 1], junk[:, :],
                                                    axis=mybir.AxisListType.X))

            extok_used = p.c["va"]
            rt_used[g] = p.c["va"]
            va_ttr = p.c["va"]

            @blk.scalar
            def _(e, va_ttr=va_ttr):
                p.wge(e, "va", va_ttr)
                e.activation(nr[:, :], ssb[:, :], AF.Sqrt).then_inc(p.inc("sa", 1), 1)
                e.activation(seb[:, :], seb[:, :], AF.Sqrt).then_inc(p.inc("sa", 1), 1)

            @blk.vector
            def _(e, ds=dist_stored):
                p.wge(e, "sa")
                p.wge(e, "so0", ds)
                p.bar(e, "va", e.tensor_scalar(inv[:, :], nr[:, :], 1e-12, None, ALU.max))
                p.bar(e, "va", e.reciprocal(inv[:, :], inv[:, :]))
                for ee in range(E):
                    for ff in range(ee + 1, E):
                        c0 = ee * 8 + ff
                        p.bar(e, "va", e.tensor_scalar(gn[:, c0:c0 + 1], gn[:, c0:c0 + 1],
                                                       inv[:, ee:ee + 1], inv[:, ff:ff + 1],
                                                       ALU.mult, ALU.mult))
                        e.tensor_copy(gn[:, ff * 8 + ee:ff * 8 + ee + 1], gn[:, c0:c0 + 1]).then_inc(p.inc("va", 1), 1)
                p.wge(e, "va")
                p.bar(e, "va", e.tensor_tensor(gn[:, :], gn[:, :], gn[:, :], ALU.mult))
                for ee in range(E):
                    p.bar(e, "va", e.reduce_sum(n2[:, ee:ee + 1], gn[:, ee * 8:(ee + 1) * 8],
                                                axis=mybir.AxisListType.X))
                p.bar(e, "va", e.tensor_scalar(q8[:, :], n2[:, :], 1e-24, None, ALU.max))
                p.bar(e, "va", e.reciprocal(q8[:, :], q8[:, :]))
                p.bar(e, "va", e.tensor_tensor(q8[:, :], q8[:, :], n2[:, :], ALU.mult))
                p.bar(e, "va", e.reduce_sum(pentile[:, :], q8[:, :], axis=mybir.AxisListType.X))
                p.bar(e, "va", e.tensor_tensor(penacc[:, :], penacc[:, :], pentile[:, :], ALU.add))
                p.bar(e, "va", e.tensor_tensor(dot[:, :], dot[:, :], inv[:, :], ALU.mult))
                p.bar(e, "va", e.tensor_tensor(nrn[:, :], nr[:, :], inv[:, :], ALU.mult))
                p.bar(e, "va", e.tensor_scalar(tmp8[:, :], seb[:, :], 1e-8, None, ALU.max))
                p.bar(e, "va", e.tensor_scalar(nrn[:, :], nrn[:, :], 1e-8, None, ALU.max))
                p.bar(e, "va", e.tensor_tensor(tmp8[:, :], tmp8[:, :], nrn[:, :], ALU.mult))
                p.bar(e, "va", e.reciprocal(tmp8[:, :], tmp8[:, :]))
                p.bar(e, "va", e.tensor_tensor(dot[:, :], dot[:, :], tmp8[:, :], ALU.mult))
                e.tensor_scalar(dist[:, :], dot[:, :], -1.0, 1.0, ALU.mult, ALU.add).then_inc(p.inc("va", 1), 1)

            va_dist = p.c["va"]

            @blk.sync
            def _(e, g=g, va_dist=va_dist):
                p.wge(e, "va", va_dist)
                e.dma_start(io["distsc"][g * 128:(g + 1) * 128, :], dist[:, :]).then_inc(p.inc("so0", 16), 16)

            dist_stored = p.c["so0"]
            if ti == NTILE - 1:
                strip_consumed[s_] = p.c["ta"]

        @blk.tensor
        def _(e):
            p.wge(e, "va")
            e.matmul(pone[0:1, 0:1], penacc[:, :], onescol[:, :], start=True, stop=True,
                     skip_group_check=True).then_inc(p.inc("ta", 1), 1)

        @blk.vector
        def _(e):
            p.wge(e, "ta")
            e.tensor_copy(tmp1[0:1, 0:1], pone[0:1, 0:1]).then_inc(p.inc("va", 1), 1)

        @blk.sync
        def _(e):
            p.wge(e, "va")
            e.dma_start(io["penpart"][:, :], tmp1[0:1, 0:1]).then_inc(p.inc("so1", 16), 16)
            p.wge(e, "so0")
            p.wge(e, "so1")
    return


def emit_l1_ffn_tail2(p, io):
    nc = p.nc
    cs = io["consts"]
    E = 8
    NTPC = T // 128
    esC = contextlib.ExitStack()
    with esC:
        sb = lambda n, shp, dt=F32: esC.enter_context(nc.sbuf_tensor(n, shp, dt))
        ps = lambda n, shp, dt=F32: esC.enter_context(nc.psum_tensor(n, shp, dt))
        blk = esC.enter_context(nc.Block())
        dist = sb("l1t_dist", [128, 8], F32)
        dldp = [sb(f"l1t_dld{i}", [128, 8], F32) for i in range(2)]
        tmp8 = sb("l1t_tmp8", [128, 8], F32)
        q8 = sb("l1t_q8", [128, 8], F32)
        nrn = sb("l1t_nrn", [128, 8], F32)
        n2 = sb("l1t_n2", [128, 8], F32)
        tmp1 = sb("l1t_tmp1c", [128, 1], F32)
        pentile = sb("l1t_pentile", [128, 1], F32)
        pens = sb("l1t_pens", [1, 8], F32)
        onesCf = sb("l1t_onesCf", [1, 128], F32)
        pone = ps("l1t_pone", [128, 512], F32)

        for sname in ["ld", "xo0", "xo1", "xc", "xm", "so1", "ta", "va", "sa", "ga",
                      "xa0", "xa1", "os"]:
            p.sem(sname)

        @blk.sync
        def _(e):
            e.dma_start(onesCf[:1, :], cs["ones1f"][:, :]).then_inc(p.inc("ld", 16), 16)
            if "pens" in io:
                e.dma_start(pens[:1, :], io["pens"][:, :]).then_inc(p.inc("ld", 16), 16)
            else:
                e.dma_start(pens[:1, 0:1], io["penpart"][:, :]).then_inc(p.inc("ld", 16), 16)

        @blk.vector
        def _(e):
            p.wge(e, "ld")
            if "pens" in io:
                p.bar(e, "va", e.reduce_sum(tmp1[0:1, 0:1], pens[0:1, :], axis=mybir.AxisListType.X))
                e.tensor_scalar(tmp1[0:1, 0:1], tmp1[0:1, 0:1], 1.0 / (NCORES * T), 1.0,
                                ALU.mult, ALU.add).then_inc(p.inc("va", 1), 1)
            else:
                e.tensor_scalar(tmp1[0:1, 0:1], pens[0:1, 0:1], 1.0 / T, 1.0,
                                ALU.mult, ALU.add).then_inc(p.inc("va", 1), 1)

        @blk.tensor
        def _(e):
            p.wge(e, "va")
            e.matmul(pone[:, 0:1], onesCf[:1, :], tmp1[0:1, 0:1], start=True, stop=True,
                     skip_group_check=True).then_inc(p.inc("ta", 1), 1)

        @blk.vector
        def _(e):
            p.wge(e, "ta")
            e.tensor_copy(pentile[:, :], pone[:, 0:1]).then_inc(p.inc("va", 1), 1)

        va_pen = p.c["va"]
        rt_used = {}

        for g in range(NTPC):
            par = g % 2

            @blk.sync
            def _(e, g=g, par=par):
                if g >= 2:
                    p.wge(e, "va", rt_used.get(("c2", g - 2), 0))
                e.dma_start(dldp[par][:, :], io["distsc"][g * 128:(g + 1) * 128, :]).then_inc(p.inc(f"xo{par}", 16), 16)

            xo_now = (f"xo{par}", p.c[f"xo{par}"])

            @blk.vector
            def _(e, g=g, par=par, xo_now=xo_now, va_pen=va_pen):
                p.wge(e, xo_now[0], xo_now[1])
                p.wge(e, "va", va_pen)
                p.wge(e, "so1")
                p.bar(e, "va", e.tensor_scalar(dist[:, :], dldp[par][:, :], pentile[:, :], None, ALU.mult))
                p.bar(e, "va", e.reduce_max(tmp1[:, :], dist[:, :], axis=mybir.AxisListType.X))
                p.bar(e, "va", e.tensor_scalar(tmp8[:, :], dist[:, :], tmp1[:, :], None, ALU.is_ge))
                p.bar(e, "va", e.scalar_tensor_tensor(q8[:, :], tmp8[:, :], -1e30, dist[:, :],
                                                      ALU.mult, ALU.add))
                p.bar(e, "va", e.reduce_max(n2[:, 0:1], q8[:, :], axis=mybir.AxisListType.X))
                p.bar(e, "va", e.tensor_scalar(nrn[:, :], q8[:, :], n2[:, 0:1], None, ALU.is_ge))
                e.tensor_tensor(n2[:, 1:2], n2[:, 0:1], tmp1[:, :], ALU.subtract).then_inc(p.inc("va", 1), 1)

            va_t2 = p.c["va"]

            @blk.scalar
            def _(e, va_t2=va_t2):
                p.wge(e, "va", va_t2)
                e.activation(n2[:, 2:3], n2[:, 1:2], AF.Exp).then_inc(p.inc("sa", 1), 1)

            sa_e2 = p.c["sa"]

            @blk.vector
            def _(e, g=g, sa_e2=sa_e2):
                p.wge(e, "sa", sa_e2)
                p.bar(e, "va", e.tensor_scalar(n2[:, 3:4], n2[:, 2:3], 1.0, None, ALU.add))
                p.bar(e, "va", e.reciprocal(n2[:, 3:4], n2[:, 3:4]))
                p.bar(e, "va", e.tensor_tensor(n2[:, 4:5], n2[:, 2:3], n2[:, 3:4], ALU.mult))
                p.bar(e, "va", e.tensor_scalar(tmp8[:, :], tmp8[:, :], n2[:, 3:4], None, ALU.mult))
                p.bar(e, "va", e.tensor_scalar(nrn[:, :], nrn[:, :], n2[:, 4:5], None, ALU.mult))
                e.tensor_tensor(dist[:, :], tmp8[:, :], nrn[:, :], ALU.add).then_inc(p.inc("va", 1), 1)

            va_cw = p.c["va"]
            rt_used[("c2", g)] = p.c["va"]

            @blk.sync
            def _(e, g=g, va_cw=va_cw):
                p.wge(e, "va", va_cw)
                e.dma_start(io["cwsc"][g * 128:(g + 1) * 128, :], dist[:, :]).then_inc(p.inc("so1", 16), 16)

        @blk.sync
        def _(e):
            p.wge(e, "so1")

    emit_l1_ffn_moe(p, io)
    return


def emit_l1_ffn_moe(p, io):
    nc = p.nc
    cs = io["consts"]
    E, FF = 8, 2048
    CH = min(512, T // 4)
    NTT = CH // 128
    QT = (T // 4) // 128
    es = contextlib.ExitStack()
    with es:
        sb = lambda n, shp, dt=F32: es.enter_context(nc.sbuf_tensor(n, shp, dt))
        ps = lambda n, shp, dt=F32: es.enter_context(nc.psum_tensor(n, shp, dt))
        blk = es.enter_context(nc.Block())
        acc = sb("l1f_acc", [128, 16 * D], BF16)
        gw_sb = [sb(f"l1f_gw_{k}", [128, FF], BF16) for k in range(KT)]
        uw_sb = [sb(f"l1f_uw_{k}", [128, FF], BF16) for k in range(KT)]
        dw_sb = [sb(f"l1f_dw_{k}", [128, D], BF16) for k in range(16)]
        xfm = [[sb(f"l1f_xfm{bb}_{k}", [128, CH], BF16) for k in range(KT)] for bb in range(2)]
        zt = sb("l1f_zt", [128, 16 * CH], BF16)
        st_ = sb("l1f_st", [128, CH], F32)
        cwt = sb("l1f_cwt", [128, NTT * 8], F32)
        x0m = sb("l1f_x0m", [128, D], BF16)
        xpre = sb("l1f_xpreM", [128, D], F32)
        lnb = (sb("l1f_statM", [128, 12], F32), sb("l1f_mvM", [128, 2], F32),
               sb("l1f_mnegM", [128, 1], F32), sb("l1f_rstdM", [128, 1], F32),
               sb("l1f_xoutM", [128, D], F32), sb("l1f_epsM", [128, 1], F32))

        gp = [ps(f"l1f_gp{i}", [128, CH], F32) for i in range(2)]
        upp = [ps(f"l1f_up{i}", [128, CH], F32) for i in range(2)]
        dn = [ps(f"l1f_dn{i}", [128, 512], F32) for i in range(2)]

        @blk.vector
        def _(e):
            e.memset(lnb[5][:, :], LN_EPS).then_inc(p.inc("va", 1), 1)

        @blk.gpsimd
        def _(e):
            for ei in range(E):
                e.dma_start(io["gwsc"][ei * FF:(ei + 1) * FF, :], io["gate_w"][ei, :, :]).then_inc(p.inc("ga", 16), 16)
                e.dma_start(io["uwsc"][ei * FF:(ei + 1) * FF, :], io["up_w"][ei, :, :]).then_inc(p.inc("ga", 16), 16)
                e.dma_start(io["dwsc"][ei * D:(ei + 1) * D, :], io["down_w"][ei, :, :]).then_inc(p.inc("ga", 16), 16)
            e.dma_start(io["gwsc"][E * FF:(E + 1) * FF, :], io["shared_gate_w"][:, :]).then_inc(p.inc("ga", 16), 16)
            e.dma_start(io["uwsc"][E * FF:(E + 1) * FF, :], io["shared_up_w"][:, :]).then_inc(p.inc("ga", 16), 16)
            e.dma_start(io["dwsc"][E * D:(E + 1) * D, :], io["shared_down_w"][:, :]).then_inc(p.inc("ga", 16), 16)

        ga_all = p.c["ga"]

        load_done = {}
        wbuf_free = {}

        def emit_wload(idx):
            ei = idx % (E + 1)
            @blk.scalar
            def _(e, ei=ei, idx=idx):
                p.wge(e, "ga", ga_all)
                p.wge(e, "ta", wbuf_free.get(idx - 1, 0))
                for k in range(KT):
                    e.dma_start(gw_sb[k][:, :], io["gwsc"][ei * FF:(ei + 1) * FF, k * 128:(k + 1) * 128],
                                transpose=True).then_inc(p.inc("ld", 16), 16)
                    e.dma_start(uw_sb[k][:, :], io["uwsc"][ei * FF:(ei + 1) * FF, k * 128:(k + 1) * 128],
                                transpose=True).then_inc(p.inc("ld", 16), 16)
                for k in range(16):
                    e.dma_start(dw_sb[k][:, :],
                                io["dwsc"][ei * D:(ei + 1) * D, k * 128:(k + 1) * 128],
                                transpose=True).then_inc(p.inc("ld", 16), 16)
            load_done[idx] = p.c["ld"]

        NCHH = (T // 4) // CH
        st_free = 0
        strip_used = {}
        cw_used = {}
        zt_used = {}
        gp_free = {}
        up_free = {}
        dn_free = {}
        acc_ln_read = 0

        emit_wload(0)

        qctr = 0
        NQTR = 4
        for half in range(NQTR):
            tok0h = half * (T // 4)
            for ei in range(E + 1):
                idx = half * (E + 1) + ei
                for c in range(NCHH):
                    tok0 = tok0h + c * CH
                    spar = qctr % 2

                    @blk.scalar
                    def _(e, tok0=tok0, spar=spar, q=qctr):
                        if q >= 2:
                            p.wge(e, "ta", strip_used.get(q - 2, 0))
                        for k in range(KT):
                            e.dma_start(xfm[spar][k][:, :],
                                        io["x_in"][tok0:tok0 + CH, k * 128:(k + 1) * 128],
                                        transpose=True).then_inc(p.inc(f"xa{spar}", 16), 16)

                    xa_now = (f"xa{spar}", p.c[f"xa{spar}"])

                    if ei < E:
                        @blk.sync
                        def _(e, tok0=tok0, q=qctr):
                            if q >= 1:
                                p.wge(e, "va", cw_used.get(q - 1, 0))
                            e.dma_start(cwt[:, :].rearrange("q (tt c) -> q tt c", tt=NTT),
                                        io["cwsc"][tok0:tok0 + CH, :].rearrange("(tt q) c -> q tt c", q=128)).then_inc(p.inc("xc", 16), 16)
                    xo_now = ("xc", p.c["xc"])

                    for m in range(16):
                        mp = m % 2
                        @blk.tensor
                        def _(e, m=m, mp=mp, spar=spar, idx=idx, q=qctr):
                            if m == 0:
                                p.wge(e, "ld", load_done[idx])
                                p.wge(e, xa_now[0], xa_now[1])
                            p.wge(e, "sa", gp_free.get(mp, 0))
                            p.wge(e, "va", up_free.get(mp, 0))
                            inst = None
                            for k in range(KT):
                                e.matmul(gp[mp][:, :], gw_sb[k][:, m * 128:(m + 1) * 128],
                                         xfm[spar][k][:, :], start=(k == 0), stop=(k == KT - 1))
                            for k in range(KT):
                                inst = e.matmul(upp[mp][:, :], uw_sb[k][:, m * 128:(m + 1) * 128],
                                                xfm[spar][k][:, :], start=(k == 0), stop=(k == KT - 1))
                            inst.then_inc(p.inc("ta", 1), 1)

                        ta_gu = p.c["ta"]

                        @blk.scalar
                        def _(e, m=m, mp=mp, ta_gu=ta_gu):
                            p.wge(e, "ta", ta_gu)
                            p.wge(e, "va", st_free)
                            e.activation(st_[:, :], gp[mp][:, :], AF.Silu).then_inc(p.inc("sa", 1), 1)

                        gp_free[mp] = p.c["sa"]
                        sa_silu = p.c["sa"]

                        @blk.vector
                        def _(e, m=m, mp=mp, sa_silu=sa_silu, q=qctr):
                            p.wge(e, "sa", sa_silu)
                            p.wge(e, "ta", zt_used.get(q - 1, 0) if m == 0 else 0)
                            e.tensor_tensor(zt[:, m * CH:(m + 1) * CH], st_[:, :], upp[mp][:, :],
                                            ALU.mult).then_inc(p.inc("va", 1), 1)

                        up_free[mp] = p.c["va"]
                        st_free = p.c["va"]

                    va_z = p.c["va"]

                    for tt in range(NTT):
                        t_loc = c * NTT + tt
                        for n in range(2):
                            dp = (tt * 2 + n) % 2
                            @blk.tensor
                            def _(e, tt=tt, n=n, dp=dp, idx=idx, va_z=va_z):
                                p.wge(e, "va", va_z)
                                p.wge(e, "va", dn_free.get(dp, 0))
                                inst = None
                                for k2 in range(16):
                                    inst = e.matmul(dn[dp][:, :], zt[:, k2 * CH + tt * 128: k2 * CH + (tt + 1) * 128],
                                                    dw_sb[k2][:, n * 512:(n + 1) * 512],
                                                    start=(k2 == 0), stop=(k2 == 15))
                                inst.then_inc(p.inc("ta", 1), 1)

                            ta_dn = p.c["ta"]

                            @blk.vector
                            def _(e, tt=tt, n=n, dp=dp, ei=ei, t_loc=t_loc, ta_dn=ta_dn, xo_now=xo_now, half=half, aln=acc_ln_read):
                                p.wge(e, "ta", ta_dn)
                                if ei < E:
                                    p.wge(e, xo_now[0], xo_now[1])
                                if ei == 0:
                                    p.wge(e, "va", aln)
                                asl = acc[:, t_loc * D + n * 512: t_loc * D + (n + 1) * 512]
                                if ei == 0:
                                    e.tensor_scalar(asl, dn[dp][:, :], cwt[:, tt * 8 + ei:tt * 8 + ei + 1],
                                                    None, ALU.mult).then_inc(p.inc("va", 1), 1)
                                elif ei < E:
                                    e.scalar_tensor_tensor(asl, dn[dp][:, :], cwt[:, tt * 8 + ei:tt * 8 + ei + 1],
                                                           asl, ALU.mult, ALU.add).then_inc(p.inc("va", 1), 1)
                                else:
                                    e.tensor_tensor(asl, asl, dn[dp][:, :], ALU.add).then_inc(p.inc("va", 1), 1)

                            dn_free[dp] = p.c["va"]

                    cw_used[qctr] = p.c["va"]
                    strip_used[qctr] = p.c["ta"]
                    zt_used[qctr] = p.c["ta"]
                    qctr += 1

                wbuf_free[idx] = p.c["ta"]
                if idx + 1 <= NQTR * (E + 1) - 1:
                    emit_wload(idx + 1)

            va_moe = p.c["va"]
            x0_free = 0
            for tl in range(QT):
                g = half * QT + tl
                @blk.sync
                def _(e, g=g, x0_free=x0_free):
                    if x0_free:
                        p.wge(e, "va", x0_free)
                    e.dma_start(x0m[:, :], io["x_in"][g * 128:(g + 1) * 128, :]).then_inc(p.inc("xm", 16), 16)

                xo_now = p.c["xm"]

                @blk.vector
                def _(e, tl=tl, xo_now=xo_now, va_moe=va_moe):
                    p.wge(e, "va", va_moe)
                    p.wge(e, "xm", xo_now)
                    p.wge(e, "os")
                    e.tensor_tensor(xpre[:, 0:512], x0m[:, 0:512], acc[:, tl * D:tl * D + 512],
                                    ALU.add).then_inc(p.inc("va", 1), 1)
                    e.tensor_tensor(xpre[:, 512:1024], x0m[:, 512:1024], acc[:, tl * D + 512:tl * D + 1024],
                                    ALU.add).then_inc(p.inc("va", 1), 1)

                x0_free = p.c["va"]
                emit_ln_store(p, blk, lnb, xpre, io["x_out"], g * 128)

            acc_ln_read = p.c["va"]

        @blk.sync
        def _(e):
            p.wge(e, "os")
    return



def emit_phase_barrier(nc, p):
    snap = [(n, v) for n, v in p.c.items() if v > 0]
    with nc.Block() as blk:
        for engname in ["tensor", "vector", "scalar", "sync", "gpsimd"]:
            @getattr(blk, engname)
            def _(e, snap=snap):
                for n, v in snap:
                    e.wait_ge(p.sems[n], v)


def build_fused_a_program():
    nc = bass.Bass("TRN2", target_bir_lowering=False)
    G3 = 3072
    consts = declare_consts(nc)

    def dt(name, shape, dtype=F32, kind=None):
        h = nc.dram_tensor(name, shape, dtype, kind=kind) if kind else nc.dram_tensor(name, shape, dtype)
        return h

    io1 = {"consts": consts}
    io1["x_in"] = dt("x_in", [T, D], F32, "ExternalInput").ap()
    for w in ["Wq", "Wk", "Wv", "Wo"]:
        io1[w] = dt("a0_" + w, [D, D], F32, "ExternalInput").ap()
    for b in ["bq", "bk", "bv", "bo"]:
        io1[b] = dt("a0_" + b, [D], F32, "ExternalInput").ap()
    io1["x_out"] = dt("x1sc", [T, D], BF16).ap()
    io1["xbf"] = dt("a0_xbf", [T, D], BF16).ap()
    io1["qsc"] = dt("a0_qsc", [T, D], BF16).ap()
    io1["wsc"] = dt("a0_wsc", [4 * D, D], BF16).ap()

    io2 = {"consts": consts, "x_in": io1["x_out"]}
    io2["gate_W"] = dt("f0_gate_W", [D, 8], F32, "ExternalInput").ap()
    io2["W1"] = dt("f0_W1", [8, D, 512], F32, "ExternalInput").ap()
    io2["b1"] = dt("f0_b1", [8, 512], F32, "ExternalInput").ap()
    io2["W2"] = dt("f0_W2", [8, 512, D], F32, "ExternalInput").ap()
    io2["b2"] = dt("f0_b2", [8, D], F32, "ExternalInput").ap()
    io2["Ws1"] = dt("f0_Ws1", [D, 512], F32, "ExternalInput").ap()
    io2["bs1"] = dt("f0_bs1", [512], F32, "ExternalInput").ap()
    io2["Ws2"] = dt("f0_Ws2", [512, D], F32, "ExternalInput").ap()
    io2["bs2"] = dt("f0_bs2", [D], F32, "ExternalInput").ap()
    io2["route_scale"] = dt("f0_route_scale", [1], F32, "ExternalInput").ap()
    io2["x_out"] = dt("x2sc", [T, D], BF16).ap()

    io3 = {"consts": consts, "x_in": io2["x_out"]}
    for w in ["Wk", "Wv", "Wr", "Wo"]:
        io3[w] = dt("a1_" + w, [D, D], F32, "ExternalInput").ap()
    io3["time_decay"] = dt("a1_time_decay", [16, 64], F32, "ExternalInput").ap()
    io3["time_first"] = dt("a1_time_first", [16, 64], F32, "ExternalInput").ap()
    io3["x_out"] = dt("x3_out", [T, D], BF16).ap()
    for wsc in ["wksc", "wvsc", "wrsc", "wosc"]:
        io3[wsc] = dt("a1_" + wsc, [D, D], BF16).ap()

    io4 = {"consts": consts, "x_in": io3["x_out"]}
    io4["gru_w_ih"] = dt("f1_gru_w_ih", [G3, D], F32, "ExternalInput").ap()
    io4["gru_w_hh"] = dt("f1_gru_w_hh", [G3, D], F32, "ExternalInput").ap()
    io4["expr_w"] = dt("f1_expr_w", [D, D], F32, "ExternalInput").ap()
    io4["ctx_w"] = dt("f1_ctx_w", [D, D], F32, "ExternalInput").ap()
    io4["ctx_b"] = dt("f1_ctx_b", [D], F32, "ExternalInput").ap()
    io4["distsc"] = dt("distsc", [T, 8], F32).ap()
    io4["penpart"] = dt("penpart", [1, 1], F32).ap()
    io4["wihsc"] = dt("f1_wihsc", [G3, D], BF16).ap()
    io4["whhsc"] = dt("f1_whhsc", [G3, D], BF16).ap()
    io4["ctxsc"] = dt("f1_ctxsc", [D, D], BF16).ap()
    io4["exprsc"] = dt("f1_exprsc", [D, D], BF16).ap()
    xwsc_h = nc.dram_tensor("f1_xwsc", [S * BPC, G3], BF16)
    io4["xwsc"] = xwsc_h.ap()
    io4["xwsc_h"] = xwsc_h
    hsc_h = nc.dram_tensor("f1_hsc", [S * BPC, D], BF16)
    io4["hsc"] = hsc_h.ap()
    io4["hsc_h"] = hsc_h

    io5 = {"consts": consts, "x_in": io3["x_out"],
           "distsc": io4["distsc"], "penpart": io4["penpart"]}
    io5["gate_w"] = dt("f1_gate_w", [8, 2048, D], F32, "ExternalInput").ap()
    io5["up_w"] = dt("f1_up_w", [8, 2048, D], F32, "ExternalInput").ap()
    io5["down_w"] = dt("f1_down_w", [8, D, 2048], F32, "ExternalInput").ap()
    io5["shared_gate_w"] = dt("f1_shared_gate_w", [2048, D], F32, "ExternalInput").ap()
    io5["shared_up_w"] = dt("f1_shared_up_w", [2048, D], F32, "ExternalInput").ap()
    io5["shared_down_w"] = dt("f1_shared_down_w", [D, 2048], F32, "ExternalInput").ap()
    io5["x_out"] = dt("x_out", [T, D], F32, "ExternalOutput").ap()
    io5["cwsc"] = dt("f1_cwsc", [T, 8], F32).ap()
    io5["gwsc"] = dt("f1_gwsc", [9 * 2048, D], BF16).ap()
    io5["uwsc"] = dt("f1_uwsc", [9 * 2048, D], BF16).ap()
    io5["dwsc"] = dt("f1_dwsc", [9 * D, 2048], BF16).ap()

    es = contextlib.ExitStack()
    with es:
        p = Prog(nc, es)
        for emit, io_ in [(emit_l0_attn, io1), (emit_l0_ffn, io2),
                          (emit_l1_attn, io3), (emit_l1_ffn, io4),
                          (emit_l1_ffn_tail2, io5)]:
            with contextlib.ExitStack() as eph:
                p.es = eph
                emit(p, io_)
            if emit is not emit_l1_ffn_tail2:
                emit_phase_barrier(nc, p)
        p.es = es
    return nc


def build_l0_attn_program():
    nc = bass.Bass("TRN2", target_bir_lowering=False)
    io = {}
    io["x_in"] = nc.dram_tensor("x_in", [T, D], F32, kind="ExternalInput").ap()
    for w in ["Wq", "Wk", "Wv", "Wo"]:
        io[w] = nc.dram_tensor(w, [D, D], F32, kind="ExternalInput").ap()
    for b in ["bq", "bk", "bv", "bo"]:
        io[b] = nc.dram_tensor(b, [D], F32, kind="ExternalInput").ap()
    io["x_out"] = nc.dram_tensor("x_out", [T, D], BF16, kind="ExternalOutput").ap()
    io["xbf"] = nc.dram_tensor("xbf", [T, D], BF16).ap()
    io["qsc"] = nc.dram_tensor("qsc", [T, D], BF16).ap()
    io["wsc"] = nc.dram_tensor("wsc", [4 * D, D], BF16).ap()
    io["consts"] = declare_consts(nc)
    es = contextlib.ExitStack()
    with es:
        p = Prog(nc, es)
        emit_l0_attn(p, io)
    return nc


def build_l1_attn_program():
    nc = bass.Bass("TRN2", target_bir_lowering=False)
    io = {}
    io["x_in"] = nc.dram_tensor("x_in", [T, D], BF16, kind="ExternalInput").ap()
    for w in ["Wk", "Wv", "Wr", "Wo"]:
        io[w] = nc.dram_tensor(w, [D, D], F32, kind="ExternalInput").ap()
    io["time_decay"] = nc.dram_tensor("time_decay", [16, 64], F32, kind="ExternalInput").ap()
    io["time_first"] = nc.dram_tensor("time_first", [16, 64], F32, kind="ExternalInput").ap()
    io["x_out"] = nc.dram_tensor("x_out", [T, D], BF16, kind="ExternalOutput").ap()
    for wsc in ["wksc", "wvsc", "wrsc", "wosc"]:
        io[wsc] = nc.dram_tensor(wsc, [D, D], BF16).ap()
    io["consts"] = declare_consts(nc)
    es = contextlib.ExitStack()
    with es:
        p = Prog(nc, es)
        emit_l1_attn(p, io)
    return nc


def build_l1_ffn_a_program():
    nc = bass.Bass("TRN2", target_bir_lowering=False)
    G3 = 3072
    io = {}
    io["x_in"] = nc.dram_tensor("x_in", [T, D], BF16, kind="ExternalInput").ap()
    io["gru_w_ih"] = nc.dram_tensor("gru_w_ih", [G3, D], F32, kind="ExternalInput").ap()
    io["gru_w_hh"] = nc.dram_tensor("gru_w_hh", [G3, D], F32, kind="ExternalInput").ap()
    io["expr_w"] = nc.dram_tensor("expr_w", [D, D], F32, kind="ExternalInput").ap()
    io["ctx_w"] = nc.dram_tensor("ctx_w", [D, D], F32, kind="ExternalInput").ap()
    io["ctx_b"] = nc.dram_tensor("ctx_b", [D], F32, kind="ExternalInput").ap()
    io["distsc"] = nc.dram_tensor("distsc", [T, 8], F32, kind="ExternalOutput").ap()
    io["penpart"] = nc.dram_tensor("penpart", [1, 1], F32, kind="ExternalOutput").ap()
    G3 = 3072
    io["wihsc"] = nc.dram_tensor("wihsc", [G3, D], BF16).ap()
    io["whhsc"] = nc.dram_tensor("whhsc", [G3, D], BF16).ap()
    io["ctxsc"] = nc.dram_tensor("ctxsc", [D, D], BF16).ap()
    io["exprsc"] = nc.dram_tensor("exprsc", [D, D], BF16).ap()
    xwsc_h = nc.dram_tensor("xwsc", [S * BPC, G3], BF16)
    io["xwsc"] = xwsc_h.ap()
    io["xwsc_h"] = xwsc_h
    hsc_h = nc.dram_tensor("hsc", [S * BPC, D], BF16)
    io["hsc"] = hsc_h.ap()
    io["hsc_h"] = hsc_h
    io["consts"] = declare_consts(nc)
    es = contextlib.ExitStack()
    with es:
        p = Prog(nc, es)
        emit_l1_ffn(p, io)
    return nc


def build_l1_ffn_b_program():
    nc = bass.Bass("TRN2", target_bir_lowering=False)
    io = {}
    io["x_in"] = nc.dram_tensor("x_in", [T, D], BF16, kind="ExternalInput").ap()
    io["distsc"] = nc.dram_tensor("distsc", [T, 8], F32, kind="ExternalInput").ap()
    io["pens"] = nc.dram_tensor("pens", [1, 8], F32, kind="ExternalInput").ap()
    io["gate_w"] = nc.dram_tensor("gate_w", [8, 2048, D], F32, kind="ExternalInput").ap()
    io["up_w"] = nc.dram_tensor("up_w", [8, 2048, D], F32, kind="ExternalInput").ap()
    io["down_w"] = nc.dram_tensor("down_w", [8, D, 2048], F32, kind="ExternalInput").ap()
    io["shared_gate_w"] = nc.dram_tensor("shared_gate_w", [2048, D], F32, kind="ExternalInput").ap()
    io["shared_up_w"] = nc.dram_tensor("shared_up_w", [2048, D], F32, kind="ExternalInput").ap()
    io["shared_down_w"] = nc.dram_tensor("shared_down_w", [D, 2048], F32, kind="ExternalInput").ap()
    io["x_out"] = nc.dram_tensor("x_out", [T, D], F32, kind="ExternalOutput").ap()
    io["cwsc"] = nc.dram_tensor("cwsc", [T, 8], F32).ap()
    io["gwsc"] = nc.dram_tensor("gwsc", [9 * 2048, D], BF16).ap()
    io["uwsc"] = nc.dram_tensor("uwsc", [9 * 2048, D], BF16).ap()
    io["dwsc"] = nc.dram_tensor("dwsc", [9 * D, 2048], BF16).ap()
    io["consts"] = declare_consts(nc)
    es = contextlib.ExitStack()
    with es:
        p = Prog(nc, es)
        emit_l1_ffn_tail2(p, io)
    return nc


_CACHED = {}


def _get_prog(name, builder):
    if name not in _CACHED:
        _CACHED[name] = builder()
    return _CACHED[name]


def kernel(**inputs):
    assert S == 4096 and T == 8192, "kernel() requires full dims"
    consts = host_consts()
    x = np.ascontiguousarray(np.asarray(inputs["L0_attn_x"], dtype=np.float32))
    shards = [np.ascontiguousarray(x[2 * c:2 * c + 2].reshape(T, D)) for c in range(NCORES)]

    def amap(names, src_prefix, dst_prefix):
        return {dst_prefix + k: np.ascontiguousarray(np.asarray(inputs[src_prefix + k], dtype=np.float32))
                for k in names}

    ncA = _get_prog("fusedall", build_fused_a_program)
    wA = {}
    wA.update(amap(["Wq", "bq", "Wk", "bk", "Wv", "bv", "Wo", "bo"], "L0_attn_", "a0_"))
    wA.update(amap(["gate_W", "W1", "b1", "W2", "b2", "Ws1", "bs1", "Ws2", "bs2"], "L0_ffn_", "f0_"))
    wA["f0_route_scale"] = np.asarray(inputs["L0_ffn_route_scale"], dtype=np.float32).reshape(1)
    wA.update(amap(["Wk", "Wv", "Wr", "Wo", "time_decay", "time_first"], "L1_attn_", "a1_"))
    wA.update(amap(["gru_w_ih", "gru_w_hh", "expr_w", "ctx_w", "ctx_b",
                    "gate_w", "up_w", "down_w",
                    "shared_gate_w", "shared_up_w", "shared_down_w"], "L1_ffn_", "f1_"))
    in_maps = []
    for c in range(NCORES):
        m = {"x_in": shards[c]}
        m.update(wA)
        m.update(consts)
        in_maps.append(m)
    res = run_bass_kernel_spmd(ncA, in_maps, core_ids=list(range(NCORES)))
    outs = [res.results[c]["x_out"] for c in range(NCORES)]
    return np.concatenate([o.reshape(BPC, S, D) for o in outs], axis=0)


def build_l0_ffn_program():
    nc = bass.Bass("TRN2", target_bir_lowering=False)
    io = {}
    io["x_in"] = nc.dram_tensor("x_in", [T, D], BF16, kind="ExternalInput").ap()
    io["gate_W"] = nc.dram_tensor("gate_W", [D, 8], F32, kind="ExternalInput").ap()
    io["W1"] = nc.dram_tensor("W1", [8, D, 512], F32, kind="ExternalInput").ap()
    io["b1"] = nc.dram_tensor("b1", [8, 512], F32, kind="ExternalInput").ap()
    io["W2"] = nc.dram_tensor("W2", [8, 512, D], F32, kind="ExternalInput").ap()
    io["b2"] = nc.dram_tensor("b2", [8, D], F32, kind="ExternalInput").ap()
    io["Ws1"] = nc.dram_tensor("Ws1", [D, 512], F32, kind="ExternalInput").ap()
    io["bs1"] = nc.dram_tensor("bs1", [512], F32, kind="ExternalInput").ap()
    io["Ws2"] = nc.dram_tensor("Ws2", [512, D], F32, kind="ExternalInput").ap()
    io["bs2"] = nc.dram_tensor("bs2", [D], F32, kind="ExternalInput").ap()
    io["route_scale"] = nc.dram_tensor("route_scale", [1], F32, kind="ExternalInput").ap()
    io["x_out"] = nc.dram_tensor("x_out", [T, D], BF16, kind="ExternalOutput").ap()
    io["consts"] = declare_consts(nc)
    es = contextlib.ExitStack()
    with es:
        p = Prog(nc, es)
        emit_l0_ffn(p, io)
    return nc


# version 09
# speedup vs baseline: 1.1552x; 1.1552x over previous
import contextlib
import numpy as np
import ml_dtypes
import concourse.bass as bass
import concourse.mybir as mybir
from concourse.bass_utils import run_bass_kernel_spmd

F32 = mybir.dt.float32
BF16 = mybir.dt.bfloat16
AF = mybir.ActivationFunctionType
ALU = mybir.AluOpType

NCORES = 8
BPC = 2
S = 4096
T = BPC * S
D = 1024
KT = D // 128
H = 16
HD = 64
LN_EPS = 1e-5
BF = ml_dtypes.bfloat16
SIM_NO_COLLECTIVE = False


def set_dims(s):
    global S, T
    S = s
    T = BPC * s


class Prog:
    def __init__(self, nc, es):
        self.nc = nc
        self.es = es
        self.es_sem = es
        self.c = {}
        self.sems = {}

    def sem(self, name):
        if name not in self.sems:
            self.sems[name] = self.es_sem.enter_context(self.nc.semaphore(name))
            self.c[name] = 0
        return self.sems[name]

    def inc(self, name, k):
        self.c[name] += k
        return self.sems[name]

    def wge(self, e, name, val=None):
        if val is None:
            val = self.c[name]
        if val > 0:
            e.wait_ge(self.sems[name], val)

    def bar(self, e, name, inst):
        inst.then_inc(self.inc(name, 1), 1)
        self.wge(e, name)

    def sb(self, name, shape, dt=F32):
        return self.es.enter_context(self.nc.sbuf_tensor(name, shape, dt))

    def ps(self, name, shape, dt=F32):
        return self.es.enter_context(self.nc.psum_tensor(name, shape, dt))


def host_consts():
    c = {}
    c["c_ones1"] = np.ones((1, 128), dtype=BF)
    c["c_ident"] = np.eye(128, dtype=BF)
    ese = np.zeros((8, 8 * 128), dtype=BF)
    for e in range(8):
        ese[e, e * 128:(e + 1) * 128] = 1
    c["c_esel"] = ese
    c["c_ones1f"] = np.ones((1, 128), dtype=np.float32)
    ii = np.arange(128)
    c["c_dgrid"] = np.where(ii[None, :] >= ii[:, None], ii[None, :] - ii[:, None], 0).astype(np.float32)
    c["c_mgridf"] = (ii[None, :] >= ii[:, None]).astype(np.float32)
    c["c_ramp"] = np.arange(130, dtype=np.float32)[None, :]
    c["c_rrcol"] = (127.0 - ii).astype(np.float32)[:, None]
    c["c_identf"] = np.eye(128, dtype=np.float32)
    c["c_onescol"] = np.ones((128, 1), dtype=BF)
    c["c_onescolf"] = np.ones((128, 1), dtype=np.float32)
    bsl = np.zeros((128, 4), dtype=BF)
    bsl[:, 0] = 1
    bsl[:, 3] = 1
    c["c_bsel"] = bsl
    sel = np.zeros((16, 8 * 128), dtype=BF)
    for p in range(8):
        sel[2 * p, p * 128: p * 128 + 64] = 1
        sel[2 * p + 1, p * 128 + 64: p * 128 + 128] = 1
    c["c_pairsel"] = sel
    return c


def declare_consts(nc):
    return {
        "ones1": nc.dram_tensor("c_ones1", [1, 128], BF16, kind="ExternalInput").ap(),
        "pairsel": nc.dram_tensor("c_pairsel", [16, 8 * 128], BF16, kind="ExternalInput").ap(),
        "ident": nc.dram_tensor("c_ident", [128, 128], BF16, kind="ExternalInput").ap(),
        "esel": nc.dram_tensor("c_esel", [8, 8 * 128], BF16, kind="ExternalInput").ap(),
        "ones1f": nc.dram_tensor("c_ones1f", [1, 128], F32, kind="ExternalInput").ap(),
        "dgrid": nc.dram_tensor("c_dgrid", [128, 128], F32, kind="ExternalInput").ap(),
        "mgridf": nc.dram_tensor("c_mgridf", [128, 128], F32, kind="ExternalInput").ap(),
        "ramp": nc.dram_tensor("c_ramp", [1, 130], F32, kind="ExternalInput").ap(),
        "rrcol": nc.dram_tensor("c_rrcol", [128, 1], F32, kind="ExternalInput").ap(),
        "identf": nc.dram_tensor("c_identf", [128, 128], F32, kind="ExternalInput").ap(),
        "onescol": nc.dram_tensor("c_onescol", [128, 1], BF16, kind="ExternalInput").ap(),
        "onescolf": nc.dram_tensor("c_onescolf", [128, 1], F32, kind="ExternalInput").ap(),
        "bsel": nc.dram_tensor("c_bsel", [128, 4], BF16, kind="ExternalInput").ap(),
    }


def emit_ln_store(p, blk, lnb, xpre, x_out, tok0):
    stat, mv, mneg, rstd, xout_tok, epscol = lnb

    @blk.vector
    def _(e):
        p.wge(e, "va")
        p.bar(e, "va", e.bn_stats(stat[:, 0:6], xpre[:, 0:512]))
        p.bar(e, "va", e.bn_stats(stat[:, 6:12], xpre[:, 512:1024]))
        p.bar(e, "va", e.bn_aggr(mv[:, :], stat[:, :]))
        e.tensor_scalar(mneg[:, :], mv[:, 0:1], -1.0, None, ALU.mult).then_inc(p.inc("va", 1), 1)

    @blk.scalar
    def _(e):
        p.wge(e, "va")
        p.bar(e, "sa", e.activation(rstd[:, :], mv[:, 1:2], AF.Ln, bias=epscol[:, :]))
        e.activation(rstd[:, :], rstd[:, :], AF.Exp, scale=-0.5).then_inc(p.inc("sa", 1), 1)

    @blk.vector
    def _(e):
        p.wge(e, "sa")
        e.tensor_scalar(xout_tok[:, :], xpre[:, :], mneg[:, :], rstd[:, :],
                        ALU.add, ALU.mult).then_inc(p.inc("va", 1), 1)

    @blk.sync
    def _(e):
        p.wge(e, "va")
        e.dma_start(x_out[tok0:tok0 + 128, :], xout_tok[:, :]).then_inc(p.inc("os", 16), 16)



def emit_l0_attn(p, io, dbg=None):
    nc = p.nc
    cs = io["consts"]
    blk = p.es.enter_context(nc.Block())

    wq = [p.sb(f"l0a_wq{k}", [128, D], BF16) for k in range(KT)]
    wk = [p.sb(f"l0a_wk{k}", [128, D], BF16) for k in range(KT)]
    wv = [p.sb(f"l0a_wv{k}", [128, D], BF16) for k in range(KT)]
    wo = [p.sb(f"l0a_wo{k}", [128, D], BF16) for k in range(KT)]
    brow = p.sb("l0a_brow", [1, 4 * D], BF16)
    ones1 = p.sb("l0a_ones1", [1, 128], BF16)
    pairsel = p.sb("l0a_psel", [16, 8 * 128], BF16)
    onescol = p.sb("l0a_onescol", [128, 1], BF16)

    STRIP = min(2048, S)
    NSTRIP = T // STRIP
    NTILE = STRIP // 128
    TPB = S // 128
    xf = [[p.sb(f"l0a_xf{b}_{k}", [128, STRIP], BF16) for k in range(KT)] for b in range(2)]

    q_tok = [p.sb(f"l0a_qtok{i}", [128, D], BF16) for i in range(2)]
    k_tok = [p.sb(f"l0a_ktok{i}", [128, D], BF16) for i in range(2)]
    v_tok = [p.sb(f"l0a_vtok{i}", [128, D], BF16) for i in range(2)]
    xm_ = p.sb("l0a_xm", [128, 512], F32)
    ee = [p.sb(f"l0a_ee{i}", [128, 512], F32) for i in range(2)]
    rl = [p.sb(f"l0a_rl{i}", [128, 512], F32) for i in range(2)]

    kv_sb = p.sb("l0a_kvsb", [128, 8 * 64], BF16)
    kvacc = p.sb("l0a_kvacc", [128, 8 * 64], F32)
    ksacc = p.sb("l0a_ksacc", [128, 8], F32)
    ks_sb = [p.sb(f"l0a_kssb{k}", [128, 16], BF16) for k in range(KT)]
    qf = [[p.sb(f"l0a_qf{b}_{k}", [128, 512], BF16) for k in range(KT)] for b in range(2)]
    att = [p.sb(f"l0a_att{k}", [128, 512], BF16) for k in range(KT)]
    rz_bf = p.sb("l0a_rzbf", [16, 512], BF16)
    rz_f = p.sb("l0a_rzf", [16, 512], F32)
    rzb_sb = [p.sb(f"l0a_rzbsb{i}", [128, 512], BF16) for i in range(2)]
    x0_tok = p.sb("l0a_x0tok", [128, D], BF16)
    xpre = p.sb("l0a_xpre", [128, D], F32)
    lnb = (p.sb("l0a_stat", [128, 12], F32), p.sb("l0a_mv", [128, 2], F32),
           p.sb("l0a_mneg", [128, 1], F32), p.sb("l0a_rstd", [128, 1], F32),
           p.sb("l0a_xouttok", [128, D], BF16), p.sb("l0a_eps", [128, 1], F32))

    bank = [p.ps(f"l0a_bank{i}", [128, 512], F32) for i in range(8)]
    pkv = bank[6]
    pks = bank[7]

    for sname in ["ld", "xa0", "xa1", "qa0", "qa1", "xo", "os", "ta", "va", "sa", "ga", "so0", "so1"]:
        p.sem(sname)

    @blk.gpsimd
    def _(e):
        for j in range(T // 128):
            e.dma_start(io["xbf"][j * 128:(j + 1) * 128, :],
                        io["x_in"][j * 128:(j + 1) * 128, :]).then_inc(p.inc("ga", 16), 16)
        for i, w in enumerate(["Wq", "Wk", "Wv", "Wo"]):
            for j in range(D // 128):
                e.dma_start(io["wsc"][i * D + j * 128:i * D + (j + 1) * 128, :],
                            io[w][j * 128:(j + 1) * 128, :]).then_inc(p.inc("ga", 16), 16)
        for i, b in enumerate(["bq", "bk", "bv", "bo"]):
            e.dma_start(brow[:1, i * D:(i + 1) * D], io[b][None, :]).then_inc(p.inc("ga", 16), 16)

    ga_all = p.c["ga"]

    @blk.sync
    def _(e):
        e.dma_start(ones1[:1, :], cs["ones1"][:, :]).then_inc(p.inc("ld", 16), 16)
        e.dma_start(pairsel[:, :], cs["pairsel"][:, :]).then_inc(p.inc("ld", 16), 16)

    @blk.scalar
    def _(e):
        p.wge(e, "ga", ga_all)
        for k in range(KT):
            for wi, wt in enumerate([wq, wk, wv, wo]):
                e.dma_start(wt[k][:, :], io["wsc"][wi * D:(wi + 1) * D, k * 128:(k + 1) * 128],
                            transpose=True).then_inc(p.inc("ld", 16), 16)

    ld_all = p.c["ld"]

    @blk.vector
    def _(e):
        e.memset(lnb[5][:, :], LN_EPS).then_inc(p.inc("va", 1), 1)
        e.memset(onescol[:, :], 1.0).then_inc(p.inc("va", 1), 1)
        for k in range(KT):
            e.memset(ks_sb[k][:, :], 0.0).then_inc(p.inc("va", 1), 1)

    va_init = p.c["va"]

    def proj_chunk(e, bi, xf_s, ti, wt, n, brow_off):
        for k in range(KT):
            e.matmul(bank[bi][:, :], xf_s[k][:, ti * 128:(ti + 1) * 128],
                     wt[k][:, n * 512:(n + 1) * 512], start=(k == 0), stop=False)
        return e.matmul(bank[bi][:, :], ones1[:1, :],
                        brow[:1, brow_off + n * 512: brow_off + (n + 1) * 512],
                        start=False, stop=True)

    strip_loaded = {}
    strip_consumed = {}
    sa_free = {}
    va_adds = {}
    kv_accum = {}
    sa_vtok = {}
    ta_kv = {}
    so_q = {}

    for b in range(BPC):
        for g in range(b * TPB, (b + 1) * TPB):
            s, ti = divmod(g, NTILE)
            par = g % 2
            if ti == 0:
                sb_ = s % 2
                @blk.scalar
                def _(e, s=s, sb_=sb_):
                    if s == 0:
                        p.wge(e, "ga", ga_all)
                    if s >= 2:
                        p.wge(e, "ta", strip_consumed[s - 2])
                    for k in range(KT):
                        e.dma_start(xf[sb_][k][:, :],
                                    io["xbf"][s * STRIP:(s + 1) * STRIP, k * 128:(k + 1) * 128],
                                    transpose=True).then_inc(p.inc(f"xa{s % 2}", 16), 16)
                strip_loaded[s] = (f"xa{s % 2}", p.c[f"xa{s % 2}"])

                if dbg is not None and isinstance(dbg, dict) and "xf" in dbg and s == 0:
                    @blk.sync
                    def _(e, sb_=sb_):
                        p.wge(e, "xa0", strip_loaded[0][1])
                        for k in range(KT):
                            e.dma_start(dbg["xf"][k, :, :], xf[sb_][k][:, :]).then_inc(p.inc("os", 16), 16)

            @blk.tensor
            def _(e, s=s, ti=ti, g=g):
                if g == 0:
                    p.wge(e, "ld", ld_all)
                    p.wge(e, "va", va_init)
                if g == TPB:
                    p.wge(e, "va")
                    p.wge(e, "sa")
                p.wge(e, strip_loaded[s][0], strip_loaded[s][1])
                if g >= 1:
                    p.wge(e, "sa", sa_free[g - 1])
                proj_chunk(e, 0, xf[s % 2], ti, wq, 0, 0)
                proj_chunk(e, 1, xf[s % 2], ti, wq, 1, 0)
                proj_chunk(e, 2, xf[s % 2], ti, wk, 0, D)
                proj_chunk(e, 3, xf[s % 2], ti, wk, 1, D)
                proj_chunk(e, 4, xf[s % 2], ti, wv, 0, 2 * D)
                proj_chunk(e, 5, xf[s % 2], ti, wv, 1, 2 * D).then_inc(p.inc("ta", 1), 1)

            ta_proj = p.c["ta"]
            va_base = p.c["va"]
            sa_ci = []

            @blk.scalar
            def _(e, ta_proj=ta_proj, g=g, par=par, va_base=va_base):
                p.wge(e, "ta", ta_proj)
                p.wge(e, "va", va_base)
                if g >= 2:
                    p.wge(e, "ta", ta_kv[g - 2])
                for ci, bi in enumerate([0, 1, 2, 3]):
                    if ci >= 2:
                        p.wge(e, "va", va_base + ci - 1)
                    p.wge(e, "sa")
                    p.bar(e, "sa", e.activation(xm_[:, :], bank[bi][:, :], AF.Relu, scale=-1.0))
                    e.activation(ee[ci % 2][:, :], xm_[:, :], AF.Exp, scale=-1.0).then_inc(p.inc("sa", 1), 1)
                    e.activation(rl[ci % 2][:, :], bank[bi][:, :], AF.Relu).then_inc(p.inc("sa", 1), 1)
                    sa_ci.append(p.c["sa"])
                e.activation(v_tok[par][:, 0:512], bank[4][:, :], AF.Copy).then_inc(p.inc("sa", 1), 1)
                e.activation(v_tok[par][:, 512:1024], bank[5][:, :], AF.Copy).then_inc(p.inc("sa", 1), 1)

            sa_free[g] = p.c["sa"]
            sa_vtok[g] = p.c["sa"]

            @blk.vector
            def _(e, g=g, par=par, sa_ci=sa_ci):
                if g >= 2:
                    p.wge(e, "ta", ta_kv[g - 2])
                    p.wge(e, so_q[g - 2][0], so_q[g - 2][1])
                for ci in range(4):
                    p.wge(e, "sa", sa_ci[ci])
                    dst = q_tok[par] if ci < 2 else k_tok[par]
                    n = ci % 2
                    e.tensor_tensor(dst[:, n * 512:(n + 1) * 512], ee[ci % 2][:, :],
                                    rl[ci % 2][:, :], ALU.add).then_inc(p.inc("va", 1), 1)

            va_adds[g] = p.c["va"]

            @blk.tensor
            def _(e, g=g, par=par):
                p.wge(e, "va", va_adds[g])
                p.wge(e, "sa", sa_vtok[g])
                if g >= 1:
                    p.wge(e, "va", kv_accum[g - 1])
                inst = None
                for h in range(H):
                    pp, hi = divmod(h, 2)
                    e.matmul(pkv[hi * 64:(hi + 1) * 64, pp * 64:(pp + 1) * 64],
                             k_tok[par][:, h * 64:(h + 1) * 64],
                             v_tok[par][:, h * 64:(h + 1) * 64],
                             start=True, stop=True, skip_group_check=True)
                    inst = e.matmul(pks[hi * 64:(hi + 1) * 64, pp:pp + 1],
                                    k_tok[par][:, h * 64:(h + 1) * 64],
                                    onescol[:, :], start=True, stop=True,
                                    skip_group_check=True)
                inst.then_inc(p.inc("ta", 1), 1)

            ta_kv[g] = p.c["ta"]
            if ti == NTILE - 1:
                strip_consumed[s] = p.c["ta"]

            @blk.vector
            def _(e, g=g, first=(g % TPB == 0)):
                p.wge(e, "ta", ta_kv[g])
                if first:
                    e.tensor_copy(kvacc[:, :], pkv[:, :]).then_inc(p.inc("va", 1), 1)
                    e.tensor_copy(ksacc[:, :], pks[:, 0:8]).then_inc(p.inc("va", 1), 1)
                else:
                    e.tensor_tensor(kvacc[:, :], kvacc[:, :], pkv[:, :], ALU.add).then_inc(p.inc("va", 1), 1)
                    e.tensor_tensor(ksacc[:, :], ksacc[:, :], pks[:, 0:8], ALU.add).then_inc(p.inc("va", 1), 1)

            kv_accum[g] = p.c["va"]

            @blk.sync
            def _(e, g=g, par=par):
                p.wge(e, "va", va_adds[g])
                e.dma_start(io["qsc"][g * 128:(g + 1) * 128, :], q_tok[par][:, :]).then_inc(p.inc(f"so{par}", 16), 16)

            so_q[g] = (f"so{par}", p.c[f"so{par}"])

        @blk.vector
        def _(e):
            p.wge(e, "va", kv_accum[(b + 1) * TPB - 1])
            e.tensor_copy(kv_sb[:, :], kvacc[:, :]).then_inc(p.inc("va", 1), 1)
            for k in range(KT):
                e.tensor_copy(ks_sb[k][0:64, 2 * k:2 * k + 1], ksacc[0:64, k:k + 1]).then_inc(p.inc("va", 1), 1)
            for k in range(KT):
                e.tensor_copy(ks_sb[k][64:128, 2 * k + 1:2 * k + 2], ksacc[64:128, k:k + 1]).then_inc(p.inc("va", 1), 1)

        va_kvsb = p.c["va"]

        if dbg is not None:
            dbgkv = dbg["kv"] if isinstance(dbg, dict) else dbg

            @blk.sync
            def _(e, b=b, va_kvsb=va_kvsb, dbgkv=dbgkv):
                p.wge(e, "va", va_kvsb)
                e.dma_start(dbgkv[b, :, 0:512], kvacc[:, :]).then_inc(p.inc("os", 16), 16)
                e.dma_start(dbgkv[b, :, 512:520], ksacc[:, :]).then_inc(p.inc("os", 16), 16)

        NCH = S // 512
        rz_done = {}
        copy_done = {}
        att_done = {}
        qf_used = {}
        wo_evac = 0
        for c in range(NCH):
            cpar = c % 2
            tok0 = b * S + c * 512

            @blk.scalar
            def _(e, b=b, c=c, cpar=cpar, tok0=tok0):
                p.wge(e, "so0")
                p.wge(e, "so1")
                if c >= 2:
                    p.wge(e, "ta", qf_used[c - 2])
                for k in range(KT):
                    e.dma_start(qf[cpar][k][:, :],
                                io["qsc"][tok0:tok0 + 512, k * 128:(k + 1) * 128],
                                transpose=True).then_inc(p.inc(f"qa{cpar}", 16), 16)

            qa_now = (f"qa{cpar}", p.c[f"qa{cpar}"])

            @blk.tensor
            def _(e, c=c, cpar=cpar, qa_now=qa_now):
                p.wge(e, qa_now[0], qa_now[1])
                p.wge(e, "va", va_kvsb)
                p.wge(e, "sa")
                if c >= 1:
                    p.wge(e, "va", rz_done[c - 1])
                for k in range(KT - 1):
                    e.matmul(bank[0][0:16, :], ks_sb[k][:, :], qf[cpar][k][:, :],
                             start=(k == 0), stop=False)
                e.matmul(bank[0][0:16, :], ks_sb[KT - 1][:, :], qf[cpar][KT - 1][:, :],
                         start=False, stop=True).then_inc(p.inc("ta", 1), 1)

            ta_z = p.c["ta"]

            @blk.vector
            def _(e, ta_z=ta_z):
                p.wge(e, "ta", ta_z)
                p.bar(e, "va", e.tensor_scalar(rz_f[:, :], bank[0][0:16, :], 1e-6, None, ALU.add))
                p.bar(e, "va", e.reciprocal(rz_f[:, :], rz_f[:, :]))
                e.tensor_copy(rz_bf[:, :], rz_f[:, :]).then_inc(p.inc("va", 1), 1)

            rz_done[c] = p.c["va"]
            va_rz = p.c["va"]

            for pp in range(8):
                pb_bank = bank[1]
                po_bank = bank[2 + (pp % 2)]

                @blk.tensor
                def _(e, pp=pp, c=c, va_rz=va_rz):
                    p.wge(e, "va", va_rz)
                    if (c, pp - 1) in copy_done:
                        p.wge(e, "va", copy_done[(c, pp - 1)])
                    elif c >= 1:
                        p.wge(e, "va", copy_done[(c - 1, 7)])
                    e.matmul(pb_bank[:, :], pairsel[:, pp * 128:(pp + 1) * 128], rz_bf[:, :],
                             start=True, stop=True).then_inc(p.inc("ta", 1), 1)

                ta_pb = p.c["ta"]

                @blk.vector
                def _(e, ta_pb=ta_pb, pp=pp):
                    p.wge(e, "ta", ta_pb)
                    e.tensor_copy(rzb_sb[pp % 2][:, :], pb_bank[:, :]).then_inc(p.inc("va", 1), 1)

                copy_done[(c, pp)] = p.c["va"]
                va_rzb = p.c["va"]

                @blk.tensor
                def _(e, pp=pp, c=c, cpar=cpar, va_rzb=va_rzb):
                    p.wge(e, "va", va_rzb)
                    if (c, pp - 2) in att_done:
                        p.wge(e, "va", att_done[(c, pp - 2)])
                    elif c >= 1:
                        p.wge(e, "va", att_done[(c - 1, 6 + pp)])
                    e.matmul(po_bank[0:64, :], kv_sb[0:64, pp * 64:(pp + 1) * 64],
                             qf[cpar][pp][0:64, :], start=True, stop=True)
                    e.matmul(po_bank[64:128, :], kv_sb[64:128, pp * 64:(pp + 1) * 64],
                             qf[cpar][pp][64:128, :], start=True, stop=True).then_inc(p.inc("ta", 1), 1)

                ta_po = p.c["ta"]

                @blk.vector
                def _(e, pp=pp, ta_po=ta_po):
                    p.wge(e, "ta", ta_po)
                    e.tensor_tensor(att[pp][:, :], po_bank[:, :], rzb_sb[pp % 2][:, :],
                                    ALU.mult).then_inc(p.inc("va", 1), 1)

                att_done[(c, pp)] = p.c["va"]

            va_att = p.c["va"]
            qf_used[c] = p.c["ta"]

            for j in range(4):
                tt0 = tok0 + j * 128

                @blk.sync
                def _(e, tt0=tt0, wo_evac=wo_evac):
                    if wo_evac:
                        p.wge(e, "va", wo_evac)
                    e.dma_start(x0_tok[:, :], io["xbf"][tt0:tt0 + 128, :]).then_inc(p.inc("xo", 16), 16)

                ld_x0 = p.c["xo"]

                @blk.tensor
                def _(e, j=j, va_att=va_att, wo_evac=wo_evac):
                    p.wge(e, "va", va_att)
                    if wo_evac:
                        p.wge(e, "va", wo_evac)
                    for n in range(2):
                        for k in range(KT):
                            e.matmul(bank[4 + n][:, :], att[k][:, j * 128:(j + 1) * 128],
                                     wo[k][:, n * 512:(n + 1) * 512], start=(k == 0), stop=False)
                        inst = e.matmul(bank[4 + n][:, :], ones1[:1, :],
                                        brow[:1, 3 * D + n * 512: 3 * D + (n + 1) * 512],
                                        start=False, stop=True)
                    inst.then_inc(p.inc("ta", 1), 1)

                ta_wo = p.c["ta"]

                @blk.vector
                def _(e, ta_wo=ta_wo, ld_x0=ld_x0):
                    p.wge(e, "ta", ta_wo)
                    p.wge(e, "xo", ld_x0)
                    p.wge(e, "os")
                    e.tensor_tensor(xpre[:, 0:512], x0_tok[:, 0:512], bank[4][:, :],
                                    ALU.add).then_inc(p.inc("va", 1), 1)
                    e.tensor_tensor(xpre[:, 512:1024], x0_tok[:, 512:1024], bank[5][:, :],
                                    ALU.add).then_inc(p.inc("va", 1), 1)

                wo_evac = p.c["va"]
                emit_ln_store(p, blk, lnb, xpre, io["x_out"], tt0)

    @blk.sync
    def _(e):
        p.wge(e, "so0")
        p.wge(e, "so1")
        p.wge(e, "os")
    return



def emit_l0_ffn(p, io):
    nc = p.nc
    cs = io["consts"]
    blk = p.es.enter_context(nc.Block())
    E, HF = 8, 512

    w1 = [p.sb(f"l0f_w1_{e}", [128, KT * HF], BF16) for e in range(E)]
    w2 = [p.sb(f"l0f_w2_{e}", [128, 4 * D], BF16) for e in range(E)]
    ws1 = p.sb("l0f_ws1", [128, KT * HF], BF16)
    ws2 = p.sb("l0f_ws2", [128, 4 * D], BF16)
    gw = p.sb("l0f_gw", [128, KT * 8], BF16)
    b1c = p.sb("l0f_b1c", [128, E * 4], F32)
    bs1c = p.sb("l0f_bs1c", [128, 4], F32)
    b2m = p.sb("l0f_b2m", [8, D], BF16)
    bs2r = p.sb("l0f_bs2r", [1, D], BF16)
    ones1 = p.sb("l0f_ones1", [1, 128], BF16)
    esel = p.sb("l0f_esel", [8, 8 * 128], BF16)
    rs_bf = p.sb("l0f_rsbf", [1, 1], BF16)
    rs_col = p.sb("l0f_rs", [128, 1], F32)
    idn = p.sb("l0f_idn", [128, 128], BF16)

    STRIP = min(1024, S)
    NSTRIP = T // STRIP
    NTILE = STRIP // 128
    xf = [[p.sb(f"l0f_xf{b}_{k}", [128, STRIP], BF16) for k in range(KT)] for b in range(2)]

    sc_sb = p.sb("l0f_scsb", [8, 128], BF16)
    s_tok = p.sb("l0f_stok", [128, 8], F32)
    m1 = p.sb("l0f_m1", [128, 1], F32)
    m2 = p.sb("l0f_m2", [128, 1], F32)
    msk1 = p.sb("l0f_msk1", [128, 8], F32)
    msk2 = p.sb("l0f_msk2", [128, 8], F32)
    smsk = p.sb("l0f_smsk", [128, 8], F32)
    r12 = p.sb("l0f_r12", [128, 1], F32)
    g_tok = p.sb("l0f_gtok", [128, 8], BF16)
    gT = p.sb("l0f_gT", [8, 128], BF16)
    gb_sb = [p.sb(f"l0f_gb{i}", [128, 128], BF16) for i in range(2)]
    h_sb = [p.sb(f"l0f_hsb{i}", [128, 4 * 128], BF16) for i in range(2)]
    x0_tok = p.sb("l0f_x0tok", [128, D], BF16)
    xpre = p.sb("l0f_xpre", [128, D], F32)
    lnb = (p.sb("l0f_stat", [128, 12], F32), p.sb("l0f_mv", [128, 2], F32),
           p.sb("l0f_mneg", [128, 1], F32), p.sb("l0f_rstd", [128, 1], F32),
           p.sb("l0f_xouttok", [128, D], BF16), p.sb("l0f_eps", [128, 1], F32))

    acc = [p.ps(f"l0f_acc{i}", [128, 512], F32) for i in range(2)]
    hps = [p.ps(f"l0f_hps{i}", [128, 512], F32) for i in range(2)]
    gbp = [p.ps(f"l0f_gbp{i}", [128, 128], F32) for i in range(2)]
    scp = p.ps("l0f_scp", [128, 512], F32)
    trp = p.ps("l0f_trp", [128, 128], BF16)

    for sname in ["ld", "xa0", "xa1", "xo", "os", "ta", "va", "sa", "ga", "so"]:
        p.sem(sname)

    @blk.gpsimd
    def _(e):
        for ee in range(E):
            e.dma_start(w1[ee][:, :].rearrange("p (k h) -> p k h", k=KT),
                        io["W1"][ee, :, :].rearrange("(k p) h -> p k h", p=128)).then_inc(p.inc("ga", 16), 16)
            e.dma_start(w2[ee][:, :].rearrange("p (k d) -> p k d", k=4),
                        io["W2"][ee, :, :].rearrange("(k p) d -> p k d", p=128)).then_inc(p.inc("ga", 16), 16)
        e.dma_start(ws1[:, :].rearrange("p (k h) -> p k h", k=KT),
                    io["Ws1"].rearrange("(k p) h -> p k h", p=128)).then_inc(p.inc("ga", 16), 16)
        e.dma_start(ws2[:, :].rearrange("p (k d) -> p k d", k=4),
                    io["Ws2"].rearrange("(k p) d -> p k d", p=128)).then_inc(p.inc("ga", 16), 16)
        e.dma_start(gw[:, :].rearrange("p (k ee) -> p k ee", k=KT),
                    io["gate_W"].rearrange("(k p) ee -> p k ee", p=128)).then_inc(p.inc("ga", 16), 16)
        e.dma_start(b2m[:, :], io["b2"][:, :]).then_inc(p.inc("ga", 16), 16)
        e.dma_start(bs2r[:1, :], io["bs2"][None, :]).then_inc(p.inc("ga", 16), 16)
        e.dma_start(rs_bf[:1, :], io["route_scale"][None, :]).then_inc(p.inc("ga", 16), 16)

    ga_all = p.c["ga"]

    @blk.sync
    def _(e):
        e.dma_start(ones1[:1, :], cs["ones1"][:, :]).then_inc(p.inc("ld", 16), 16)
        e.dma_start(esel[:, :], cs["esel"][:, :]).then_inc(p.inc("ld", 16), 16)
        e.dma_start(idn[:, :], cs["ident"][:, :]).then_inc(p.inc("ld", 16), 16)
        for ee in range(E):
            for m in range(4):
                e.dma_start(b1c[:, ee * 4 + m:ee * 4 + m + 1],
                            io["b1"][ee, m * 128:(m + 1) * 128].rearrange("(q o) -> q o", o=1)).then_inc(p.inc("ld", 16), 16)
        for m in range(4):
            e.dma_start(bs1c[:, m:m + 1],
                        io["bs1"][m * 128:(m + 1) * 128].rearrange("(q o) -> q o", o=1)).then_inc(p.inc("ld", 16), 16)

    ld_all = p.c["ld"]

    @blk.vector
    def _(e):
        e.memset(lnb[5][:, :], LN_EPS).then_inc(p.inc("va", 1), 1)

    @blk.tensor
    def _(e):
        p.wge(e, "ga", ga_all)
        p.wge(e, "ld", ld_all)
        e.matmul(gbp[0][:, 0:1], ones1[:1, :], rs_bf[:1, :], start=True, stop=True).then_inc(p.inc("ta", 1), 1)

    @blk.vector
    def _(e):
        p.wge(e, "ta")
        e.tensor_copy(rs_col[:, :], gbp[0][:, 0:1]).then_inc(p.inc("va", 1), 1)

    va_init = p.c["va"]

    strip_loaded = {}
    strip_consumed = {}

    sig_done = 0
    gt_evac = 0
    gb_evac = {}
    h_used = {}
    h_evac = {}
    acc_read = 0

    for g in range(T // 128):
        s, ti = divmod(g, NTILE)
        if ti == 0:
            sb_ = s % 2
            @blk.scalar
            def _(e, s=s, sb_=sb_):
                if s >= 2:
                    p.wge(e, "ta", strip_consumed[s - 2])
                for k in range(KT):
                    e.dma_start(xf[sb_][k][:, :],
                                io["x_in"][s * STRIP:(s + 1) * STRIP, k * 128:(k + 1) * 128],
                                transpose=True).then_inc(p.inc(f"xa{s % 2}", 16), 16)
            strip_loaded[s] = (f"xa{s % 2}", p.c[f"xa{s % 2}"])

        xs = xf[s % 2]
        tsl = slice(ti * 128, (ti + 1) * 128)

        @blk.tensor
        def _(e, g=g, s=s, tsl=tsl, sig_done=sig_done):
            if g == 0:
                p.wge(e, "va", va_init)
            p.wge(e, strip_loaded[s][0], strip_loaded[s][1])
            if g >= 1:
                p.wge(e, "sa", sig_done)
            inst = None
            for k in range(KT):
                inst = e.matmul(scp[0:8, 0:128], gw[:, k * 8:(k + 1) * 8], xs[k][:, tsl],
                                start=(k == 0), stop=(k == KT - 1))
            inst.then_inc(p.inc("ta", 1), 1)

        ta_sc = p.c["ta"]

        @blk.vector
        def _(e, ta_sc=ta_sc, g=g, gt_evac=gt_evac):
            p.wge(e, "ta", ta_sc)
            e.tensor_copy(sc_sb[:, :], scp[0:8, 0:128]).then_inc(p.inc("va", 1), 1)

        va_sc = p.c["va"]

        @blk.tensor
        def _(e, va_sc=va_sc, g=g, gt_evac=gt_evac):
            p.wge(e, "va", va_sc)
            if g >= 1:
                p.wge(e, "va", gt_evac)
            e.transpose(trp[:, 0:8], sc_sb[:, :], idn[0:8, 0:8]).then_inc(p.inc("ta", 1), 1)

        ta_tr = p.c["ta"]

        @blk.scalar
        def _(e, ta_tr=ta_tr):
            p.wge(e, "ta", ta_tr)
            e.activation(s_tok[:, :], trp[:, 0:8], AF.Sigmoid).then_inc(p.inc("sa", 1), 1)

        sig_done = p.c["sa"]

        @blk.vector
        def _(e):
            p.wge(e, "sa", sig_done)
            p.bar(e, "va", e.tensor_scalar(s_tok[:, :], s_tok[:, :], rs_col[:, :], None, ALU.mult))
            p.bar(e, "va", e.reduce_max(m1[:, :], s_tok[:, :], axis=mybir.AxisListType.X))
            p.bar(e, "va", e.tensor_scalar(msk1[:, :], s_tok[:, :], m1[:, :], None, ALU.is_ge))
            p.bar(e, "va", e.scalar_tensor_tensor(smsk[:, :], msk1[:, :], -1e30, s_tok[:, :], ALU.mult, ALU.add))
            p.bar(e, "va", e.reduce_max(m2[:, :], smsk[:, :], axis=mybir.AxisListType.X))
            p.bar(e, "va", e.tensor_scalar(msk2[:, :], smsk[:, :], m2[:, :], None, ALU.is_ge))
            p.bar(e, "va", e.tensor_tensor(r12[:, :], m1[:, :], m2[:, :], ALU.add))
            p.bar(e, "va", e.reciprocal(r12[:, :], r12[:, :]))
            p.bar(e, "va", e.tensor_tensor(msk1[:, :], msk1[:, :], msk2[:, :], ALU.add))
            p.bar(e, "va", e.tensor_tensor(msk1[:, :], msk1[:, :], s_tok[:, :], ALU.mult))
            e.tensor_scalar(g_tok[:, :], msk1[:, :], r12[:, :], None, ALU.mult).then_inc(p.inc("va", 1), 1)

        va_g = p.c["va"]

        @blk.tensor
        def _(e, va_g=va_g):
            p.wge(e, "va", va_g)
            e.transpose(trp[0:8, 0:128], g_tok[:, :], idn[:, :]).then_inc(p.inc("ta", 1), 1)

        ta_gt = p.c["ta"]

        @blk.vector
        def _(e, ta_gt=ta_gt):
            p.wge(e, "ta", ta_gt)
            e.tensor_copy(gT[:, :], trp[0:8, 0:128]).then_inc(p.inc("va", 1), 1)

        gt_evac = p.c["va"]

        n_acc_mm = 0
        first_acc = True
        for ei in range(E + 1):
            par = ei % 2
            is_sh = (ei == E)
            wgt1 = ws1 if is_sh else w1[ei]
            wgt2 = ws2 if is_sh else w2[ei]

            @blk.tensor
            def _(e, ei=ei, par=par, tsl=tsl, wgt1=wgt1, g=g):
                key = (par,)
                if (g, ei - 2) in h_used or (g > 0 and ei < 2):
                    pass
                if h_evac.get(par, 0):
                    p.wge(e, "sa", h_evac[par][0])
                    p.wge(e, "va", h_evac[par][1])
                inst = None
                for m in range(4):
                    for k in range(KT):
                        inst = e.matmul(hps[par][:, m * 128:(m + 1) * 128],
                                        wgt1[:, k * HF + m * 128: k * HF + (m + 1) * 128],
                                        xs[k][:, tsl], start=(k == 0), stop=(k == KT - 1))
                inst.then_inc(p.inc("ta", 1), 1)

            ta_h = p.c["ta"]

            if not is_sh:
                gt_now = gt_evac
                @blk.tensor
                def _(e, ei=ei, par=par, gt_now=gt_now):
                    p.wge(e, "va", gt_now)
                    if gb_evac.get(par, 0):
                        p.wge(e, "va", gb_evac[par])
                    e.matmul(gbp[par][:, :], esel[:, ei * 128:(ei + 1) * 128], gT[:, :],
                             start=True, stop=True).then_inc(p.inc("ta", 1), 1)

                ta_gb = p.c["ta"]

                @blk.vector
                def _(e, par=par, ta_gb=ta_gb):
                    p.wge(e, "ta", ta_gb)
                    e.tensor_copy(gb_sb[par][:, :], gbp[par][:, :]).then_inc(p.inc("va", 1), 1)

                gb_evac[par] = p.c["va"]

            @blk.scalar
            def _(e, ei=ei, par=par, ta_h=ta_h, is_sh=is_sh):
                p.wge(e, "ta", ta_h)
                if h_used.get(par, 0):
                    p.wge(e, "ta", h_used[par])
                for m in range(4):
                    bias = bs1c[:, m:m + 1] if is_sh else b1c[:, ei * 4 + m:ei * 4 + m + 1]
                    e.activation(h_sb[par][:, m * 128:(m + 1) * 128],
                                 hps[par][:, m * 128:(m + 1) * 128], AF.Gelu,
                                 bias=bias).then_inc(p.inc("sa", 1), 1)

            sa_h = p.c["sa"]

            if not is_sh:
                @blk.vector
                def _(e, par=par, sa_h=sa_h):
                    p.wge(e, "sa", sa_h)
                    p.wge(e, "va", gb_evac[par])
                    for m in range(4):
                        e.tensor_tensor(h_sb[par][:, m * 128:(m + 1) * 128],
                                        h_sb[par][:, m * 128:(m + 1) * 128],
                                        gb_sb[par][:, :], ALU.mult).then_inc(p.inc("va", 1), 1)
                h_ready = ("va", p.c["va"])
            else:
                h_ready = ("sa", sa_h)

            h_evac[par] = (sa_h, p.c["va"])

            @blk.tensor
            def _(e, ei=ei, par=par, h_ready=h_ready, first_acc=first_acc, g=g, acc_read=acc_read):
                p.wge(e, h_ready[0], h_ready[1])
                if first_acc and g >= 1:
                    p.wge(e, "va", acc_read)
                inst = None
                for n in range(2):
                    for k2 in range(4):
                        inst = e.matmul(acc[n][:, :], h_sb[par][:, k2 * 128:(k2 + 1) * 128],
                                        wgt2[:, k2 * D + n * 512: k2 * D + n * 512 + 512],
                                        start=(first_acc and k2 == 0),
                                        stop=False)
                inst.then_inc(p.inc("ta", 1), 1)

            h_used[par] = p.c["ta"]
            first_acc = False

        @blk.tensor
        def _(e, tsl=tsl, gt_now=gt_evac):
            p.wge(e, "va", gt_now)
            inst = None
            for n in range(2):
                e.matmul(acc[n][:, :], gT[:, :], b2m[:, n * 512:(n + 1) * 512], start=False, stop=False)
                inst = e.matmul(acc[n][:, :], ones1[:1, :], bs2r[:1, n * 512:(n + 1) * 512],
                                start=False, stop=True)
            inst.then_inc(p.inc("ta", 1), 1)

        ta_acc = p.c["ta"]
        if ti == NTILE - 1:
            strip_consumed[s] = p.c["ta"]

        @blk.sync
        def _(e, g=g, acc_read=acc_read):
            if acc_read:
                p.wge(e, "va", acc_read)
            e.dma_start(x0_tok[:, :], io["x_in"][g * 128:(g + 1) * 128, :]).then_inc(p.inc("xo", 16), 16)

        ld_x0 = p.c["xo"]

        @blk.vector
        def _(e, ta_acc=ta_acc, ld_x0=ld_x0):
            p.wge(e, "ta", ta_acc)
            p.wge(e, "xo", ld_x0)
            p.wge(e, "os")
            e.tensor_tensor(xpre[:, 0:512], x0_tok[:, 0:512], acc[0][:, :], ALU.add).then_inc(p.inc("va", 1), 1)
            e.tensor_tensor(xpre[:, 512:1024], x0_tok[:, 512:1024], acc[1][:, :], ALU.add).then_inc(p.inc("va", 1), 1)

        acc_read = p.c["va"]
        emit_ln_store(p, blk, lnb, xpre, io["x_out"], g * 128)

    @blk.sync
    def _(e):
        p.wge(e, "os")
    return



def emit_l1_attn(p, io):
    nc = p.nc
    cs = io["consts"]
    blk = p.es.enter_context(nc.Block())
    C = 128
    TPB = S // C

    wk = [p.sb(f"l1a_wk{k}", [128, D], BF16) for k in range(KT)]
    wv = [p.sb(f"l1a_wv{k}", [128, D], BF16) for k in range(KT)]
    wr = [p.sb(f"l1a_wr{k}", [128, D], BF16) for k in range(KT)]
    wo = [p.sb(f"l1a_wo{k}", [128, D], BF16) for k in range(KT)]
    ones1 = p.sb("l1a_ones1", [1, 128], BF16)
    ones1f = p.sb("l1a_ones1f", [1, 128], F32)
    idn = p.sb("l1a_idn", [128, 128], BF16)
    idnf = p.sb("l1a_idnf", [128, 128], F32)
    dgrid = p.sb("l1a_dgrid", [128, 128], F32)
    mgrid = p.sb("l1a_mgrid", [128, 128], F32)
    ramp = p.sb("l1a_ramp", [1, 130], F32)
    rrcol = p.sb("l1a_rrcol", [128, 1], F32)

    STRIP = min(2048, S)
    NSTRIP = T // STRIP
    NTILE = STRIP // 128
    xf = [[p.sb(f"l1a_xf{b}_{k}", [128, STRIP], BF16) for k in range(KT)] for b in range(2)]

    lamv = p.sb("l1a_lamv", [16, 1], F32)
    lnl = p.sb("l1a_lnl", [16, 1], F32)
    lrow = p.sb("l1a_lrow", [1, 16], F32)
    lc = p.sb("l1a_lc", [128, 1], F32)
    rampl = p.sb("l1a_rampl", [1, 128], F32)
    lamtile = [p.sb(f"l1a_lt{t}", [128, 128], BF16) for t in range(KT)]
    ptile = [p.sb(f"l1a_pt{h}", [128, 128], BF16) for h in range(H)]
    lamrev = p.sb("l1a_lamrev", [128, H], F32)
    lam128 = p.sb("l1a_lam128", [128, H], F32)
    ucol = p.sb("l1a_ucol", [128, KT], F32)
    tmpP = p.sb("l1a_tmpP", [128, 128], F32)
    tcol = p.sb("l1a_tcol", [128, 1], F32)

    k_tok = p.sb("l1a_ktok", [128, D], BF16)
    v_tok = p.sb("l1a_vtok", [128, D], BF16)
    r_tok = p.sb("l1a_rtok", [128, D], BF16)
    kf = p.sb("l1a_kf", [128, KT * 128], BF16)
    rf = p.sb("l1a_rf", [128, KT * 128], BF16)
    khat = p.sb("l1a_khat", [128, 128], BF16)
    rtl = p.sb("l1a_rtl", [128, 128], BF16)
    abuf = [p.sb(f"l1a_ab{i}", [128, 128], BF16) for i in range(2)]
    att = p.sb("l1a_att", [128, KT * 128], BF16)
    s_f32 = p.sb("l1a_sf32", [128, 8 * 64], F32)
    s_bf = p.sb("l1a_sbf", [128, 8 * 64], BF16)
    x0_tok = p.sb("l1a_x0tok", [128, D], BF16)
    xpre = p.sb("l1a_xpre", [128, D], F32)
    lnb = (p.sb("l1a_stat", [128, 12], F32), p.sb("l1a_mv", [128, 2], F32),
           p.sb("l1a_mneg", [128, 1], F32), p.sb("l1a_rstd", [128, 1], F32),
           p.sb("l1a_xouttok", [128, D], BF16), p.sb("l1a_eps", [128, 1], F32))

    bank = [p.ps(f"l1a_bank{i}", [128, 512], F32) for i in range(6)]
    trp = p.ps("l1a_trp", [128, 128], BF16)

    for sname in ["ld", "xa0", "xa1", "xo", "os", "ta", "va", "sa", "ga"]:
        p.sem(sname)

    @blk.gpsimd
    def _(e):
        for wnm, wsc in [("Wk", "wksc"), ("Wv", "wvsc"), ("Wr", "wrsc"), ("Wo", "wosc")]:
            e.dma_start(io[wsc][:, :], io[wnm][:, :]).then_inc(p.inc("ga", 16), 16)

    ga_all = p.c["ga"]

    @blk.sync
    def _(e):
        e.dma_start(ones1[:1, :], cs["ones1"][:, :]).then_inc(p.inc("ld", 16), 16)
        e.dma_start(ones1f[:1, :], cs["ones1f"][:, :]).then_inc(p.inc("ld", 16), 16)
        e.dma_start(idn[:, :], cs["ident"][:, :]).then_inc(p.inc("ld", 16), 16)
        e.dma_start(idnf[:, :], cs["identf"][:, :]).then_inc(p.inc("ld", 16), 16)
        e.dma_start(dgrid[:, :], cs["dgrid"][:, :]).then_inc(p.inc("ld", 16), 16)
        e.dma_start(mgrid[:, :], cs["mgridf"][:, :]).then_inc(p.inc("ld", 16), 16)
        e.dma_start(ramp[:1, :], cs["ramp"][:, :]).then_inc(p.inc("ld", 16), 16)
        e.dma_start(rrcol[:, :], cs["rrcol"][:, :]).then_inc(p.inc("ld", 16), 16)
        e.dma_start(lamv[:, :], io["time_decay"][:, 0:1], allow_slow_non_contiguous=True).then_inc(p.inc("ld", 16), 16)
        for t in range(KT):
            e.dma_start(ucol[:, t:t + 1],
                        io["time_first"][2 * t:2 * t + 2, :].rearrange("a (b o) -> (a b) o", o=1)).then_inc(p.inc("ld", 16), 16)

    @blk.scalar
    def _(e):
        p.wge(e, "ga", ga_all)
        for k in range(KT):
            for wt, wsc in [(wk, "wksc"), (wv, "wvsc"), (wr, "wrsc"), (wo, "wosc")]:
                e.dma_start(wt[k][:, :], io[wsc][:, k * 128:(k + 1) * 128],
                            transpose=True).then_inc(p.inc("ld", 16), 16)

    ld_all = p.c["ld"]

    @blk.vector
    def _(e):
        e.memset(lnb[5][:, :], LN_EPS).then_inc(p.inc("va", 1), 1)

    @blk.scalar
    def _(e):
        p.wge(e, "ld", ld_all)
        e.activation(lnl[:, :], lamv[0:16, :], AF.Ln).then_inc(p.inc("sa", 1), 1)

    @blk.tensor
    def _(e):
        p.wge(e, "sa")
        e.transpose(bank[4][0:1, 0:16], lnl[0:16, :], idnf[0:16, 0:16]).then_inc(p.inc("ta", 1), 1)

    @blk.vector
    def _(e):
        p.wge(e, "ta")
        e.tensor_copy(lrow[:1, :], bank[4][0:1, 0:16]).then_inc(p.inc("va", 1), 1)

    for t in range(KT):
        for hi in range(2):
            h = 2 * t + hi
            @blk.vector
            def _(e, h=h):
                p.wge(e, "va")
                p.wge(e, "ta")
                e.tensor_scalar(rampl[:1, :], ramp[0:1, 1:129], lrow[0:1, h:h + 1],
                                None, ALU.mult).then_inc(p.inc("va", 1), 1)

            @blk.tensor
            def _(e, h=h, hi=hi):
                p.wge(e, "va")
                if not (h == 0):
                    p.wge(e, "sa")
                e.matmul(bank[4][hi * 64:(hi + 1) * 64, 0:128], ones1f[:1, 0:64], rampl[:1, :],
                         start=True, stop=True).then_inc(p.inc("ta", 1), 1)

        @blk.scalar
        def _(e, t=t):
            p.wge(e, "ta")
            e.activation(lamtile[t][:, :], bank[4][:, 0:128], AF.Exp).then_inc(p.inc("sa", 1), 1)

    for h in range(H):
        @blk.tensor
        def _(e, h=h):
            p.wge(e, "sa")
            p.wge(e, "va")
            e.matmul(bank[4][:, 128:129], ones1f[:1, :], lrow[0:1, h:h + 1],
                     start=True, stop=True).then_inc(p.inc("ta", 1), 1)

        @blk.vector
        def _(e, h=h):
            p.wge(e, "ta")
            e.tensor_copy(lc[:, :], bank[4][:, 128:129]).then_inc(p.inc("va", 1), 1)

        @blk.vector
        def _(e, h=h):
            p.wge(e, "va")
            p.bar(e, "va", e.tensor_scalar(tmpP[:, :], dgrid[:, :], lc[:, :], None, ALU.mult))
            p.bar(e, "va", e.tensor_scalar(tcol[:, :], rrcol[:, :], lc[:, :], None, ALU.mult))
            e.tensor_scalar(lam128[:, h:h + 1], lc[:, :], 128.0, None, ALU.mult).then_inc(p.inc("va", 1), 1)

        @blk.scalar
        def _(e, h=h):
            p.wge(e, "va")
            p.bar(e, "sa", e.activation(tmpP[:, :], tmpP[:, :], AF.Exp))
            p.bar(e, "sa", e.activation(lamrev[:, h:h + 1], tcol[:, :], AF.Exp))
            e.activation(lam128[:, h:h + 1], lam128[:, h:h + 1], AF.Exp).then_inc(p.inc("sa", 1), 1)

        @blk.vector
        def _(e, h=h):
            p.wge(e, "sa")
            e.tensor_tensor(ptile[h][:, :], tmpP[:, :], mgrid[:, :], ALU.mult).then_inc(p.inc("va", 1), 1)

    va_setup = p.c["va"]
    sa_setup = p.c["sa"]

    strip_loaded = {}
    strip_consumed = {}
    proj_evac_for = {}
    proj_evac = 0
    amask_done = {}
    abuf_used = {}
    khat_used = {}
    tm_state_prev = 0
    y_free = {}
    trp_free = ("va", 0)
    rtl_used = 0
    tok_used_ta = 0
    tok_used_va = 0
    kf_used = 0
    att_used = 0
    wo_free = 0

    for b in range(BPC):
        @blk.vector
        def _(e, b=b):
            p.wge(e, "ta")
            e.memset(s_f32[:, :], 0.0).then_inc(p.inc("va", 1), 1)
            e.memset(s_bf[:, :], 0.0).then_inc(p.inc("va", 1), 1)

        va_zero = p.c["va"]

        for cc in range(TPB):
            g = b * TPB + cc
            s_, ti = divmod(g, NTILE)
            if ti == 0:
                sb_ = s_ % 2
                @blk.scalar
                def _(e, s_=s_, sb_=sb_):
                    if s_ >= 2:
                        p.wge(e, "ta", strip_consumed[s_ - 2])
                    for k in range(KT):
                        e.dma_start(xf[sb_][k][:, :],
                                    io["x_in"][s_ * STRIP:(s_ + 1) * STRIP, k * 128:(k + 1) * 128],
                                    transpose=True).then_inc(p.inc(f"xa{s_ % 2}", 16), 16)
                strip_loaded[s_] = (f"xa{s_ % 2}", p.c[f"xa{s_ % 2}"])

            xs = xf[s_ % 2]
            tsl = slice(ti * 128, (ti + 1) * 128)

            for wi, (wt, dst) in enumerate([(wk, k_tok), (wv, v_tok), (wr, r_tok)]):
                for n in range(2):
                    bi = (wi * 2 + n) % 2
                    @blk.tensor
                    def _(e, g=g, s_=s_, tsl=tsl, wt=wt, n=n, bi=bi, wi=wi, va_zero=va_zero):
                        if g == 0 and wi == 0 and n == 0:
                            p.wge(e, "va", va_setup)
                            p.wge(e, "sa", sa_setup)
                        if wi == 0 and n == 0:
                            p.wge(e, "va", va_zero)
                            p.wge(e, strip_loaded[s_][0], strip_loaded[s_][1])
                        p.wge(e, "sa", proj_evac_for.get(bi, 0))
                        p.wge(e, "va", amask_done.get(bi, 0))
                        inst = None
                        for k in range(KT):
                            inst = e.matmul(bank[bi][:, :], xs[k][:, tsl],
                                            wt[k][:, n * 512:(n + 1) * 512],
                                            start=(k == 0), stop=(k == KT - 1))
                        inst.then_inc(p.inc("ta", 1), 1)

                    ta_pr = p.c["ta"]

                    @blk.scalar
                    def _(e, dst=dst, n=n, bi=bi, ta_pr=ta_pr, tu=(tok_used_ta, tok_used_va)):
                        p.wge(e, "ta", ta_pr)
                        p.wge(e, "ta", tu[0])
                        p.wge(e, "va", tu[1])
                        e.activation(dst[:, n * 512:(n + 1) * 512], bank[bi][:, :],
                                     AF.Copy).then_inc(p.inc("sa", 1), 1)

                    proj_evac_for[bi] = p.c["sa"]
            proj_evac = p.c["sa"]

            for t in range(KT):
                @blk.tensor
                def _(e, t=t, tf=trp_free, pe=proj_evac, ku=kf_used):
                    p.wge(e, "sa", pe)
                    p.wge(e, tf[0], tf[1])
                    e.transpose(trp[:, :], k_tok[:, t * 128:(t + 1) * 128], idn[:, :]).then_inc(p.inc("ta", 1), 1)

                ta_t1 = p.c["ta"]

                @blk.vector
                def _(e, t=t, ta_t1=ta_t1, ku=kf_used):
                    p.wge(e, "ta", ta_t1)
                    p.wge(e, "ta", ku)
                    e.tensor_copy(kf[:, t * 128:(t + 1) * 128], trp[:, :]).then_inc(p.inc("va", 1), 1)

                trp_free = ("va", p.c["va"])

                @blk.tensor
                def _(e, t=t, tf=trp_free):
                    p.wge(e, tf[0], tf[1])
                    e.transpose(trp[:, :], r_tok[:, t * 128:(t + 1) * 128], idn[:, :]).then_inc(p.inc("ta", 1), 1)

                ta_t2 = p.c["ta"]

                @blk.scalar
                def _(e, t=t, ta_t2=ta_t2):
                    p.wge(e, "ta", ta_t2)
                    p.wge(e, "va")
                    e.activation(tmpP[:, :], trp[:, :], AF.Sigmoid).then_inc(p.inc("sa", 1), 1)

                trp_free = ("sa", p.c["sa"])
                sa_sig = p.c["sa"]

                @blk.vector
                def _(e, t=t, sa_sig=sa_sig, ku=kf_used):
                    p.wge(e, "sa", sa_sig)
                    p.wge(e, "ta", ku)
                    e.tensor_scalar(rf[:, t * 128:(t + 1) * 128], tmpP[:, :], ucol[:, t:t + 1],
                                    None, ALU.mult).then_inc(p.inc("va", 1), 1)

            va_fm = p.c["va"]

            for pp in range(KT):
                @blk.vector
                def _(e, pp=pp, va_fm=va_fm, ru=rtl_used):
                    p.wge(e, "va", va_fm)
                    p.wge(e, "ta", ru)
                    e.tensor_tensor(rtl[:, :], rf[:, pp * 128:(pp + 1) * 128],
                                    lamtile[pp][:, :], ALU.mult).then_inc(p.inc("va", 1), 1)

                va_rtl = p.c["va"]
                ybank = bank[2 + pp % 2]

                for hi in range(2):
                    h = 2 * pp + hi
                    abi = h % 2

                    @blk.tensor
                    def _(e, pp=pp, hi=hi, h=h, abi=abi, va_fm=va_fm):
                        p.wge(e, "va", va_fm)
                        p.wge(e, "va", amask_done.get(abi, 0))
                        e.matmul(bank[abi][:, 0:128], kf[hi * 64:(hi + 1) * 64, pp * 128:(pp + 1) * 128],
                                 rf[hi * 64:(hi + 1) * 64, pp * 128:(pp + 1) * 128],
                                 start=True, stop=True).then_inc(p.inc("ta", 1), 1)

                    ta_ar = p.c["ta"]

                    @blk.vector
                    def _(e, h=h, abi=abi, ta_ar=ta_ar, au=abuf_used.get(abi, 0)):
                        p.wge(e, "ta", ta_ar)
                        p.wge(e, "ta", au)
                        e.tensor_tensor(abuf[abi][:, :], bank[abi][:, 0:128], ptile[h][:, :],
                                        ALU.mult).then_inc(p.inc("va", 1), 1)

                    amask_done[abi] = p.c["va"]
                    va_am = p.c["va"]

                    @blk.vector
                    def _(e, h=h, hi=hi, pe=proj_evac, ku=khat_used.get(hi, 0)):
                        p.wge(e, "sa", pe)
                        p.wge(e, "ta", ku)
                        e.tensor_scalar(khat[:, hi * 64:(hi + 1) * 64], k_tok[:, h * 64:(h + 1) * 64],
                                        lamrev[:, h:h + 1], None, ALU.mult).then_inc(p.inc("va", 1), 1)

                    va_kh = p.c["va"]

                    @blk.tensor
                    def _(e, pp=pp, hi=hi, h=h, abi=abi, va_am=va_am, va_rtl=va_rtl, va_kh=va_kh,
                          yf=y_free.get(pp % 2, 0), tf=tm_state_prev):
                        p.wge(e, "va", va_am)
                        p.wge(e, "va", va_rtl)
                        p.wge(e, "va", va_kh)
                        p.wge(e, "sa", yf)
                        p.wge(e, "va", tf)
                        e.matmul(ybank[hi * 64:(hi + 1) * 64, 0:128],
                                 s_bf[hi * 64:(hi + 1) * 64, pp * 64:(pp + 1) * 64],
                                 rtl[hi * 64:(hi + 1) * 64, :], start=True, stop=False)
                        e.matmul(ybank[hi * 64:(hi + 1) * 64, 0:128],
                                 v_tok[:, h * 64:(h + 1) * 64], abuf[abi][:, :],
                                 start=False, stop=True).then_inc(p.inc("ta", 1), 1)
                        e.matmul(bank[4][hi * 64:(hi + 1) * 64, (pp % 2) * 64:(pp % 2) * 64 + 64],
                                 khat[:, hi * 64:(hi + 1) * 64], v_tok[:, h * 64:(h + 1) * 64],
                                 start=True, stop=True, skip_group_check=True).then_inc(p.inc("ta", 1), 1)

                    abuf_used[abi] = p.c["ta"]
                    rtl_used = p.c["ta"]
                    khat_used[hi] = p.c["ta"]
                    ta_tm = p.c["ta"]

                    @blk.vector
                    def _(e, pp=pp, hi=hi, h=h, ta_tm=ta_tm):
                        p.wge(e, "ta", ta_tm)
                        r0, r1 = hi * 64, (hi + 1) * 64
                        c0, c1 = pp * 64, (pp + 1) * 64
                        p.bar(e, "va", e.scalar_tensor_tensor(
                            s_f32[r0:r1, c0:c1], s_f32[r0:r1, c0:c1], lam128[r0:r1, h:h + 1],
                            bank[4][r0:r1, (pp % 2) * 64:(pp % 2) * 64 + 64], ALU.mult, ALU.add))
                        e.tensor_copy(s_bf[r0:r1, c0:c1], s_f32[r0:r1, c0:c1]).then_inc(p.inc("va", 1), 1)

                    tm_state_prev = p.c["va"]

                @blk.scalar
                def _(e, pp=pp, au=att_used):
                    p.wge(e, "ta")
                    p.wge(e, "ta", au)
                    e.activation(att[:, pp * 128:(pp + 1) * 128], ybank[:, 0:128],
                                 AF.Copy).then_inc(p.inc("sa", 1), 1)

                y_free[pp % 2] = p.c["sa"]

            sa_att = p.c["sa"]

            @blk.sync
            def _(e, g=g, wf=wo_free):
                if wf:
                    p.wge(e, "va", wf)
                e.dma_start(x0_tok[:, :], io["x_in"][g * 128:(g + 1) * 128, :]).then_inc(p.inc("xo", 16), 16)

            xo_now = p.c["xo"]

            for n in range(2):
                @blk.tensor
                def _(e, n=n, sa_att=sa_att, wf=wo_free, vnow=p.c["va"]):
                    p.wge(e, "sa", sa_att)
                    if n == 0:
                        p.wge(e, "va", wf)
                    else:
                        p.wge(e, "va", vnow)
                    inst = None
                    for k in range(KT):
                        inst = e.matmul(bank[5][:, :], att[:, k * 128:(k + 1) * 128],
                                        wo[k][:, n * 512:(n + 1) * 512],
                                        start=(k == 0), stop=(k == KT - 1))
                    inst.then_inc(p.inc("ta", 1), 1)

                ta_wo = p.c["ta"]

                @blk.vector
                def _(e, n=n, ta_wo=ta_wo, xo_now=xo_now):
                    p.wge(e, "ta", ta_wo)
                    p.wge(e, "xo", xo_now)
                    p.wge(e, "os")
                    e.tensor_tensor(xpre[:, n * 512:(n + 1) * 512], x0_tok[:, n * 512:(n + 1) * 512],
                                    bank[5][:, :], ALU.add).then_inc(p.inc("va", 1), 1)

            wo_free = p.c["va"]
            tok_used_ta = p.c["ta"]
            tok_used_va = p.c["va"]
            kf_used = p.c["ta"]
            att_used = p.c["ta"]
            if ti == NTILE - 1:
                strip_consumed[s_] = p.c["ta"]
            emit_ln_store(p, blk, lnb, xpre, io["x_out"], g * 128)

    @blk.sync
    def _(e):
        p.wge(e, "os")
    return



def emit_l1_ffn(p, io):
    nc = p.nc
    cs = io["consts"]
    G3 = 3072
    NB = S // 8
    h0tok = p.sb("l1f_h0tok", [2, D], F32)

    for sname in ["ld", "xa0", "xa1", "xo0", "xo1", "xc", "xm", "ta", "va", "sa", "ga", "so0", "so1", "os",
                  "gq0", "gq1", "gh", "gt", "gv", "gs"]:
        p.sem(sname)

    esA = contextlib.ExitStack()
    with esA:
        sbA = lambda n, shp, dt=F32: esA.enter_context(nc.sbuf_tensor(n, shp, dt))
        psA = lambda n, shp, dt=F32: esA.enter_context(nc.psum_tensor(n, shp, dt))
        blk = esA.enter_context(nc.Block())
        wih = [sbA(f"l1f_wih{k}", [128, G3], BF16) for k in range(KT)]
        ctxw = [sbA(f"l1f_ctxw{k}", [128, D], BF16) for k in range(KT)]
        STRIP = min(2048, S)
        NSTRIP = T // STRIP
        NTILE = STRIP // 128
        xf = [[sbA(f"l1f_xfA{bb}_{k}", [128, STRIP], BF16) for k in range(KT)] for bb in range(2)]
        xw_tok = [sbA(f"l1f_xwtok{i}", [128, G3], BF16) for i in range(2)]
        x0A = [sbA(f"l1f_x0A{i}", [128, D], BF16) for i in range(2)]
        bsel = sbA("l1f_bsel", [128, 4], BF16)
        hm = sbA("l1f_hm", [2, D], BF16)
        hmT = sbA("l1f_hmT", [128, KT * 2], BF16)
        ctxbrow = sbA("l1f_ctxbrow", [1, D], BF16)
        onesA = sbA("l1f_onesA", [1, 128], BF16)
        idnA = sbA("l1f_idnA", [128, 128], BF16)
        bank = [psA(f"l1f_bankA{i}", [128, 512], F32) for i in range(4)]
        pmean = psA("l1f_pmean", [2, D], F32)
        ptrA = psA("l1f_ptrA", [128, 2], BF16)

        @blk.gpsimd
        def _(e):
            e.dma_start(io["wihsc"][:, :], io["gru_w_ih"][:, :]).then_inc(p.inc("ga", 16), 16)
            e.dma_start(io["whhsc"][:, :], io["gru_w_hh"][:, :]).then_inc(p.inc("ga", 16), 16)
            e.dma_start(io["ctxsc"][:, :], io["ctx_w"][:, :]).then_inc(p.inc("ga", 16), 16)
            e.dma_start(ctxbrow[:1, :], io["ctx_b"][None, :]).then_inc(p.inc("ga", 16), 16)

        ga_all = p.c["ga"]

        @blk.sync
        def _(e):
            e.dma_start(onesA[:1, :], cs["ones1"][:, :]).then_inc(p.inc("ld", 16), 16)
            e.dma_start(idnA[:, :], cs["ident"][:, :]).then_inc(p.inc("ld", 16), 16)
            e.dma_start(bsel[:, :], cs["bsel"][:, :]).then_inc(p.inc("ld", 16), 16)

        @blk.scalar
        def _(e):
            p.wge(e, "ga", ga_all)
            for k in range(KT):
                e.dma_start(wih[k][:, :], io["wihsc"][:, k * 128:(k + 1) * 128],
                            transpose=True).then_inc(p.inc("ld", 16), 16)
                e.dma_start(ctxw[k][:, :], io["ctxsc"][:, k * 128:(k + 1) * 128],
                            transpose=True).then_inc(p.inc("ld", 16), 16)

        ld_all = p.c["ld"]

        strip_loaded = {}
        strip_consumed = {}
        xw_stored = {}
        xw_evac = {}
        x0_used = {}
        for g in range(T // 128):
            s_, ti = divmod(g, NTILE)
            b = g // (S // 128)
            par = g % 2
            if ti == 0:
                sb_ = s_ % 2
                @blk.scalar
                def _(e, s_=s_, sb_=sb_):
                    if s_ >= 2:
                        p.wge(e, "ta", strip_consumed[s_ - 2])
                    for k in range(KT):
                        e.dma_start(xf[sb_][k][:, :],
                                    io["x_in"][s_ * STRIP:(s_ + 1) * STRIP, k * 128:(k + 1) * 128],
                                    transpose=True).then_inc(p.inc(f"xa{s_ % 2}", 16), 16)
                strip_loaded[s_] = (f"xa{s_ % 2}", p.c[f"xa{s_ % 2}"])

            xs = xf[s_ % 2]
            tsl = slice(ti * 128, (ti + 1) * 128)

            @blk.sync
            def _(e, g=g, par=par):
                if g >= 2:
                    p.wge(e, "ta", x0_used.get(g - 2, 0))
                e.dma_start(x0A[par][:, :], io["x_in"][g * 128:(g + 1) * 128, :]).then_inc(p.inc(f"xo{par}", 16), 16)

            xo_now = (f"xo{par}", p.c[f"xo{par}"])

            for n6 in range(6):
                bi = n6 % 3
                @blk.tensor
                def _(e, g=g, s_=s_, tsl=tsl, n6=n6, bi=bi):
                    if g == 0 and n6 == 0:
                        p.wge(e, "ld", ld_all)
                    if n6 == 0:
                        p.wge(e, strip_loaded[s_][0], strip_loaded[s_][1])
                    p.wge(e, "sa", xw_evac.get(bi, 0))
                    inst = None
                    for k in range(KT):
                        inst = e.matmul(bank[bi][:, :], xs[k][:, tsl],
                                        wih[k][:, n6 * 512:(n6 + 1) * 512],
                                        start=(k == 0), stop=(k == KT - 1))
                    inst.then_inc(p.inc("ta", 1), 1)

                ta_xw = p.c["ta"]

                @blk.scalar
                def _(e, g=g, n6=n6, bi=bi, par=par, ta_xw=ta_xw):
                    p.wge(e, "ta", ta_xw)
                    if g >= 2 and n6 == 0:
                        p.wge(e, xw_stored[g - 2][0], xw_stored[g - 2][1])
                    e.activation(xw_tok[par][:, n6 * 512:(n6 + 1) * 512], bank[bi][:, :],
                                 AF.Copy).then_inc(p.inc("sa", 1), 1)

                xw_evac[bi] = p.c["sa"]

            sa_xwtok = p.c["sa"]

            @blk.tensor
            def _(e, g=g, b=b, par=par, xo_now=xo_now, first=(g == 0), last=(g == T // 128 - 1)):
                p.wge(e, xo_now[0], xo_now[1])
                inst = None
                for n in range(2):
                    inst = e.matmul(pmean[:, n * 512:(n + 1) * 512], bsel[:, 2 * b:2 * b + 2],
                                    x0A[par][:, n * 512:(n + 1) * 512],
                                    start=first, stop=last, skip_group_check=True)
                inst.then_inc(p.inc("ta", 1), 1)

            x0_used[g] = p.c["ta"]

            @blk.sync
            def _(e, g=g, b=b, par=par, sa_xwtok=sa_xwtok):
                p.wge(e, "sa", sa_xwtok)
                s0 = (g % (S // 128)) * 128
                e.dma_start(io["xwsc"].rearrange("(s bb) c -> s bb c", bb=BPC)[s0:s0 + 128, b, :],
                            xw_tok[par][:, :]).then_inc(p.inc(f"so{par}", 16), 16)

            xw_stored[g] = (f"so{par}", p.c[f"so{par}"])
            if ti == NTILE - 1:
                strip_consumed[s_] = p.c["ta"]

        @blk.vector
        def _(e):
            p.wge(e, "ta")
            e.tensor_scalar(hm[:, :], pmean[:, :], 1.0 / S, None, ALU.mult).then_inc(p.inc("va", 1), 1)

        for k in range(KT):
            @blk.tensor
            def _(e, k=k):
                p.wge(e, "va")
                e.transpose(ptrA[:, :], hm[:, k * 128:(k + 1) * 128], idnA[0:2, 0:2]).then_inc(p.inc("ta", 1), 1)

            @blk.vector
            def _(e, k=k):
                p.wge(e, "ta")
                e.tensor_copy(hmT[:, k * 2:(k + 1) * 2], ptrA[:, :]).then_inc(p.inc("va", 1), 1)

        for n in range(2):
            @blk.tensor
            def _(e, n=n):
                p.wge(e, "va")
                for k in range(KT):
                    e.matmul(bank[3][0:2, :], hmT[:, k * 2:(k + 1) * 2],
                             ctxw[k][:, n * 512:(n + 1) * 512],
                             start=(k == 0), stop=False, skip_group_check=True)
                e.matmul(bank[3][0:2, :], onesA[:1, 0:2],
                         ctxbrow[:1, n * 512:(n + 1) * 512],
                         start=False, stop=True, skip_group_check=True).then_inc(p.inc("ta", 1), 1)

            @blk.vector
            def _(e, n=n):
                p.wge(e, "ta")
                e.tensor_copy(h0tok[:, n * 512:(n + 1) * 512], bank[3][0:2, :]).then_inc(p.inc("va", 1), 1)

    emit_l1_ffn_scan(p, io, h0tok)
    emit_l1_ffn_tail(p, io)
    return


def emit_l1_ffn_scan(p, io, h0tok):
    nc = p.nc
    cs = io["consts"]
    G3 = 3072
    NB = S // 8
    es = contextlib.ExitStack()
    with es:
        sb = lambda n, shp, dt=F32: es.enter_context(nc.sbuf_tensor(n, shp, dt))
        ps = lambda n, shp, dt=F32: es.enter_context(nc.psum_tensor(n, shp, dt))
        blk = es.enter_context(nc.Block())
        whh = [sb(f"l1f_whh{k}", [128, G3], BF16) for k in range(KT)]
        xwt = sb("l1f_xwt", [2, 16 * G3], BF16)
        hr = sb("l1f_hr", [2, 8 * D], BF16)
        hT = sb("l1f_hT", [128, 16], BF16)
        htok = sb("l1f_htok", [2, 2 * D], F32)
        rz = sb("l1f_rz", [2, 2048], F32)
        t3b = sb("l1f_t3", [2, D], F32)
        t4b = sb("l1f_t4", [2, D], F32)
        nnb = sb("l1f_nn", [2, D], F32)
        ddb = sb("l1f_dd", [2, D], F32)
        mmb = sb("l1f_mm", [2, D], F32)
        idnB = sb("l1f_idnB", [128, 128], BF16)
        h0bf = sb("l1f_h0bf", [2, D], BF16)
        phw = [ps(f"l1f_phw{i}", [2, 512], F32) for i in range(6)]
        ptrBig = ps("l1f_ptrBig", [128, 16], BF16)

        GVB, GTB, GSB, GHB = 512, 256, 256, 0
        NIT2 = NB // 2

        @blk.sync
        def _(e):
            e.dma_start(idnB[:, :], cs["ident"][:, :]).then_inc(p.inc("ld", 16), 16)

        @blk.scalar
        def _(e):
            p.wge(e, "ga")
            for k in range(KT):
                e.dma_start(whh[k][:, :], io["whhsc"][:, k * 128:(k + 1) * 128],
                            transpose=True).then_inc(p.inc("ld", 16), 16)

        ld_all = p.c["ld"]

        @blk.vector
        def _(e):
            p.wge(e, "va")
            p.bar(e, "gv", e.tensor_copy(htok[:, D:2 * D], h0tok[:, :]))
            e.tensor_copy(h0bf[:, :], h0tok[:, :]).then_inc(p.inc("gv", 1), 1)

        @blk.tensor
        def _(e):
            p.wge(e, "ld", ld_all)
            p.wge(e, "gv")
            for k in range(KT):
                e.transpose(ptrBig[:, 2 * k:2 * k + 2], h0bf[:, k * 128:(k + 1) * 128],
                            idnB[0:2, 0:2]).then_inc(p.inc("gt", 1), 1)

        @blk.vector
        def _(e):
            p.wge(e, "gt")
            e.tensor_copy(hT[:, :], ptrBig[:, :]).then_inc(p.inc("gv", 1), 1)

        @blk.vector
        def _(e):
            p.wge(e, "gv")
            e.sem_inc(p.sems["gv"], GVB - p.c["gv"])
        p.c["gv"] = GVB

        @blk.scalar
        def _(e):
            p.wge(e, "gv", GVB)
            p.wge(e, "gs")
            e.sem_inc(p.sems["gs"], GSB - p.c["gs"])
        p.c["gs"] = GSB

        @blk.tensor
        def _(e):
            p.wge(e, "gt")
            e.sem_inc(p.sems["gt"], GTB - p.c["gt"])
        p.c["gt"] = GTB



        xwsc_h = io["xwsc_h"]
        hsc_h = io["hsc_h"]

        @blk.gpsimd
        def _(e):
            p.wge(e, "so0")
            p.wge(e, "so1")
            r1 = e.alloc_register("xwoff")
            r2 = e.alloc_register("hoff")
            for kb in range(2):
                e.dma_start(bass.AP(xwt, kb * 8 * G3, [[16 * G3, 2], [G3, 8], [1, G3]]),
                            bass.AP(xwsc_h, kb * 8 * 2 * G3, [[G3, 2], [2 * G3, 8], [1, G3]])).then_inc(p.inc(f"gq{kb}", 16), 16)
            with e.Fori(1, NIT2) as m:
                for sub in range(2):
                    e.wait_ge(p.sems["gs"], (m * 96 + sub * 48 - 48) + GSB)
                    e.reg_mul(r2, m, 2 * 8 * 2 * D)
                    e.reg_add(r2, r2, (sub - 2) * 8 * 2 * D)
                    e.dma_start(bass.AP(hsc_h, r2, [[D, 2], [2 * D, 8], [1, D]]),
                                bass.AP(hr, 0, [[8 * D, 2], [D, 8], [1, D]])).then_inc(p.inc("gh", 16), 16)
                    e.wait_ge(p.sems["gv"], (m * 112 + sub * 56 - 112) + 56 + GVB)
                    e.reg_mul(r1, m, 2 * 8 * 2 * G3)
                    e.reg_add(r1, r1, sub * 8 * 2 * G3)
                    e.dma_start(bass.AP(xwt, sub * 8 * G3, [[16 * G3, 2], [G3, 8], [1, G3]]),
                                bass.AP(xwsc_h, r1, [[G3, 2], [2 * G3, 8], [1, G3]])).then_inc(p.inc(f"gq{sub}", 16), 16)
            for kk in [NB - 2, NB - 1]:
                e.wait_ge(p.sems["gs"], kk * 48 + 48 + GSB)
                e.dma_start(bass.AP(hsc_h, kk * 8 * 2 * D, [[D, 2], [2 * D, 8], [1, D]]),
                            bass.AP(hr, 0, [[8 * D, 2], [D, 8], [1, D]])).then_inc(p.inc("gh", 16), 16)
            e.free_register(r1)
            e.free_register(r2)
            e.wait_ge(p.sems["gh"], NB * 16)

        p.c["gq0"] = 16 * (NB // 2)
        p.c["gq1"] = 16 * (NB // 2)
        p.c["gh"] = 16 * NB


        @blk.tensor
        def _(e):
            rw = e.alloc_register("t_rw")
            def wge_rt(sem, coef, const, it2):
                e.reg_mul(rw, it2, coef)
                e.reg_add(rw, rw, const)
                e.wait_ge(sem, rw)
            with e.Fori(0, NIT2) as it2:
                for jj in range(16):
                    kbslot = jj // 8
                    wge_rt(p.sems["gv"], 112, 7 * jj + GVB, it2)
                    wge_rt(p.sems[f"gq{jj // 8}"], 16, 16, it2)
                    wge_rt(p.sems["gs"], 96, 6 * jj + GSB, it2)
                    for n6 in range(6):
                        inst = None
                        for k in range(KT):
                            inst = e.matmul(phw[n6][:, :], hT[:, 2 * k:2 * k + 2],
                                            whh[k][:, n6 * 512:(n6 + 1) * 512],
                                            start=(k == 0), stop=(k == KT - 1 and n6 >= 4))
                        if n6 < 4:
                            inst = e.matmul(phw[n6][:, :], idnB[0:2, 0:2],
                                            bass.AP(xwt, kbslot * 8 * G3 + (jj % 8) * G3 + n6 * 512,
                                                    [[16 * G3, 2], [1, 512]]),
                                            start=False, stop=True)
                        if n6 == 3:
                            inst.then_inc(p.sems["gt"], 1)
                    inst.then_inc(p.sems["gt"], 1)
                    wge_rt(p.sems["gs"], 96, 6 * jj + 6 + GSB, it2)
                    for k in range(KT):
                        e.transpose(ptrBig[:, 2 * k:2 * k + 2],
                                    bass.AP(hr, (jj % 8) * D + k * 128, [[8 * D, 2], [1, 128]]),
                                    idnB[0:2, 0:2]).then_inc(p.sems["gt"], 1)
            e.free_register(rw)

        p.c["gt"] = GTB + 10 * S

        @blk.scalar
        def _(e):
            rw = e.alloc_register("s_rw")
            def wge_rt(sem, coef, const, it2):
                e.reg_mul(rw, it2, coef)
                e.reg_add(rw, rw, const)
                e.wait_ge(sem, rw)
            with e.Fori(0, NIT2) as it2:
                for jj in range(16):
                    wge_rt(p.sems["gt"], 160, 10 * jj + 1 + GTB, it2)
                    for c in range(4):
                        e.activation(rz[:, c * 512:(c + 1) * 512], phw[c][:, :],
                                     AF.Sigmoid).then_inc(p.sems["gs"], 1)
                    wge_rt(p.sems["gv"], 112, 7 * jj + 3 + GVB, it2)
                    e.activation(nnb[:, :], t4b[:, :], AF.Tanh).then_inc(p.sems["gs"], 1)
                    wge_rt(p.sems["gv"], 112, 7 * jj + 6 + GVB, it2)
                    wge_rt(p.sems["gh"], 32, 16 * (jj // 8), it2)
                    e.activation(bass.AP(hr, (jj % 8) * D, [[8 * D, 2], [1, D]]),
                                 htok[:, (jj % 2) * D:(jj % 2) * D + D],
                                 AF.Copy).then_inc(p.sems["gs"], 1)
            e.free_register(rw)

        p.c["gs"] = GSB + 6 * S

        @blk.vector
        def _(e):
            rw = e.alloc_register("v_rw")
            def wge_rt(sem, coef, const, it2):
                e.reg_mul(rw, it2, coef)
                e.reg_add(rw, rw, const)
                e.wait_ge(sem, rw)
            with e.Fori(0, NIT2) as it2:
                for jj in range(16):
                    kbslot = jj // 8
                    cur = (jj % 2) * D
                    prv = ((jj + 1) % 2) * D
                    wge_rt(p.sems["gs"], 96, 6 * jj + 2 + GSB, it2)
                    wge_rt(p.sems["gt"], 160, 10 * jj + 2 + GTB, it2)
                    e.tensor_tensor(t3b[:, 0:512], rz[:, 0:512], phw[4][:, :], ALU.mult).then_inc(p.sems["gv"], 1)
                    e.tensor_tensor(t3b[:, 512:1024], rz[:, 512:1024], phw[5][:, :], ALU.mult).then_inc(p.sems["gv"], 1)
                    wge_rt(p.sems["gv"], 112, 7 * jj + 2 + GVB, it2)
                    e.tensor_tensor(t4b[:, 0:1024],
                                    bass.AP(xwt, kbslot * 8 * G3 + (jj % 8) * G3 + 2048, [[16 * G3, 2], [1, 1024]]),
                                    t3b[:, 0:1024], ALU.add).then_inc(p.sems["gv"], 1)
                    wge_rt(p.sems["gs"], 96, 6 * jj + 5 + GSB, it2)
                    e.tensor_tensor(ddb[:, :], htok[:, prv:prv + D], nnb[:, :], ALU.subtract).then_inc(p.sems["gv"], 1)
                    wge_rt(p.sems["gv"], 112, 7 * jj + 4 + GVB, it2)
                    e.tensor_tensor(mmb[:, :], ddb[:, :], rz[:, 1024:2048], ALU.mult).then_inc(p.sems["gv"], 1)
                    wge_rt(p.sems["gv"], 112, 7 * jj + 5 + GVB, it2)
                    e.tensor_tensor(htok[:, cur:cur + D], mmb[:, :], nnb[:, :], ALU.add).then_inc(p.sems["gv"], 1)
                    wge_rt(p.sems["gt"], 160, 10 * jj + 10 + GTB, it2)
                    e.tensor_copy(hT[:, :], ptrBig[:, :]).then_inc(p.sems["gv"], 1)
            e.free_register(rw)

        p.c["gv"] = GVB + 7 * S
    return

def emit_l1_ffn_tail(p, io):
    nc = p.nc
    cs = io["consts"]
    E, FF = 8, 2048
    NTPC = T // 128

    esC = contextlib.ExitStack()
    with esC:
        sb = lambda n, shp, dt=F32: esC.enter_context(nc.sbuf_tensor(n, shp, dt))
        ps = lambda n, shp, dt=F32: esC.enter_context(nc.psum_tensor(n, shp, dt))
        blk = esC.enter_context(nc.Block())
        exw = [sb(f"l1f_exw{k}", [128, D], BF16) for k in range(KT)]
        STRIP = min(2048, S)
        NSTRIP = T // STRIP
        NTILE = STRIP // 128
        xf = [[sb(f"l1f_xfC{bb}_{k}", [128, STRIP], BF16) for k in range(KT)] for bb in range(2)]
        ex_tok = sb("l1f_extok", [128, D], BF16)
        rt_tok = [sb(f"l1f_rttok{i}", [128, D], BF16) for i in range(2)]
        junk = sb("l1f_junk", [128, 128], F32)
        dot = sb("l1f_dot", [128, 8], F32)
        ssb = sb("l1f_ss", [128, 8], F32)
        seb = sb("l1f_se", [128, 8], F32)
        gn = sb("l1f_gn", [128, 64], F32)
        inv = sb("l1f_inv", [128, 8], F32)
        nr = sb("l1f_nr", [128, 8], F32)
        nrn = sb("l1f_nrn", [128, 8], F32)
        n2 = sb("l1f_n2", [128, 8], F32)
        q8 = sb("l1f_q8", [128, 8], F32)
        pentile = sb("l1f_pentile", [128, 1], F32)
        penacc = sb("l1f_penacc", [128, 1], F32)
        dist = sb("l1f_dist", [128, 8], F32)
        dldp = [sb(f"l1f_dld{i}", [128, 8], F32) for i in range(2)]
        tmp8 = sb("l1f_tmp8", [128, 8], F32)
        tmp1 = sb("l1f_tmp1c", [128, 1], F32)
        onesCf = sb("l1f_onesCf", [1, 128], F32)
        onescol = sb("l1f_onescolC", [128, 1], F32)
        pone = ps("l1f_pone", [128, 512], F32)
        pex = [ps(f"l1f_pex{i}", [128, 512], F32) for i in range(2)]

        @blk.gpsimd
        def _(e):
            e.dma_start(io["exprsc"][:, :], io["expr_w"][:, :]).then_inc(p.inc("ga", 16), 16)

        ga_now = p.c["ga"]

        @blk.sync
        def _(e):
            e.dma_start(onesCf[:1, :], cs["ones1f"][:, :]).then_inc(p.inc("ld", 16), 16)
            e.dma_start(onescol[:, :], cs["onescolf"][:, :]).then_inc(p.inc("ld", 16), 16)

        @blk.scalar
        def _(e):
            p.wge(e, "ga", ga_now)
            for k in range(KT):
                e.dma_start(exw[k][:, :], io["exprsc"][:, k * 128:(k + 1) * 128],
                            transpose=True).then_inc(p.inc("ld", 16), 16)

        ld_all = p.c["ld"]

        @blk.vector
        def _(e):
            e.memset(penacc[:, :], 0.0).then_inc(p.inc("va", 1), 1)

        va_pen = p.c["va"]

        strip_loaded = {}
        strip_consumed = {}
        rt_used = {}
        ex_evac = {}
        extok_used = 0
        dist_stored = 0
        for g in range(NTPC):
            s_, ti = divmod(g, NTILE)
            par = g % 2
            if ti == 0:
                sb_ = s_ % 2
                @blk.scalar
                def _(e, s_=s_, sb_=sb_):
                    if s_ >= 2:
                        p.wge(e, "ta", strip_consumed[s_ - 2])
                    for k in range(KT):
                        e.dma_start(xf[sb_][k][:, :],
                                    io["x_in"][s_ * STRIP:(s_ + 1) * STRIP, k * 128:(k + 1) * 128],
                                    transpose=True).then_inc(p.inc(f"xa{s_ % 2}", 16), 16)
                strip_loaded[s_] = (f"xa{s_ % 2}", p.c[f"xa{s_ % 2}"])

            xs = xf[s_ % 2]
            tsl = slice(ti * 128, (ti + 1) * 128)
            b = g // (S // 128)
            s0 = (g % (S // 128)) * 128

            @blk.sync
            def _(e, g=g, par=par, b=b, s0=s0):
                if g >= 2:
                    p.wge(e, "va", rt_used.get(g - 2, 0))
                e.dma_start(rt_tok[par][:, :],
                            io["hsc"].rearrange("(s bb) c -> s bb c", bb=BPC)[s0:s0 + 128, b, :]).then_inc(p.inc(f"xo{par}", 16), 16)

            xo_now = (f"xo{par}", p.c[f"xo{par}"])

            for n in range(2):
                @blk.tensor
                def _(e, g=g, s_=s_, tsl=tsl, n=n):
                    if g == 0 and n == 0:
                        p.wge(e, "ld", ld_all)
                        p.wge(e, "va", va_pen)
                    if n == 0:
                        p.wge(e, strip_loaded[s_][0], strip_loaded[s_][1])
                    p.wge(e, "sa", ex_evac.get(n, 0))
                    inst = None
                    for k in range(KT):
                        inst = e.matmul(pex[n][:, :], xs[k][:, tsl], exw[k][:, n * 512:(n + 1) * 512],
                                        start=(k == 0), stop=(k == KT - 1))
                    inst.then_inc(p.inc("ta", 1), 1)

                ta_ex = p.c["ta"]

                @blk.scalar
                def _(e, n=n, ta_ex=ta_ex):
                    p.wge(e, "ta", ta_ex)
                    p.wge(e, "va", extok_used)
                    e.activation(ex_tok[:, n * 512:(n + 1) * 512], pex[n][:, :],
                                 AF.Copy).then_inc(p.inc("sa", 1), 1)

                ex_evac[n] = p.c["sa"]

            sa_ex = p.c["sa"]

            @blk.vector
            def _(e, g=g, par=par, sa_ex=sa_ex, xo_now=xo_now):
                p.wge(e, "sa", sa_ex)
                p.wge(e, xo_now[0], xo_now[1])
                for ee in range(E):
                    sl = slice(ee * 128, (ee + 1) * 128)
                    p.bar(e, "va", e.tensor_tensor(junk[:, :], ex_tok[:, sl], rt_tok[par][:, sl], ALU.mult))
                    p.bar(e, "va", e.reduce_sum(dot[:, ee:ee + 1], junk[:, :], axis=mybir.AxisListType.X))
                    p.bar(e, "va", e.tensor_tensor(junk[:, :], rt_tok[par][:, sl], rt_tok[par][:, sl], ALU.mult))
                    p.bar(e, "va", e.reduce_sum(ssb[:, ee:ee + 1], junk[:, :], axis=mybir.AxisListType.X))
                    p.bar(e, "va", e.tensor_tensor(junk[:, :], ex_tok[:, sl], ex_tok[:, sl], ALU.mult))
                    p.bar(e, "va", e.reduce_sum(seb[:, ee:ee + 1], junk[:, :], axis=mybir.AxisListType.X))
                p.bar(e, "va", e.memset(gn[:, :], 0.0))
                for ee in range(E):
                    for ff in range(ee + 1, E):
                        p.bar(e, "va", e.tensor_tensor(junk[:, :], rt_tok[par][:, ee * 128:(ee + 1) * 128],
                                                       rt_tok[par][:, ff * 128:(ff + 1) * 128], ALU.mult))
                        p.bar(e, "va", e.reduce_sum(gn[:, ee * 8 + ff:ee * 8 + ff + 1], junk[:, :],
                                                    axis=mybir.AxisListType.X))

            extok_used = p.c["va"]
            rt_used[g] = p.c["va"]
            va_ttr = p.c["va"]

            @blk.scalar
            def _(e, va_ttr=va_ttr):
                p.wge(e, "va", va_ttr)
                e.activation(nr[:, :], ssb[:, :], AF.Sqrt).then_inc(p.inc("sa", 1), 1)
                e.activation(seb[:, :], seb[:, :], AF.Sqrt).then_inc(p.inc("sa", 1), 1)

            @blk.vector
            def _(e, ds=dist_stored):
                p.wge(e, "sa")
                p.wge(e, "so0", ds)
                p.bar(e, "va", e.tensor_scalar(inv[:, :], nr[:, :], 1e-12, None, ALU.max))
                p.bar(e, "va", e.reciprocal(inv[:, :], inv[:, :]))
                for ee in range(E):
                    for ff in range(ee + 1, E):
                        c0 = ee * 8 + ff
                        p.bar(e, "va", e.tensor_scalar(gn[:, c0:c0 + 1], gn[:, c0:c0 + 1],
                                                       inv[:, ee:ee + 1], inv[:, ff:ff + 1],
                                                       ALU.mult, ALU.mult))
                        e.tensor_copy(gn[:, ff * 8 + ee:ff * 8 + ee + 1], gn[:, c0:c0 + 1]).then_inc(p.inc("va", 1), 1)
                p.wge(e, "va")
                p.bar(e, "va", e.tensor_tensor(gn[:, :], gn[:, :], gn[:, :], ALU.mult))
                for ee in range(E):
                    p.bar(e, "va", e.reduce_sum(n2[:, ee:ee + 1], gn[:, ee * 8:(ee + 1) * 8],
                                                axis=mybir.AxisListType.X))
                p.bar(e, "va", e.tensor_scalar(q8[:, :], n2[:, :], 1e-24, None, ALU.max))
                p.bar(e, "va", e.reciprocal(q8[:, :], q8[:, :]))
                p.bar(e, "va", e.tensor_tensor(q8[:, :], q8[:, :], n2[:, :], ALU.mult))
                p.bar(e, "va", e.reduce_sum(pentile[:, :], q8[:, :], axis=mybir.AxisListType.X))
                p.bar(e, "va", e.tensor_tensor(penacc[:, :], penacc[:, :], pentile[:, :], ALU.add))
                p.bar(e, "va", e.tensor_tensor(dot[:, :], dot[:, :], inv[:, :], ALU.mult))
                p.bar(e, "va", e.tensor_tensor(nrn[:, :], nr[:, :], inv[:, :], ALU.mult))
                p.bar(e, "va", e.tensor_scalar(tmp8[:, :], seb[:, :], 1e-8, None, ALU.max))
                p.bar(e, "va", e.tensor_scalar(nrn[:, :], nrn[:, :], 1e-8, None, ALU.max))
                p.bar(e, "va", e.tensor_tensor(tmp8[:, :], tmp8[:, :], nrn[:, :], ALU.mult))
                p.bar(e, "va", e.reciprocal(tmp8[:, :], tmp8[:, :]))
                p.bar(e, "va", e.tensor_tensor(dot[:, :], dot[:, :], tmp8[:, :], ALU.mult))
                e.tensor_scalar(dist[:, :], dot[:, :], -1.0, 1.0, ALU.mult, ALU.add).then_inc(p.inc("va", 1), 1)

            va_dist = p.c["va"]

            @blk.sync
            def _(e, g=g, va_dist=va_dist):
                p.wge(e, "va", va_dist)
                e.dma_start(io["distsc"][g * 128:(g + 1) * 128, :], dist[:, :]).then_inc(p.inc("so0", 16), 16)

            dist_stored = p.c["so0"]
            if ti == NTILE - 1:
                strip_consumed[s_] = p.c["ta"]

        @blk.tensor
        def _(e):
            p.wge(e, "va")
            e.matmul(pone[0:1, 0:1], penacc[:, :], onescol[:, :], start=True, stop=True,
                     skip_group_check=True).then_inc(p.inc("ta", 1), 1)

        @blk.vector
        def _(e):
            p.wge(e, "ta")
            e.tensor_copy(tmp1[0:1, 0:1], pone[0:1, 0:1]).then_inc(p.inc("va", 1), 1)

        @blk.sync
        def _(e):
            p.wge(e, "va")
            e.dma_start(io["penpart"][:, :], tmp1[0:1, 0:1]).then_inc(p.inc("so1", 16), 16)
            p.wge(e, "so0")
            p.wge(e, "so1")
    return


def emit_l1_ffn_tail2(p, io):
    nc = p.nc
    cs = io["consts"]
    E = 8
    NTPC = T // 128
    esC = contextlib.ExitStack()
    with esC:
        sb = lambda n, shp, dt=F32: esC.enter_context(nc.sbuf_tensor(n, shp, dt))
        ps = lambda n, shp, dt=F32: esC.enter_context(nc.psum_tensor(n, shp, dt))
        blk = esC.enter_context(nc.Block())
        dist = sb("l1t_dist", [128, 8], F32)
        dldp = [sb(f"l1t_dld{i}", [128, 8], F32) for i in range(2)]
        tmp8 = sb("l1t_tmp8", [128, 8], F32)
        q8 = sb("l1t_q8", [128, 8], F32)
        nrn = sb("l1t_nrn", [128, 8], F32)
        n2 = sb("l1t_n2", [128, 8], F32)
        tmp1 = sb("l1t_tmp1c", [128, 1], F32)
        pentile = sb("l1t_pentile", [128, 1], F32)
        pens = sb("l1t_pens", [1, 8], F32)
        onesCf = sb("l1t_onesCf", [1, 128], F32)
        pone = ps("l1t_pone", [128, 512], F32)

        for sname in ["ld", "xo0", "xo1", "xc", "xm", "so1", "ta", "va", "sa", "ga",
                      "xa0", "xa1", "os"]:
            p.sem(sname)

        @blk.sync
        def _(e):
            e.dma_start(onesCf[:1, :], cs["ones1f"][:, :]).then_inc(p.inc("ld", 16), 16)
            if "pens" in io:
                e.dma_start(pens[:1, :], io["pens"][:, :]).then_inc(p.inc("ld", 16), 16)
            else:
                e.dma_start(pens[:1, 0:1], io["penpart"][:, :]).then_inc(p.inc("ld", 16), 16)

        @blk.vector
        def _(e):
            p.wge(e, "ld")
            if "pens" in io:
                p.bar(e, "va", e.reduce_sum(tmp1[0:1, 0:1], pens[0:1, :], axis=mybir.AxisListType.X))
                e.tensor_scalar(tmp1[0:1, 0:1], tmp1[0:1, 0:1], 1.0 / (NCORES * T), 1.0,
                                ALU.mult, ALU.add).then_inc(p.inc("va", 1), 1)
            else:
                e.tensor_scalar(tmp1[0:1, 0:1], pens[0:1, 0:1], 1.0 / T, 1.0,
                                ALU.mult, ALU.add).then_inc(p.inc("va", 1), 1)

        @blk.tensor
        def _(e):
            p.wge(e, "va")
            e.matmul(pone[:, 0:1], onesCf[:1, :], tmp1[0:1, 0:1], start=True, stop=True,
                     skip_group_check=True).then_inc(p.inc("ta", 1), 1)

        @blk.vector
        def _(e):
            p.wge(e, "ta")
            e.tensor_copy(pentile[:, :], pone[:, 0:1]).then_inc(p.inc("va", 1), 1)

        va_pen = p.c["va"]
        rt_used = {}

        for g in range(NTPC):
            par = g % 2

            @blk.sync
            def _(e, g=g, par=par):
                if g >= 2:
                    p.wge(e, "va", rt_used.get(("c2", g - 2), 0))
                e.dma_start(dldp[par][:, :], io["distsc"][g * 128:(g + 1) * 128, :]).then_inc(p.inc(f"xo{par}", 16), 16)

            xo_now = (f"xo{par}", p.c[f"xo{par}"])

            @blk.vector
            def _(e, g=g, par=par, xo_now=xo_now, va_pen=va_pen):
                p.wge(e, xo_now[0], xo_now[1])
                p.wge(e, "va", va_pen)
                p.wge(e, "so1")
                p.bar(e, "va", e.tensor_scalar(dist[:, :], dldp[par][:, :], pentile[:, :], None, ALU.mult))
                p.bar(e, "va", e.reduce_max(tmp1[:, :], dist[:, :], axis=mybir.AxisListType.X))
                p.bar(e, "va", e.tensor_scalar(tmp8[:, :], dist[:, :], tmp1[:, :], None, ALU.is_ge))
                p.bar(e, "va", e.scalar_tensor_tensor(q8[:, :], tmp8[:, :], -1e30, dist[:, :],
                                                      ALU.mult, ALU.add))
                p.bar(e, "va", e.reduce_max(n2[:, 0:1], q8[:, :], axis=mybir.AxisListType.X))
                p.bar(e, "va", e.tensor_scalar(nrn[:, :], q8[:, :], n2[:, 0:1], None, ALU.is_ge))
                e.tensor_tensor(n2[:, 1:2], n2[:, 0:1], tmp1[:, :], ALU.subtract).then_inc(p.inc("va", 1), 1)

            va_t2 = p.c["va"]

            @blk.scalar
            def _(e, va_t2=va_t2):
                p.wge(e, "va", va_t2)
                e.activation(n2[:, 2:3], n2[:, 1:2], AF.Exp).then_inc(p.inc("sa", 1), 1)

            sa_e2 = p.c["sa"]

            @blk.vector
            def _(e, g=g, sa_e2=sa_e2):
                p.wge(e, "sa", sa_e2)
                p.bar(e, "va", e.tensor_scalar(n2[:, 3:4], n2[:, 2:3], 1.0, None, ALU.add))
                p.bar(e, "va", e.reciprocal(n2[:, 3:4], n2[:, 3:4]))
                p.bar(e, "va", e.tensor_tensor(n2[:, 4:5], n2[:, 2:3], n2[:, 3:4], ALU.mult))
                p.bar(e, "va", e.tensor_scalar(tmp8[:, :], tmp8[:, :], n2[:, 3:4], None, ALU.mult))
                p.bar(e, "va", e.tensor_scalar(nrn[:, :], nrn[:, :], n2[:, 4:5], None, ALU.mult))
                e.tensor_tensor(dist[:, :], tmp8[:, :], nrn[:, :], ALU.add).then_inc(p.inc("va", 1), 1)

            va_cw = p.c["va"]
            rt_used[("c2", g)] = p.c["va"]

            @blk.sync
            def _(e, g=g, va_cw=va_cw):
                p.wge(e, "va", va_cw)
                e.dma_start(io["cwsc"][g * 128:(g + 1) * 128, :], dist[:, :]).then_inc(p.inc("so1", 16), 16)

        @blk.sync
        def _(e):
            p.wge(e, "so1")

    emit_l1_ffn_moe(p, io)
    return


def emit_l1_ffn_moe(p, io):
    nc = p.nc
    cs = io["consts"]
    E, FF = 8, 2048
    CH = min(512, T // 4)
    NTT = CH // 128
    QT = (T // 4) // 128
    es = contextlib.ExitStack()
    with es:
        sb = lambda n, shp, dt=F32: es.enter_context(nc.sbuf_tensor(n, shp, dt))
        ps = lambda n, shp, dt=F32: es.enter_context(nc.psum_tensor(n, shp, dt))
        blk = es.enter_context(nc.Block())
        acc = sb("l1f_acc", [128, 16 * D], BF16)
        gw_sb = [sb(f"l1f_gw_{k}", [128, FF], BF16) for k in range(KT)]
        uw_sb = [sb(f"l1f_uw_{k}", [128, FF], BF16) for k in range(KT)]
        dw_sb = [sb(f"l1f_dw_{k}", [128, D], BF16) for k in range(16)]
        xfm = [[sb(f"l1f_xfm{bb}_{k}", [128, CH], BF16) for k in range(KT)] for bb in range(2)]
        zt = sb("l1f_zt", [128, 16 * CH], BF16)
        st_ = sb("l1f_st", [128, CH], F32)
        cwt = sb("l1f_cwt", [128, NTT * 8], F32)
        x0m = sb("l1f_x0m", [128, D], BF16)
        xpre = sb("l1f_xpreM", [128, D], F32)
        lnb = (sb("l1f_statM", [128, 12], F32), sb("l1f_mvM", [128, 2], F32),
               sb("l1f_mnegM", [128, 1], F32), sb("l1f_rstdM", [128, 1], F32),
               sb("l1f_xoutM", [128, D], F32), sb("l1f_epsM", [128, 1], F32))

        gp = [ps(f"l1f_gp{i}", [128, CH], F32) for i in range(2)]
        upp = [ps(f"l1f_up{i}", [128, CH], F32) for i in range(2)]
        dn = [ps(f"l1f_dn{i}", [128, 512], F32) for i in range(2)]

        @blk.vector
        def _(e):
            e.memset(lnb[5][:, :], LN_EPS).then_inc(p.inc("va", 1), 1)

        @blk.gpsimd
        def _(e):
            for ei in range(E):
                e.dma_start(io["gwsc"][ei * FF:(ei + 1) * FF, :], io["gate_w"][ei, :, :]).then_inc(p.inc("ga", 16), 16)
                e.dma_start(io["uwsc"][ei * FF:(ei + 1) * FF, :], io["up_w"][ei, :, :]).then_inc(p.inc("ga", 16), 16)
                e.dma_start(io["dwsc"][ei * D:(ei + 1) * D, :], io["down_w"][ei, :, :]).then_inc(p.inc("ga", 16), 16)
            e.dma_start(io["gwsc"][E * FF:(E + 1) * FF, :], io["shared_gate_w"][:, :]).then_inc(p.inc("ga", 16), 16)
            e.dma_start(io["uwsc"][E * FF:(E + 1) * FF, :], io["shared_up_w"][:, :]).then_inc(p.inc("ga", 16), 16)
            e.dma_start(io["dwsc"][E * D:(E + 1) * D, :], io["shared_down_w"][:, :]).then_inc(p.inc("ga", 16), 16)

        ga_all = p.c["ga"]

        load_done = {}
        wbuf_free = {}

        def emit_wload(idx):
            ei = idx % (E + 1)
            @blk.scalar
            def _(e, ei=ei, idx=idx):
                p.wge(e, "ga", ga_all)
                p.wge(e, "ta", wbuf_free.get(idx - 1, 0))
                for k in range(KT):
                    e.dma_start(gw_sb[k][:, :], io["gwsc"][ei * FF:(ei + 1) * FF, k * 128:(k + 1) * 128],
                                transpose=True).then_inc(p.inc("ld", 16), 16)
                    e.dma_start(uw_sb[k][:, :], io["uwsc"][ei * FF:(ei + 1) * FF, k * 128:(k + 1) * 128],
                                transpose=True).then_inc(p.inc("ld", 16), 16)
                for k in range(16):
                    e.dma_start(dw_sb[k][:, :],
                                io["dwsc"][ei * D:(ei + 1) * D, k * 128:(k + 1) * 128],
                                transpose=True).then_inc(p.inc("ld", 16), 16)
            load_done[idx] = p.c["ld"]

        NCHH = (T // 4) // CH
        st_free = 0
        strip_used = {}
        cw_used = {}
        zt_used = {}
        gp_free = {}
        up_free = {}
        dn_free = {}
        acc_ln_read = 0

        emit_wload(0)

        qctr = 0
        NQTR = 4
        for half in range(NQTR):
            tok0h = half * (T // 4)
            for ei in range(E + 1):
                idx = half * (E + 1) + ei
                for c in range(NCHH):
                    tok0 = tok0h + c * CH
                    spar = qctr % 2

                    @blk.scalar
                    def _(e, tok0=tok0, spar=spar, q=qctr):
                        if q >= 2:
                            p.wge(e, "ta", strip_used.get(q - 2, 0))
                        for k in range(KT):
                            e.dma_start(xfm[spar][k][:, :],
                                        io["x_in"][tok0:tok0 + CH, k * 128:(k + 1) * 128],
                                        transpose=True).then_inc(p.inc(f"xa{spar}", 16), 16)

                    xa_now = (f"xa{spar}", p.c[f"xa{spar}"])

                    if ei < E:
                        @blk.sync
                        def _(e, tok0=tok0, q=qctr):
                            if q >= 1:
                                p.wge(e, "va", cw_used.get(q - 1, 0))
                            e.dma_start(cwt[:, :].rearrange("q (tt c) -> q tt c", tt=NTT),
                                        io["cwsc"][tok0:tok0 + CH, :].rearrange("(tt q) c -> q tt c", q=128)).then_inc(p.inc("xc", 16), 16)
                    xo_now = ("xc", p.c["xc"])

                    for m in range(16):
                        mp = m % 2
                        @blk.tensor
                        def _(e, m=m, mp=mp, spar=spar, idx=idx, q=qctr):
                            if m == 0:
                                p.wge(e, "ld", load_done[idx])
                                p.wge(e, xa_now[0], xa_now[1])
                            p.wge(e, "sa", gp_free.get(mp, 0))
                            p.wge(e, "va", up_free.get(mp, 0))
                            inst = None
                            for k in range(KT):
                                e.matmul(gp[mp][:, :], gw_sb[k][:, m * 128:(m + 1) * 128],
                                         xfm[spar][k][:, :], start=(k == 0), stop=(k == KT - 1))
                            for k in range(KT):
                                inst = e.matmul(upp[mp][:, :], uw_sb[k][:, m * 128:(m + 1) * 128],
                                                xfm[spar][k][:, :], start=(k == 0), stop=(k == KT - 1))
                            inst.then_inc(p.inc("ta", 1), 1)

                        ta_gu = p.c["ta"]

                        @blk.scalar
                        def _(e, m=m, mp=mp, ta_gu=ta_gu):
                            p.wge(e, "ta", ta_gu)
                            p.wge(e, "va", st_free)
                            e.activation(st_[:, :], gp[mp][:, :], AF.Silu).then_inc(p.inc("sa", 1), 1)

                        gp_free[mp] = p.c["sa"]
                        sa_silu = p.c["sa"]

                        @blk.vector
                        def _(e, m=m, mp=mp, sa_silu=sa_silu, q=qctr):
                            p.wge(e, "sa", sa_silu)
                            p.wge(e, "ta", zt_used.get(q - 1, 0) if m == 0 else 0)
                            e.tensor_tensor(zt[:, m * CH:(m + 1) * CH], st_[:, :], upp[mp][:, :],
                                            ALU.mult).then_inc(p.inc("va", 1), 1)

                        up_free[mp] = p.c["va"]
                        st_free = p.c["va"]

                    va_z = p.c["va"]

                    for tt in range(NTT):
                        t_loc = c * NTT + tt
                        for n in range(2):
                            dp = (tt * 2 + n) % 2
                            @blk.tensor
                            def _(e, tt=tt, n=n, dp=dp, idx=idx, va_z=va_z):
                                p.wge(e, "va", va_z)
                                p.wge(e, "va", dn_free.get(dp, 0))
                                inst = None
                                for k2 in range(16):
                                    inst = e.matmul(dn[dp][:, :], zt[:, k2 * CH + tt * 128: k2 * CH + (tt + 1) * 128],
                                                    dw_sb[k2][:, n * 512:(n + 1) * 512],
                                                    start=(k2 == 0), stop=(k2 == 15))
                                inst.then_inc(p.inc("ta", 1), 1)

                            ta_dn = p.c["ta"]

                            @blk.vector
                            def _(e, tt=tt, n=n, dp=dp, ei=ei, t_loc=t_loc, ta_dn=ta_dn, xo_now=xo_now, half=half, aln=acc_ln_read):
                                p.wge(e, "ta", ta_dn)
                                if ei < E:
                                    p.wge(e, xo_now[0], xo_now[1])
                                if ei == 0:
                                    p.wge(e, "va", aln)
                                asl = acc[:, t_loc * D + n * 512: t_loc * D + (n + 1) * 512]
                                if ei == 0:
                                    e.tensor_scalar(asl, dn[dp][:, :], cwt[:, tt * 8 + ei:tt * 8 + ei + 1],
                                                    None, ALU.mult).then_inc(p.inc("va", 1), 1)
                                elif ei < E:
                                    e.scalar_tensor_tensor(asl, dn[dp][:, :], cwt[:, tt * 8 + ei:tt * 8 + ei + 1],
                                                           asl, ALU.mult, ALU.add).then_inc(p.inc("va", 1), 1)
                                else:
                                    e.tensor_tensor(asl, asl, dn[dp][:, :], ALU.add).then_inc(p.inc("va", 1), 1)

                            dn_free[dp] = p.c["va"]

                    cw_used[qctr] = p.c["va"]
                    strip_used[qctr] = p.c["ta"]
                    zt_used[qctr] = p.c["ta"]
                    qctr += 1

                wbuf_free[idx] = p.c["ta"]
                if idx + 1 <= NQTR * (E + 1) - 1:
                    emit_wload(idx + 1)

            va_moe = p.c["va"]
            x0_free = 0
            for tl in range(QT):
                g = half * QT + tl
                @blk.sync
                def _(e, g=g, x0_free=x0_free):
                    if x0_free:
                        p.wge(e, "va", x0_free)
                    e.dma_start(x0m[:, :], io["x_in"][g * 128:(g + 1) * 128, :]).then_inc(p.inc("xm", 16), 16)

                xo_now = p.c["xm"]

                @blk.vector
                def _(e, tl=tl, xo_now=xo_now, va_moe=va_moe):
                    p.wge(e, "va", va_moe)
                    p.wge(e, "xm", xo_now)
                    p.wge(e, "os")
                    e.tensor_tensor(xpre[:, 0:512], x0m[:, 0:512], acc[:, tl * D:tl * D + 512],
                                    ALU.add).then_inc(p.inc("va", 1), 1)
                    e.tensor_tensor(xpre[:, 512:1024], x0m[:, 512:1024], acc[:, tl * D + 512:tl * D + 1024],
                                    ALU.add).then_inc(p.inc("va", 1), 1)

                x0_free = p.c["va"]
                emit_ln_store(p, blk, lnb, xpre, io["x_out"], g * 128)

            acc_ln_read = p.c["va"]

        @blk.sync
        def _(e):
            p.wge(e, "os")
    return



def emit_phase_barrier(nc, p):
    snap = [(n, v) for n, v in p.c.items() if v > 0]
    with nc.Block() as blk:
        for engname in ["tensor", "vector", "scalar", "sync", "gpsimd"]:
            @getattr(blk, engname)
            def _(e, snap=snap):
                for n, v in snap:
                    e.wait_ge(p.sems[n], v)


def build_fused_a_program():
    nc = bass.Bass("TRN2", target_bir_lowering=False)
    G3 = 3072
    consts = declare_consts(nc)

    def dt(name, shape, dtype=F32, kind=None):
        h = nc.dram_tensor(name, shape, dtype, kind=kind) if kind else nc.dram_tensor(name, shape, dtype)
        return h

    io1 = {"consts": consts}
    io1["x_in"] = dt("x_in", [T, D], F32, "ExternalInput").ap()
    for w in ["Wq", "Wk", "Wv", "Wo"]:
        io1[w] = dt("a0_" + w, [D, D], F32, "ExternalInput").ap()
    for b in ["bq", "bk", "bv", "bo"]:
        io1[b] = dt("a0_" + b, [D], F32, "ExternalInput").ap()
    io1["x_out"] = dt("x1sc", [T, D], BF16).ap()
    io1["xbf"] = dt("a0_xbf", [T, D], BF16).ap()
    io1["qsc"] = dt("a0_qsc", [T, D], BF16).ap()
    io1["wsc"] = dt("a0_wsc", [4 * D, D], BF16).ap()

    io2 = {"consts": consts, "x_in": io1["x_out"]}
    io2["gate_W"] = dt("f0_gate_W", [D, 8], F32, "ExternalInput").ap()
    io2["W1"] = dt("f0_W1", [8, D, 512], F32, "ExternalInput").ap()
    io2["b1"] = dt("f0_b1", [8, 512], F32, "ExternalInput").ap()
    io2["W2"] = dt("f0_W2", [8, 512, D], F32, "ExternalInput").ap()
    io2["b2"] = dt("f0_b2", [8, D], F32, "ExternalInput").ap()
    io2["Ws1"] = dt("f0_Ws1", [D, 512], F32, "ExternalInput").ap()
    io2["bs1"] = dt("f0_bs1", [512], F32, "ExternalInput").ap()
    io2["Ws2"] = dt("f0_Ws2", [512, D], F32, "ExternalInput").ap()
    io2["bs2"] = dt("f0_bs2", [D], F32, "ExternalInput").ap()
    io2["route_scale"] = dt("f0_route_scale", [1], F32, "ExternalInput").ap()
    io2["x_out"] = dt("x2sc", [T, D], BF16).ap()

    io3 = {"consts": consts, "x_in": io2["x_out"]}
    for w in ["Wk", "Wv", "Wr", "Wo"]:
        io3[w] = dt("a1_" + w, [D, D], F32, "ExternalInput").ap()
    io3["time_decay"] = dt("a1_time_decay", [16, 64], F32, "ExternalInput").ap()
    io3["time_first"] = dt("a1_time_first", [16, 64], F32, "ExternalInput").ap()
    io3["x_out"] = dt("x3_out", [T, D], BF16).ap()
    for wsc in ["wksc", "wvsc", "wrsc", "wosc"]:
        io3[wsc] = dt("a1_" + wsc, [D, D], BF16).ap()

    io4 = {"consts": consts, "x_in": io3["x_out"]}
    io4["gru_w_ih"] = dt("f1_gru_w_ih", [G3, D], F32, "ExternalInput").ap()
    io4["gru_w_hh"] = dt("f1_gru_w_hh", [G3, D], F32, "ExternalInput").ap()
    io4["expr_w"] = dt("f1_expr_w", [D, D], F32, "ExternalInput").ap()
    io4["ctx_w"] = dt("f1_ctx_w", [D, D], F32, "ExternalInput").ap()
    io4["ctx_b"] = dt("f1_ctx_b", [D], F32, "ExternalInput").ap()
    io4["distsc"] = dt("distsc", [T, 8], F32).ap()
    io4["penpart"] = dt("penpart", [1, 1], F32).ap()
    io4["wihsc"] = dt("f1_wihsc", [G3, D], BF16).ap()
    io4["whhsc"] = dt("f1_whhsc", [G3, D], BF16).ap()
    io4["ctxsc"] = dt("f1_ctxsc", [D, D], BF16).ap()
    io4["exprsc"] = dt("f1_exprsc", [D, D], BF16).ap()
    xwsc_h = nc.dram_tensor("f1_xwsc", [S * BPC, G3], BF16)
    io4["xwsc"] = xwsc_h.ap()
    io4["xwsc_h"] = xwsc_h
    hsc_h = nc.dram_tensor("f1_hsc", [S * BPC, D], BF16)
    io4["hsc"] = hsc_h.ap()
    io4["hsc_h"] = hsc_h

    io5 = {"consts": consts, "x_in": io3["x_out"],
           "distsc": io4["distsc"], "penpart": io4["penpart"]}
    io5["gate_w"] = dt("f1_gate_w", [8, 2048, D], F32, "ExternalInput").ap()
    io5["up_w"] = dt("f1_up_w", [8, 2048, D], F32, "ExternalInput").ap()
    io5["down_w"] = dt("f1_down_w", [8, D, 2048], F32, "ExternalInput").ap()
    io5["shared_gate_w"] = dt("f1_shared_gate_w", [2048, D], F32, "ExternalInput").ap()
    io5["shared_up_w"] = dt("f1_shared_up_w", [2048, D], F32, "ExternalInput").ap()
    io5["shared_down_w"] = dt("f1_shared_down_w", [D, 2048], F32, "ExternalInput").ap()
    io5["x_out"] = dt("x_out", [T, D], F32, "ExternalOutput").ap()
    io5["cwsc"] = dt("f1_cwsc", [T, 8], F32).ap()
    io5["gwsc"] = dt("f1_gwsc", [9 * 2048, D], BF16).ap()
    io5["uwsc"] = dt("f1_uwsc", [9 * 2048, D], BF16).ap()
    io5["dwsc"] = dt("f1_dwsc", [9 * D, 2048], BF16).ap()

    es = contextlib.ExitStack()
    with es:
        p = Prog(nc, es)
        for emit, io_ in [(emit_l0_attn, io1), (emit_l0_ffn, io2),
                          (emit_l1_attn, io3), (emit_l1_ffn, io4),
                          (emit_l1_ffn_tail2, io5)]:
            with contextlib.ExitStack() as eph:
                p.es = eph
                emit(p, io_)
            if emit is not emit_l1_ffn_tail2:
                emit_phase_barrier(nc, p)
        p.es = es
    return nc


def build_l0_attn_program():
    nc = bass.Bass("TRN2", target_bir_lowering=False)
    io = {}
    io["x_in"] = nc.dram_tensor("x_in", [T, D], F32, kind="ExternalInput").ap()
    for w in ["Wq", "Wk", "Wv", "Wo"]:
        io[w] = nc.dram_tensor(w, [D, D], F32, kind="ExternalInput").ap()
    for b in ["bq", "bk", "bv", "bo"]:
        io[b] = nc.dram_tensor(b, [D], F32, kind="ExternalInput").ap()
    io["x_out"] = nc.dram_tensor("x_out", [T, D], BF16, kind="ExternalOutput").ap()
    io["xbf"] = nc.dram_tensor("xbf", [T, D], BF16).ap()
    io["qsc"] = nc.dram_tensor("qsc", [T, D], BF16).ap()
    io["wsc"] = nc.dram_tensor("wsc", [4 * D, D], BF16).ap()
    io["consts"] = declare_consts(nc)
    es = contextlib.ExitStack()
    with es:
        p = Prog(nc, es)
        emit_l0_attn(p, io)
    return nc


def build_l1_attn_program():
    nc = bass.Bass("TRN2", target_bir_lowering=False)
    io = {}
    io["x_in"] = nc.dram_tensor("x_in", [T, D], BF16, kind="ExternalInput").ap()
    for w in ["Wk", "Wv", "Wr", "Wo"]:
        io[w] = nc.dram_tensor(w, [D, D], F32, kind="ExternalInput").ap()
    io["time_decay"] = nc.dram_tensor("time_decay", [16, 64], F32, kind="ExternalInput").ap()
    io["time_first"] = nc.dram_tensor("time_first", [16, 64], F32, kind="ExternalInput").ap()
    io["x_out"] = nc.dram_tensor("x_out", [T, D], BF16, kind="ExternalOutput").ap()
    for wsc in ["wksc", "wvsc", "wrsc", "wosc"]:
        io[wsc] = nc.dram_tensor(wsc, [D, D], BF16).ap()
    io["consts"] = declare_consts(nc)
    es = contextlib.ExitStack()
    with es:
        p = Prog(nc, es)
        emit_l1_attn(p, io)
    return nc


def build_l1_ffn_a_program():
    nc = bass.Bass("TRN2", target_bir_lowering=False)
    G3 = 3072
    io = {}
    io["x_in"] = nc.dram_tensor("x_in", [T, D], BF16, kind="ExternalInput").ap()
    io["gru_w_ih"] = nc.dram_tensor("gru_w_ih", [G3, D], F32, kind="ExternalInput").ap()
    io["gru_w_hh"] = nc.dram_tensor("gru_w_hh", [G3, D], F32, kind="ExternalInput").ap()
    io["expr_w"] = nc.dram_tensor("expr_w", [D, D], F32, kind="ExternalInput").ap()
    io["ctx_w"] = nc.dram_tensor("ctx_w", [D, D], F32, kind="ExternalInput").ap()
    io["ctx_b"] = nc.dram_tensor("ctx_b", [D], F32, kind="ExternalInput").ap()
    io["distsc"] = nc.dram_tensor("distsc", [T, 8], F32, kind="ExternalOutput").ap()
    io["penpart"] = nc.dram_tensor("penpart", [1, 1], F32, kind="ExternalOutput").ap()
    G3 = 3072
    io["wihsc"] = nc.dram_tensor("wihsc", [G3, D], BF16).ap()
    io["whhsc"] = nc.dram_tensor("whhsc", [G3, D], BF16).ap()
    io["ctxsc"] = nc.dram_tensor("ctxsc", [D, D], BF16).ap()
    io["exprsc"] = nc.dram_tensor("exprsc", [D, D], BF16).ap()
    xwsc_h = nc.dram_tensor("xwsc", [S * BPC, G3], BF16)
    io["xwsc"] = xwsc_h.ap()
    io["xwsc_h"] = xwsc_h
    hsc_h = nc.dram_tensor("hsc", [S * BPC, D], BF16)
    io["hsc"] = hsc_h.ap()
    io["hsc_h"] = hsc_h
    io["consts"] = declare_consts(nc)
    es = contextlib.ExitStack()
    with es:
        p = Prog(nc, es)
        emit_l1_ffn(p, io)
    return nc


def build_l1_ffn_b_program():
    nc = bass.Bass("TRN2", target_bir_lowering=False)
    io = {}
    io["x_in"] = nc.dram_tensor("x_in", [T, D], BF16, kind="ExternalInput").ap()
    io["distsc"] = nc.dram_tensor("distsc", [T, 8], F32, kind="ExternalInput").ap()
    io["pens"] = nc.dram_tensor("pens", [1, 8], F32, kind="ExternalInput").ap()
    io["gate_w"] = nc.dram_tensor("gate_w", [8, 2048, D], F32, kind="ExternalInput").ap()
    io["up_w"] = nc.dram_tensor("up_w", [8, 2048, D], F32, kind="ExternalInput").ap()
    io["down_w"] = nc.dram_tensor("down_w", [8, D, 2048], F32, kind="ExternalInput").ap()
    io["shared_gate_w"] = nc.dram_tensor("shared_gate_w", [2048, D], F32, kind="ExternalInput").ap()
    io["shared_up_w"] = nc.dram_tensor("shared_up_w", [2048, D], F32, kind="ExternalInput").ap()
    io["shared_down_w"] = nc.dram_tensor("shared_down_w", [D, 2048], F32, kind="ExternalInput").ap()
    io["x_out"] = nc.dram_tensor("x_out", [T, D], F32, kind="ExternalOutput").ap()
    io["cwsc"] = nc.dram_tensor("cwsc", [T, 8], F32).ap()
    io["gwsc"] = nc.dram_tensor("gwsc", [9 * 2048, D], BF16).ap()
    io["uwsc"] = nc.dram_tensor("uwsc", [9 * 2048, D], BF16).ap()
    io["dwsc"] = nc.dram_tensor("dwsc", [9 * D, 2048], BF16).ap()
    io["consts"] = declare_consts(nc)
    es = contextlib.ExitStack()
    with es:
        p = Prog(nc, es)
        emit_l1_ffn_tail2(p, io)
    return nc


_CACHED = {}


def _get_prog(name, builder):
    if name not in _CACHED:
        _CACHED[name] = builder()
    return _CACHED[name]


def kernel(**inputs):
    assert S == 4096 and T == 8192, "kernel() requires full dims"
    consts = host_consts()
    x = np.ascontiguousarray(np.asarray(inputs["L0_attn_x"], dtype=np.float32))
    shards = [np.ascontiguousarray(x[2 * c:2 * c + 2].reshape(T, D)) for c in range(NCORES)]

    def amap(names, src_prefix, dst_prefix):
        return {dst_prefix + k: np.ascontiguousarray(np.asarray(inputs[src_prefix + k], dtype=np.float32))
                for k in names}

    ncA = _get_prog("fusedall", build_fused_a_program)
    wA = {}
    wA.update(amap(["Wq", "bq", "Wk", "bk", "Wv", "bv", "Wo", "bo"], "L0_attn_", "a0_"))
    wA.update(amap(["gate_W", "W1", "b1", "W2", "b2", "Ws1", "bs1", "Ws2", "bs2"], "L0_ffn_", "f0_"))
    wA["f0_route_scale"] = np.asarray(inputs["L0_ffn_route_scale"], dtype=np.float32).reshape(1)
    wA.update(amap(["Wk", "Wv", "Wr", "Wo", "time_decay", "time_first"], "L1_attn_", "a1_"))
    wA.update(amap(["gru_w_ih", "gru_w_hh", "expr_w", "ctx_w", "ctx_b",
                    "gate_w", "up_w", "down_w",
                    "shared_gate_w", "shared_up_w", "shared_down_w"], "L1_ffn_", "f1_"))
    in_maps = []
    for c in range(NCORES):
        m = {"x_in": shards[c]}
        m.update(wA)
        m.update(consts)
        in_maps.append(m)
    res = run_bass_kernel_spmd(ncA, in_maps, core_ids=list(range(NCORES)))
    outs = [res.results[c]["x_out"] for c in range(NCORES)]
    return np.concatenate([o.reshape(BPC, S, D) for o in outs], axis=0)


def build_l0_ffn_program():
    nc = bass.Bass("TRN2", target_bir_lowering=False)
    io = {}
    io["x_in"] = nc.dram_tensor("x_in", [T, D], BF16, kind="ExternalInput").ap()
    io["gate_W"] = nc.dram_tensor("gate_W", [D, 8], F32, kind="ExternalInput").ap()
    io["W1"] = nc.dram_tensor("W1", [8, D, 512], F32, kind="ExternalInput").ap()
    io["b1"] = nc.dram_tensor("b1", [8, 512], F32, kind="ExternalInput").ap()
    io["W2"] = nc.dram_tensor("W2", [8, 512, D], F32, kind="ExternalInput").ap()
    io["b2"] = nc.dram_tensor("b2", [8, D], F32, kind="ExternalInput").ap()
    io["Ws1"] = nc.dram_tensor("Ws1", [D, 512], F32, kind="ExternalInput").ap()
    io["bs1"] = nc.dram_tensor("bs1", [512], F32, kind="ExternalInput").ap()
    io["Ws2"] = nc.dram_tensor("Ws2", [512, D], F32, kind="ExternalInput").ap()
    io["bs2"] = nc.dram_tensor("bs2", [D], F32, kind="ExternalInput").ap()
    io["route_scale"] = nc.dram_tensor("route_scale", [1], F32, kind="ExternalInput").ap()
    io["x_out"] = nc.dram_tensor("x_out", [T, D], BF16, kind="ExternalOutput").ap()
    io["consts"] = declare_consts(nc)
    es = contextlib.ExitStack()
    with es:
        p = Prog(nc, es)
        emit_l0_ffn(p, io)
    return nc


# version 10 (final)
# speedup vs baseline: 1.2163x; 1.0529x over previous
import contextlib
import numpy as np
import ml_dtypes
import concourse.bass as bass
import concourse.mybir as mybir
from concourse.bass_utils import run_bass_kernel_spmd

F32 = mybir.dt.float32
BF16 = mybir.dt.bfloat16
AF = mybir.ActivationFunctionType
ALU = mybir.AluOpType

NCORES = 8
BPC = 2
S = 4096
T = BPC * S
D = 1024
KT = D // 128
H = 16
HD = 64
LN_EPS = 1e-5
BF = ml_dtypes.bfloat16
SIM_NO_COLLECTIVE = False


def set_dims(s):
    global S, T
    S = s
    T = BPC * s


class Prog:
    def __init__(self, nc, es):
        self.nc = nc
        self.es = es
        self.es_sem = es
        self.c = {}
        self.sems = {}

    def sem(self, name):
        if name not in self.sems:
            self.sems[name] = self.es_sem.enter_context(self.nc.semaphore(name))
            self.c[name] = 0
        return self.sems[name]

    def inc(self, name, k):
        self.c[name] += k
        return self.sems[name]

    def wge(self, e, name, val=None):
        if val is None:
            val = self.c[name]
        if val > 0:
            e.wait_ge(self.sems[name], val)

    def bar(self, e, name, inst):
        inst.then_inc(self.inc(name, 1), 1)
        self.wge(e, name)

    def sb(self, name, shape, dt=F32):
        return self.es.enter_context(self.nc.sbuf_tensor(name, shape, dt))

    def ps(self, name, shape, dt=F32):
        return self.es.enter_context(self.nc.psum_tensor(name, shape, dt))


def host_consts():
    c = {}
    c["c_ones1"] = np.ones((1, 128), dtype=BF)
    c["c_ident"] = np.eye(128, dtype=BF)
    ese = np.zeros((8, 8 * 128), dtype=BF)
    for e in range(8):
        ese[e, e * 128:(e + 1) * 128] = 1
    c["c_esel"] = ese
    c["c_ones1f"] = np.ones((1, 128), dtype=np.float32)
    ii = np.arange(128)
    c["c_dgrid"] = np.where(ii[None, :] >= ii[:, None], ii[None, :] - ii[:, None], 0).astype(np.float32)
    c["c_mgridf"] = (ii[None, :] >= ii[:, None]).astype(np.float32)
    c["c_ramp"] = np.arange(130, dtype=np.float32)[None, :]
    c["c_rrcol"] = (127.0 - ii).astype(np.float32)[:, None]
    c["c_identf"] = np.eye(128, dtype=np.float32)
    c["c_onescol"] = np.ones((128, 1), dtype=BF)
    c["c_onescolf"] = np.ones((128, 1), dtype=np.float32)
    bsl = np.zeros((128, 4), dtype=BF)
    bsl[:, 0] = 1
    bsl[:, 3] = 1
    c["c_bsel"] = bsl
    sel = np.zeros((16, 8 * 128), dtype=BF)
    for p in range(8):
        sel[2 * p, p * 128: p * 128 + 64] = 1
        sel[2 * p + 1, p * 128 + 64: p * 128 + 128] = 1
    c["c_pairsel"] = sel
    return c


def declare_consts(nc):
    return {
        "ones1": nc.dram_tensor("c_ones1", [1, 128], BF16, kind="ExternalInput").ap(),
        "pairsel": nc.dram_tensor("c_pairsel", [16, 8 * 128], BF16, kind="ExternalInput").ap(),
        "ident": nc.dram_tensor("c_ident", [128, 128], BF16, kind="ExternalInput").ap(),
        "esel": nc.dram_tensor("c_esel", [8, 8 * 128], BF16, kind="ExternalInput").ap(),
        "ones1f": nc.dram_tensor("c_ones1f", [1, 128], F32, kind="ExternalInput").ap(),
        "dgrid": nc.dram_tensor("c_dgrid", [128, 128], F32, kind="ExternalInput").ap(),
        "mgridf": nc.dram_tensor("c_mgridf", [128, 128], F32, kind="ExternalInput").ap(),
        "ramp": nc.dram_tensor("c_ramp", [1, 130], F32, kind="ExternalInput").ap(),
        "rrcol": nc.dram_tensor("c_rrcol", [128, 1], F32, kind="ExternalInput").ap(),
        "identf": nc.dram_tensor("c_identf", [128, 128], F32, kind="ExternalInput").ap(),
        "onescol": nc.dram_tensor("c_onescol", [128, 1], BF16, kind="ExternalInput").ap(),
        "onescolf": nc.dram_tensor("c_onescolf", [128, 1], F32, kind="ExternalInput").ap(),
        "bsel": nc.dram_tensor("c_bsel", [128, 4], BF16, kind="ExternalInput").ap(),
    }


def emit_ln_store(p, blk, lnb, xpre, x_out, tok0):
    stat, mv, mneg, rstd, xout_tok, epscol = lnb

    @blk.vector
    def _(e):
        p.wge(e, "va")
        p.bar(e, "va", e.bn_stats(stat[:, 0:6], xpre[:, 0:512]))
        p.bar(e, "va", e.bn_stats(stat[:, 6:12], xpre[:, 512:1024]))
        p.bar(e, "va", e.bn_aggr(mv[:, :], stat[:, :]))
        e.tensor_scalar(mneg[:, :], mv[:, 0:1], -1.0, None, ALU.mult).then_inc(p.inc("va", 1), 1)

    @blk.scalar
    def _(e):
        p.wge(e, "va")
        p.bar(e, "sa", e.activation(rstd[:, :], mv[:, 1:2], AF.Ln, bias=epscol[:, :]))
        e.activation(rstd[:, :], rstd[:, :], AF.Exp, scale=-0.5).then_inc(p.inc("sa", 1), 1)

    @blk.vector
    def _(e):
        p.wge(e, "sa")
        e.tensor_scalar(xout_tok[:, :], xpre[:, :], mneg[:, :], rstd[:, :],
                        ALU.add, ALU.mult).then_inc(p.inc("va", 1), 1)

    @blk.sync
    def _(e):
        p.wge(e, "va")
        e.dma_start(x_out[tok0:tok0 + 128, :], xout_tok[:, :]).then_inc(p.inc("os", 16), 16)



def emit_l0_attn(p, io, dbg=None):
    nc = p.nc
    cs = io["consts"]
    blk = p.es.enter_context(nc.Block())

    wq = [p.sb(f"l0a_wq{k}", [128, D], BF16) for k in range(KT)]
    wk = [p.sb(f"l0a_wk{k}", [128, D], BF16) for k in range(KT)]
    wv = [p.sb(f"l0a_wv{k}", [128, D], BF16) for k in range(KT)]
    wo = [p.sb(f"l0a_wo{k}", [128, D], BF16) for k in range(KT)]
    brow = p.sb("l0a_brow", [1, 4 * D], BF16)
    ones1 = p.sb("l0a_ones1", [1, 128], BF16)
    pairsel = p.sb("l0a_psel", [16, 8 * 128], BF16)
    onescol = p.sb("l0a_onescol", [128, 1], BF16)

    STRIP = min(2048, S)
    NSTRIP = T // STRIP
    NTILE = STRIP // 128
    TPB = S // 128
    xf = [[p.sb(f"l0a_xf{b}_{k}", [128, STRIP], BF16) for k in range(KT)] for b in range(2)]

    q_tok = [p.sb(f"l0a_qtok{i}", [128, D], BF16) for i in range(2)]
    k_tok = [p.sb(f"l0a_ktok{i}", [128, D], BF16) for i in range(2)]
    v_tok = [p.sb(f"l0a_vtok{i}", [128, D], BF16) for i in range(2)]
    xm_ = p.sb("l0a_xm", [128, 512], F32)
    ee = [p.sb(f"l0a_ee{i}", [128, 512], F32) for i in range(2)]
    rl = [p.sb(f"l0a_rl{i}", [128, 512], F32) for i in range(2)]

    kv_sb = p.sb("l0a_kvsb", [128, 8 * 64], BF16)
    kvacc = p.sb("l0a_kvacc", [128, 8 * 64], F32)
    ksacc = p.sb("l0a_ksacc", [128, 8], F32)
    ks_sb = [p.sb(f"l0a_kssb{k}", [128, 16], BF16) for k in range(KT)]
    qf = [[p.sb(f"l0a_qf{b}_{k}", [128, 512], BF16) for k in range(KT)] for b in range(2)]
    att = [p.sb(f"l0a_att{k}", [128, 512], BF16) for k in range(KT)]
    rz_bf = p.sb("l0a_rzbf", [16, 512], BF16)
    rz_f = p.sb("l0a_rzf", [16, 512], F32)
    rzb_sb = [p.sb(f"l0a_rzbsb{i}", [128, 512], BF16) for i in range(2)]
    x0_tok = p.sb("l0a_x0tok", [128, D], BF16)
    xpre = p.sb("l0a_xpre", [128, D], F32)
    lnb = (p.sb("l0a_stat", [128, 12], F32), p.sb("l0a_mv", [128, 2], F32),
           p.sb("l0a_mneg", [128, 1], F32), p.sb("l0a_rstd", [128, 1], F32),
           p.sb("l0a_xouttok", [128, D], BF16), p.sb("l0a_eps", [128, 1], F32))

    bank = [p.ps(f"l0a_bank{i}", [128, 512], F32) for i in range(8)]
    pkv = bank[6]
    pks = bank[7]

    for sname in ["ld", "xa0", "xa1", "qa0", "qa1", "xo", "os", "ta", "va", "sa", "ga", "so0", "so1"]:
        p.sem(sname)

    @blk.gpsimd
    def _(e):
        for j in range(T // 128):
            e.dma_start(io["xbf"][j * 128:(j + 1) * 128, :],
                        io["x_in"][j * 128:(j + 1) * 128, :]).then_inc(p.inc("ga", 16), 16)
        for i, w in enumerate(["Wq", "Wk", "Wv", "Wo"]):
            for j in range(D // 128):
                e.dma_start(io["wsc"][i * D + j * 128:i * D + (j + 1) * 128, :],
                            io[w][j * 128:(j + 1) * 128, :]).then_inc(p.inc("ga", 16), 16)
        for i, b in enumerate(["bq", "bk", "bv", "bo"]):
            e.dma_start(brow[:1, i * D:(i + 1) * D], io[b][None, :]).then_inc(p.inc("ga", 16), 16)

    ga_all = p.c["ga"]

    @blk.sync
    def _(e):
        e.dma_start(ones1[:1, :], cs["ones1"][:, :]).then_inc(p.inc("ld", 16), 16)
        e.dma_start(pairsel[:, :], cs["pairsel"][:, :]).then_inc(p.inc("ld", 16), 16)

    @blk.scalar
    def _(e):
        p.wge(e, "ga", ga_all)
        for k in range(KT):
            for wi, wt in enumerate([wq, wk, wv, wo]):
                e.dma_start(wt[k][:, :], io["wsc"][wi * D:(wi + 1) * D, k * 128:(k + 1) * 128],
                            transpose=True).then_inc(p.inc("ld", 16), 16)

    ld_all = p.c["ld"]

    @blk.vector
    def _(e):
        e.memset(lnb[5][:, :], LN_EPS).then_inc(p.inc("va", 1), 1)
        e.memset(onescol[:, :], 1.0).then_inc(p.inc("va", 1), 1)
        for k in range(KT):
            e.memset(ks_sb[k][:, :], 0.0).then_inc(p.inc("va", 1), 1)

    va_init = p.c["va"]

    def proj_chunk(e, bi, xf_s, ti, wt, n, brow_off):
        for k in range(KT):
            e.matmul(bank[bi][:, :], xf_s[k][:, ti * 128:(ti + 1) * 128],
                     wt[k][:, n * 512:(n + 1) * 512], start=(k == 0), stop=False)
        return e.matmul(bank[bi][:, :], ones1[:1, :],
                        brow[:1, brow_off + n * 512: brow_off + (n + 1) * 512],
                        start=False, stop=True)

    strip_loaded = {}
    strip_consumed = {}
    sa_free = {}
    va_adds = {}
    kv_accum = {}
    sa_vtok = {}
    ta_kv = {}
    so_q = {}

    for b in range(BPC):
        for g in range(b * TPB, (b + 1) * TPB):
            s, ti = divmod(g, NTILE)
            par = g % 2
            if ti == 0:
                sb_ = s % 2
                @blk.scalar
                def _(e, s=s, sb_=sb_):
                    if s == 0:
                        p.wge(e, "ga", ga_all)
                    if s >= 2:
                        p.wge(e, "ta", strip_consumed[s - 2])
                    for k in range(KT):
                        e.dma_start(xf[sb_][k][:, :],
                                    io["xbf"][s * STRIP:(s + 1) * STRIP, k * 128:(k + 1) * 128],
                                    transpose=True).then_inc(p.inc(f"xa{s % 2}", 16), 16)
                strip_loaded[s] = (f"xa{s % 2}", p.c[f"xa{s % 2}"])

                if dbg is not None and isinstance(dbg, dict) and "xf" in dbg and s == 0:
                    @blk.sync
                    def _(e, sb_=sb_):
                        p.wge(e, "xa0", strip_loaded[0][1])
                        for k in range(KT):
                            e.dma_start(dbg["xf"][k, :, :], xf[sb_][k][:, :]).then_inc(p.inc("os", 16), 16)

            @blk.tensor
            def _(e, s=s, ti=ti, g=g):
                if g == 0:
                    p.wge(e, "ld", ld_all)
                    p.wge(e, "va", va_init)
                if g == TPB:
                    p.wge(e, "va")
                    p.wge(e, "sa")
                p.wge(e, strip_loaded[s][0], strip_loaded[s][1])
                if g >= 1:
                    p.wge(e, "sa", sa_free[g - 1])
                proj_chunk(e, 0, xf[s % 2], ti, wq, 0, 0)
                proj_chunk(e, 1, xf[s % 2], ti, wq, 1, 0)
                proj_chunk(e, 2, xf[s % 2], ti, wk, 0, D)
                proj_chunk(e, 3, xf[s % 2], ti, wk, 1, D)
                proj_chunk(e, 4, xf[s % 2], ti, wv, 0, 2 * D)
                proj_chunk(e, 5, xf[s % 2], ti, wv, 1, 2 * D).then_inc(p.inc("ta", 1), 1)

            ta_proj = p.c["ta"]
            va_base = p.c["va"]
            sa_ci = []

            @blk.scalar
            def _(e, ta_proj=ta_proj, g=g, par=par, va_base=va_base):
                p.wge(e, "ta", ta_proj)
                p.wge(e, "va", va_base)
                if g >= 2:
                    p.wge(e, "ta", ta_kv[g - 2])
                for ci, bi in enumerate([0, 1, 2, 3]):
                    if ci >= 2:
                        p.wge(e, "va", va_base + ci - 1)
                    p.wge(e, "sa")
                    p.bar(e, "sa", e.activation(xm_[:, :], bank[bi][:, :], AF.Relu, scale=-1.0))
                    e.activation(ee[ci % 2][:, :], xm_[:, :], AF.Exp, scale=-1.0).then_inc(p.inc("sa", 1), 1)
                    e.activation(rl[ci % 2][:, :], bank[bi][:, :], AF.Relu).then_inc(p.inc("sa", 1), 1)
                    sa_ci.append(p.c["sa"])
                e.activation(v_tok[par][:, 0:512], bank[4][:, :], AF.Copy).then_inc(p.inc("sa", 1), 1)
                e.activation(v_tok[par][:, 512:1024], bank[5][:, :], AF.Copy).then_inc(p.inc("sa", 1), 1)

            sa_free[g] = p.c["sa"]
            sa_vtok[g] = p.c["sa"]

            @blk.vector
            def _(e, g=g, par=par, sa_ci=sa_ci):
                if g >= 2:
                    p.wge(e, "ta", ta_kv[g - 2])
                    p.wge(e, so_q[g - 2][0], so_q[g - 2][1])
                for ci in range(4):
                    p.wge(e, "sa", sa_ci[ci])
                    dst = q_tok[par] if ci < 2 else k_tok[par]
                    n = ci % 2
                    e.tensor_tensor(dst[:, n * 512:(n + 1) * 512], ee[ci % 2][:, :],
                                    rl[ci % 2][:, :], ALU.add).then_inc(p.inc("va", 1), 1)

            va_adds[g] = p.c["va"]

            @blk.tensor
            def _(e, g=g, par=par):
                p.wge(e, "va", va_adds[g])
                p.wge(e, "sa", sa_vtok[g])
                if g >= 1:
                    p.wge(e, "va", kv_accum[g - 1])
                inst = None
                for h in range(H):
                    pp, hi = divmod(h, 2)
                    e.matmul(pkv[hi * 64:(hi + 1) * 64, pp * 64:(pp + 1) * 64],
                             k_tok[par][:, h * 64:(h + 1) * 64],
                             v_tok[par][:, h * 64:(h + 1) * 64],
                             start=True, stop=True, skip_group_check=True)
                    inst = e.matmul(pks[hi * 64:(hi + 1) * 64, pp:pp + 1],
                                    k_tok[par][:, h * 64:(h + 1) * 64],
                                    onescol[:, :], start=True, stop=True,
                                    skip_group_check=True)
                inst.then_inc(p.inc("ta", 1), 1)

            ta_kv[g] = p.c["ta"]
            if ti == NTILE - 1:
                strip_consumed[s] = p.c["ta"]

            @blk.vector
            def _(e, g=g, first=(g % TPB == 0)):
                p.wge(e, "ta", ta_kv[g])
                if first:
                    e.tensor_copy(kvacc[:, :], pkv[:, :]).then_inc(p.inc("va", 1), 1)
                    e.tensor_copy(ksacc[:, :], pks[:, 0:8]).then_inc(p.inc("va", 1), 1)
                else:
                    e.tensor_tensor(kvacc[:, :], kvacc[:, :], pkv[:, :], ALU.add).then_inc(p.inc("va", 1), 1)
                    e.tensor_tensor(ksacc[:, :], ksacc[:, :], pks[:, 0:8], ALU.add).then_inc(p.inc("va", 1), 1)

            kv_accum[g] = p.c["va"]

            @blk.sync
            def _(e, g=g, par=par):
                p.wge(e, "va", va_adds[g])
                e.dma_start(io["qsc"][g * 128:(g + 1) * 128, :], q_tok[par][:, :]).then_inc(p.inc(f"so{par}", 16), 16)

            so_q[g] = (f"so{par}", p.c[f"so{par}"])

        @blk.vector
        def _(e):
            p.wge(e, "va", kv_accum[(b + 1) * TPB - 1])
            e.tensor_copy(kv_sb[:, :], kvacc[:, :]).then_inc(p.inc("va", 1), 1)
            for k in range(KT):
                e.tensor_copy(ks_sb[k][0:64, 2 * k:2 * k + 1], ksacc[0:64, k:k + 1]).then_inc(p.inc("va", 1), 1)
            for k in range(KT):
                e.tensor_copy(ks_sb[k][64:128, 2 * k + 1:2 * k + 2], ksacc[64:128, k:k + 1]).then_inc(p.inc("va", 1), 1)

        va_kvsb = p.c["va"]

        if dbg is not None:
            dbgkv = dbg["kv"] if isinstance(dbg, dict) else dbg

            @blk.sync
            def _(e, b=b, va_kvsb=va_kvsb, dbgkv=dbgkv):
                p.wge(e, "va", va_kvsb)
                e.dma_start(dbgkv[b, :, 0:512], kvacc[:, :]).then_inc(p.inc("os", 16), 16)
                e.dma_start(dbgkv[b, :, 512:520], ksacc[:, :]).then_inc(p.inc("os", 16), 16)

        NCH = S // 512
        rz_done = {}
        copy_done = {}
        att_done = {}
        qf_used = {}
        wo_evac = 0
        for c in range(NCH):
            cpar = c % 2
            tok0 = b * S + c * 512

            @blk.scalar
            def _(e, b=b, c=c, cpar=cpar, tok0=tok0):
                p.wge(e, "so0")
                p.wge(e, "so1")
                if c >= 2:
                    p.wge(e, "ta", qf_used[c - 2])
                for k in range(KT):
                    e.dma_start(qf[cpar][k][:, :],
                                io["qsc"][tok0:tok0 + 512, k * 128:(k + 1) * 128],
                                transpose=True).then_inc(p.inc(f"qa{cpar}", 16), 16)

            qa_now = (f"qa{cpar}", p.c[f"qa{cpar}"])

            @blk.tensor
            def _(e, c=c, cpar=cpar, qa_now=qa_now):
                p.wge(e, qa_now[0], qa_now[1])
                p.wge(e, "va", va_kvsb)
                p.wge(e, "sa")
                if c >= 1:
                    p.wge(e, "va", rz_done[c - 1])
                for k in range(KT - 1):
                    e.matmul(bank[0][0:16, :], ks_sb[k][:, :], qf[cpar][k][:, :],
                             start=(k == 0), stop=False)
                e.matmul(bank[0][0:16, :], ks_sb[KT - 1][:, :], qf[cpar][KT - 1][:, :],
                         start=False, stop=True).then_inc(p.inc("ta", 1), 1)

            ta_z = p.c["ta"]

            @blk.vector
            def _(e, ta_z=ta_z):
                p.wge(e, "ta", ta_z)
                p.bar(e, "va", e.tensor_scalar(rz_f[:, :], bank[0][0:16, :], 1e-6, None, ALU.add))
                p.bar(e, "va", e.reciprocal(rz_f[:, :], rz_f[:, :]))
                e.tensor_copy(rz_bf[:, :], rz_f[:, :]).then_inc(p.inc("va", 1), 1)

            rz_done[c] = p.c["va"]
            va_rz = p.c["va"]

            for pp in range(8):
                pb_bank = bank[1]
                po_bank = bank[2 + (pp % 2)]

                @blk.tensor
                def _(e, pp=pp, c=c, va_rz=va_rz):
                    p.wge(e, "va", va_rz)
                    if (c, pp - 1) in copy_done:
                        p.wge(e, "va", copy_done[(c, pp - 1)])
                    elif c >= 1:
                        p.wge(e, "va", copy_done[(c - 1, 7)])
                    e.matmul(pb_bank[:, :], pairsel[:, pp * 128:(pp + 1) * 128], rz_bf[:, :],
                             start=True, stop=True).then_inc(p.inc("ta", 1), 1)

                ta_pb = p.c["ta"]

                @blk.vector
                def _(e, ta_pb=ta_pb, pp=pp):
                    p.wge(e, "ta", ta_pb)
                    e.tensor_copy(rzb_sb[pp % 2][:, :], pb_bank[:, :]).then_inc(p.inc("va", 1), 1)

                copy_done[(c, pp)] = p.c["va"]
                va_rzb = p.c["va"]

                @blk.tensor
                def _(e, pp=pp, c=c, cpar=cpar, va_rzb=va_rzb):
                    p.wge(e, "va", va_rzb)
                    if (c, pp - 2) in att_done:
                        p.wge(e, "va", att_done[(c, pp - 2)])
                    elif c >= 1:
                        p.wge(e, "va", att_done[(c - 1, 6 + pp)])
                    e.matmul(po_bank[0:64, :], kv_sb[0:64, pp * 64:(pp + 1) * 64],
                             qf[cpar][pp][0:64, :], start=True, stop=True)
                    e.matmul(po_bank[64:128, :], kv_sb[64:128, pp * 64:(pp + 1) * 64],
                             qf[cpar][pp][64:128, :], start=True, stop=True).then_inc(p.inc("ta", 1), 1)

                ta_po = p.c["ta"]

                @blk.vector
                def _(e, pp=pp, ta_po=ta_po):
                    p.wge(e, "ta", ta_po)
                    e.tensor_tensor(att[pp][:, :], po_bank[:, :], rzb_sb[pp % 2][:, :],
                                    ALU.mult).then_inc(p.inc("va", 1), 1)

                att_done[(c, pp)] = p.c["va"]

            va_att = p.c["va"]
            qf_used[c] = p.c["ta"]

            for j in range(4):
                tt0 = tok0 + j * 128

                @blk.sync
                def _(e, tt0=tt0, wo_evac=wo_evac):
                    if wo_evac:
                        p.wge(e, "va", wo_evac)
                    e.dma_start(x0_tok[:, :], io["xbf"][tt0:tt0 + 128, :]).then_inc(p.inc("xo", 16), 16)

                ld_x0 = p.c["xo"]

                @blk.tensor
                def _(e, j=j, va_att=va_att, wo_evac=wo_evac):
                    p.wge(e, "va", va_att)
                    if wo_evac:
                        p.wge(e, "va", wo_evac)
                    for n in range(2):
                        for k in range(KT):
                            e.matmul(bank[4 + n][:, :], att[k][:, j * 128:(j + 1) * 128],
                                     wo[k][:, n * 512:(n + 1) * 512], start=(k == 0), stop=False)
                        inst = e.matmul(bank[4 + n][:, :], ones1[:1, :],
                                        brow[:1, 3 * D + n * 512: 3 * D + (n + 1) * 512],
                                        start=False, stop=True)
                    inst.then_inc(p.inc("ta", 1), 1)

                ta_wo = p.c["ta"]

                @blk.vector
                def _(e, ta_wo=ta_wo, ld_x0=ld_x0):
                    p.wge(e, "ta", ta_wo)
                    p.wge(e, "xo", ld_x0)
                    p.wge(e, "os")
                    e.tensor_tensor(xpre[:, 0:512], x0_tok[:, 0:512], bank[4][:, :],
                                    ALU.add).then_inc(p.inc("va", 1), 1)
                    e.tensor_tensor(xpre[:, 512:1024], x0_tok[:, 512:1024], bank[5][:, :],
                                    ALU.add).then_inc(p.inc("va", 1), 1)

                wo_evac = p.c["va"]
                emit_ln_store(p, blk, lnb, xpre, io["x_out"], tt0)

    @blk.sync
    def _(e):
        p.wge(e, "so0")
        p.wge(e, "so1")
        p.wge(e, "os")
    return



def emit_l0_ffn(p, io):
    nc = p.nc
    cs = io["consts"]
    blk = p.es.enter_context(nc.Block())
    E, HF = 8, 512

    w1 = [p.sb(f"l0f_w1_{e}", [128, KT * HF], BF16) for e in range(E)]
    w2 = [p.sb(f"l0f_w2_{e}", [128, 4 * D], BF16) for e in range(E)]
    ws1 = p.sb("l0f_ws1", [128, KT * HF], BF16)
    ws2 = p.sb("l0f_ws2", [128, 4 * D], BF16)
    gw = p.sb("l0f_gw", [128, KT * 8], BF16)
    b1c = p.sb("l0f_b1c", [128, E * 4], F32)
    bs1c = p.sb("l0f_bs1c", [128, 4], F32)
    b2m = p.sb("l0f_b2m", [8, D], BF16)
    bs2r = p.sb("l0f_bs2r", [1, D], BF16)
    ones1 = p.sb("l0f_ones1", [1, 128], BF16)
    esel = p.sb("l0f_esel", [8, 8 * 128], BF16)
    rs_bf = p.sb("l0f_rsbf", [1, 1], BF16)
    rs_col = p.sb("l0f_rs", [128, 1], F32)
    idn = p.sb("l0f_idn", [128, 128], BF16)

    STRIP = min(1024, S)
    NSTRIP = T // STRIP
    NTILE = STRIP // 128
    xf = [[p.sb(f"l0f_xf{b}_{k}", [128, STRIP], BF16) for k in range(KT)] for b in range(2)]

    sc_sb = p.sb("l0f_scsb", [8, 128], BF16)
    s_tok = p.sb("l0f_stok", [128, 8], F32)
    m1 = p.sb("l0f_m1", [128, 1], F32)
    m2 = p.sb("l0f_m2", [128, 1], F32)
    msk1 = p.sb("l0f_msk1", [128, 8], F32)
    msk2 = p.sb("l0f_msk2", [128, 8], F32)
    smsk = p.sb("l0f_smsk", [128, 8], F32)
    r12 = p.sb("l0f_r12", [128, 1], F32)
    g_tok = p.sb("l0f_gtok", [128, 8], BF16)
    gT = p.sb("l0f_gT", [8, 128], BF16)
    gb_sb = [p.sb(f"l0f_gb{i}", [128, 128], BF16) for i in range(2)]
    h_sb = [p.sb(f"l0f_hsb{i}", [128, 4 * 128], BF16) for i in range(2)]
    x0_tok = p.sb("l0f_x0tok", [128, D], BF16)
    xpre = p.sb("l0f_xpre", [128, D], F32)
    lnb = (p.sb("l0f_stat", [128, 12], F32), p.sb("l0f_mv", [128, 2], F32),
           p.sb("l0f_mneg", [128, 1], F32), p.sb("l0f_rstd", [128, 1], F32),
           p.sb("l0f_xouttok", [128, D], BF16), p.sb("l0f_eps", [128, 1], F32))

    acc = [p.ps(f"l0f_acc{i}", [128, 512], F32) for i in range(2)]
    hps = [p.ps(f"l0f_hps{i}", [128, 512], F32) for i in range(2)]
    gbp = [p.ps(f"l0f_gbp{i}", [128, 128], F32) for i in range(2)]
    scp = p.ps("l0f_scp", [128, 512], F32)
    trp = p.ps("l0f_trp", [128, 128], BF16)

    for sname in ["ld", "xa0", "xa1", "xo", "os", "ta", "va", "sa", "ga", "so"]:
        p.sem(sname)

    @blk.gpsimd
    def _(e):
        for ee in range(E):
            e.dma_start(w1[ee][:, :].rearrange("p (k h) -> p k h", k=KT),
                        io["W1"][ee, :, :].rearrange("(k p) h -> p k h", p=128)).then_inc(p.inc("ga", 16), 16)
            e.dma_start(w2[ee][:, :].rearrange("p (k d) -> p k d", k=4),
                        io["W2"][ee, :, :].rearrange("(k p) d -> p k d", p=128)).then_inc(p.inc("ga", 16), 16)
        e.dma_start(ws1[:, :].rearrange("p (k h) -> p k h", k=KT),
                    io["Ws1"].rearrange("(k p) h -> p k h", p=128)).then_inc(p.inc("ga", 16), 16)
        e.dma_start(ws2[:, :].rearrange("p (k d) -> p k d", k=4),
                    io["Ws2"].rearrange("(k p) d -> p k d", p=128)).then_inc(p.inc("ga", 16), 16)
        e.dma_start(gw[:, :].rearrange("p (k ee) -> p k ee", k=KT),
                    io["gate_W"].rearrange("(k p) ee -> p k ee", p=128)).then_inc(p.inc("ga", 16), 16)
        e.dma_start(b2m[:, :], io["b2"][:, :]).then_inc(p.inc("ga", 16), 16)
        e.dma_start(bs2r[:1, :], io["bs2"][None, :]).then_inc(p.inc("ga", 16), 16)
        e.dma_start(rs_bf[:1, :], io["route_scale"][None, :]).then_inc(p.inc("ga", 16), 16)

    ga_all = p.c["ga"]

    @blk.sync
    def _(e):
        e.dma_start(ones1[:1, :], cs["ones1"][:, :]).then_inc(p.inc("ld", 16), 16)
        e.dma_start(esel[:, :], cs["esel"][:, :]).then_inc(p.inc("ld", 16), 16)
        e.dma_start(idn[:, :], cs["ident"][:, :]).then_inc(p.inc("ld", 16), 16)
        for ee in range(E):
            for m in range(4):
                e.dma_start(b1c[:, ee * 4 + m:ee * 4 + m + 1],
                            io["b1"][ee, m * 128:(m + 1) * 128].rearrange("(q o) -> q o", o=1)).then_inc(p.inc("ld", 16), 16)
        for m in range(4):
            e.dma_start(bs1c[:, m:m + 1],
                        io["bs1"][m * 128:(m + 1) * 128].rearrange("(q o) -> q o", o=1)).then_inc(p.inc("ld", 16), 16)

    ld_all = p.c["ld"]

    @blk.vector
    def _(e):
        e.memset(lnb[5][:, :], LN_EPS).then_inc(p.inc("va", 1), 1)

    @blk.tensor
    def _(e):
        p.wge(e, "ga", ga_all)
        p.wge(e, "ld", ld_all)
        e.matmul(gbp[0][:, 0:1], ones1[:1, :], rs_bf[:1, :], start=True, stop=True).then_inc(p.inc("ta", 1), 1)

    @blk.vector
    def _(e):
        p.wge(e, "ta")
        e.tensor_copy(rs_col[:, :], gbp[0][:, 0:1]).then_inc(p.inc("va", 1), 1)

    va_init = p.c["va"]

    strip_loaded = {}
    strip_consumed = {}

    sig_done = 0
    gt_evac = 0
    gb_evac = {}
    h_used = {}
    h_evac = {}
    acc_read = 0

    for g in range(T // 128):
        s, ti = divmod(g, NTILE)
        if ti == 0:
            sb_ = s % 2
            @blk.scalar
            def _(e, s=s, sb_=sb_):
                if s >= 2:
                    p.wge(e, "ta", strip_consumed[s - 2])
                for k in range(KT):
                    e.dma_start(xf[sb_][k][:, :],
                                io["x_in"][s * STRIP:(s + 1) * STRIP, k * 128:(k + 1) * 128],
                                transpose=True).then_inc(p.inc(f"xa{s % 2}", 16), 16)
            strip_loaded[s] = (f"xa{s % 2}", p.c[f"xa{s % 2}"])

        xs = xf[s % 2]
        tsl = slice(ti * 128, (ti + 1) * 128)

        @blk.tensor
        def _(e, g=g, s=s, tsl=tsl, sig_done=sig_done):
            if g == 0:
                p.wge(e, "va", va_init)
            p.wge(e, strip_loaded[s][0], strip_loaded[s][1])
            if g >= 1:
                p.wge(e, "sa", sig_done)
            inst = None
            for k in range(KT):
                inst = e.matmul(scp[0:8, 0:128], gw[:, k * 8:(k + 1) * 8], xs[k][:, tsl],
                                start=(k == 0), stop=(k == KT - 1))
            inst.then_inc(p.inc("ta", 1), 1)

        ta_sc = p.c["ta"]

        @blk.vector
        def _(e, ta_sc=ta_sc, g=g, gt_evac=gt_evac):
            p.wge(e, "ta", ta_sc)
            e.tensor_copy(sc_sb[:, :], scp[0:8, 0:128]).then_inc(p.inc("va", 1), 1)

        va_sc = p.c["va"]

        @blk.tensor
        def _(e, va_sc=va_sc, g=g, gt_evac=gt_evac):
            p.wge(e, "va", va_sc)
            if g >= 1:
                p.wge(e, "va", gt_evac)
            e.transpose(trp[:, 0:8], sc_sb[:, :], idn[0:8, 0:8]).then_inc(p.inc("ta", 1), 1)

        ta_tr = p.c["ta"]

        @blk.scalar
        def _(e, ta_tr=ta_tr):
            p.wge(e, "ta", ta_tr)
            e.activation(s_tok[:, :], trp[:, 0:8], AF.Sigmoid).then_inc(p.inc("sa", 1), 1)

        sig_done = p.c["sa"]

        @blk.vector
        def _(e):
            p.wge(e, "sa", sig_done)
            p.bar(e, "va", e.tensor_scalar(s_tok[:, :], s_tok[:, :], rs_col[:, :], None, ALU.mult))
            p.bar(e, "va", e.reduce_max(m1[:, :], s_tok[:, :], axis=mybir.AxisListType.X))
            p.bar(e, "va", e.tensor_scalar(msk1[:, :], s_tok[:, :], m1[:, :], None, ALU.is_ge))
            p.bar(e, "va", e.scalar_tensor_tensor(smsk[:, :], msk1[:, :], -1e30, s_tok[:, :], ALU.mult, ALU.add))
            p.bar(e, "va", e.reduce_max(m2[:, :], smsk[:, :], axis=mybir.AxisListType.X))
            p.bar(e, "va", e.tensor_scalar(msk2[:, :], smsk[:, :], m2[:, :], None, ALU.is_ge))
            p.bar(e, "va", e.tensor_tensor(r12[:, :], m1[:, :], m2[:, :], ALU.add))
            p.bar(e, "va", e.reciprocal(r12[:, :], r12[:, :]))
            p.bar(e, "va", e.tensor_tensor(msk1[:, :], msk1[:, :], msk2[:, :], ALU.add))
            p.bar(e, "va", e.tensor_tensor(msk1[:, :], msk1[:, :], s_tok[:, :], ALU.mult))
            e.tensor_scalar(g_tok[:, :], msk1[:, :], r12[:, :], None, ALU.mult).then_inc(p.inc("va", 1), 1)

        va_g = p.c["va"]

        @blk.tensor
        def _(e, va_g=va_g):
            p.wge(e, "va", va_g)
            e.transpose(trp[0:8, 0:128], g_tok[:, :], idn[:, :]).then_inc(p.inc("ta", 1), 1)

        ta_gt = p.c["ta"]

        @blk.vector
        def _(e, ta_gt=ta_gt):
            p.wge(e, "ta", ta_gt)
            e.tensor_copy(gT[:, :], trp[0:8, 0:128]).then_inc(p.inc("va", 1), 1)

        gt_evac = p.c["va"]

        n_acc_mm = 0
        first_acc = True
        for ei in range(E + 1):
            par = ei % 2
            is_sh = (ei == E)
            wgt1 = ws1 if is_sh else w1[ei]
            wgt2 = ws2 if is_sh else w2[ei]

            @blk.tensor
            def _(e, ei=ei, par=par, tsl=tsl, wgt1=wgt1, g=g):
                key = (par,)
                if (g, ei - 2) in h_used or (g > 0 and ei < 2):
                    pass
                if h_evac.get(par, 0):
                    p.wge(e, "sa", h_evac[par][0])
                    p.wge(e, "va", h_evac[par][1])
                inst = None
                for m in range(4):
                    for k in range(KT):
                        inst = e.matmul(hps[par][:, m * 128:(m + 1) * 128],
                                        wgt1[:, k * HF + m * 128: k * HF + (m + 1) * 128],
                                        xs[k][:, tsl], start=(k == 0), stop=(k == KT - 1))
                inst.then_inc(p.inc("ta", 1), 1)

            ta_h = p.c["ta"]

            if not is_sh:
                gt_now = gt_evac
                @blk.tensor
                def _(e, ei=ei, par=par, gt_now=gt_now):
                    p.wge(e, "va", gt_now)
                    if gb_evac.get(par, 0):
                        p.wge(e, "va", gb_evac[par])
                    e.matmul(gbp[par][:, :], esel[:, ei * 128:(ei + 1) * 128], gT[:, :],
                             start=True, stop=True).then_inc(p.inc("ta", 1), 1)

                ta_gb = p.c["ta"]

                @blk.vector
                def _(e, par=par, ta_gb=ta_gb):
                    p.wge(e, "ta", ta_gb)
                    e.tensor_copy(gb_sb[par][:, :], gbp[par][:, :]).then_inc(p.inc("va", 1), 1)

                gb_evac[par] = p.c["va"]

            @blk.scalar
            def _(e, ei=ei, par=par, ta_h=ta_h, is_sh=is_sh):
                p.wge(e, "ta", ta_h)
                if h_used.get(par, 0):
                    p.wge(e, "ta", h_used[par])
                for m in range(4):
                    bias = bs1c[:, m:m + 1] if is_sh else b1c[:, ei * 4 + m:ei * 4 + m + 1]
                    e.activation(h_sb[par][:, m * 128:(m + 1) * 128],
                                 hps[par][:, m * 128:(m + 1) * 128], AF.Gelu,
                                 bias=bias).then_inc(p.inc("sa", 1), 1)

            sa_h = p.c["sa"]

            if not is_sh:
                @blk.vector
                def _(e, par=par, sa_h=sa_h):
                    p.wge(e, "sa", sa_h)
                    p.wge(e, "va", gb_evac[par])
                    for m in range(4):
                        e.tensor_tensor(h_sb[par][:, m * 128:(m + 1) * 128],
                                        h_sb[par][:, m * 128:(m + 1) * 128],
                                        gb_sb[par][:, :], ALU.mult).then_inc(p.inc("va", 1), 1)
                h_ready = ("va", p.c["va"])
            else:
                h_ready = ("sa", sa_h)

            h_evac[par] = (sa_h, p.c["va"])

            @blk.tensor
            def _(e, ei=ei, par=par, h_ready=h_ready, first_acc=first_acc, g=g, acc_read=acc_read):
                p.wge(e, h_ready[0], h_ready[1])
                if first_acc and g >= 1:
                    p.wge(e, "va", acc_read)
                inst = None
                for n in range(2):
                    for k2 in range(4):
                        inst = e.matmul(acc[n][:, :], h_sb[par][:, k2 * 128:(k2 + 1) * 128],
                                        wgt2[:, k2 * D + n * 512: k2 * D + n * 512 + 512],
                                        start=(first_acc and k2 == 0),
                                        stop=False)
                inst.then_inc(p.inc("ta", 1), 1)

            h_used[par] = p.c["ta"]
            first_acc = False

        @blk.tensor
        def _(e, tsl=tsl, gt_now=gt_evac):
            p.wge(e, "va", gt_now)
            inst = None
            for n in range(2):
                e.matmul(acc[n][:, :], gT[:, :], b2m[:, n * 512:(n + 1) * 512], start=False, stop=False)
                inst = e.matmul(acc[n][:, :], ones1[:1, :], bs2r[:1, n * 512:(n + 1) * 512],
                                start=False, stop=True)
            inst.then_inc(p.inc("ta", 1), 1)

        ta_acc = p.c["ta"]
        if ti == NTILE - 1:
            strip_consumed[s] = p.c["ta"]

        @blk.sync
        def _(e, g=g, acc_read=acc_read):
            if acc_read:
                p.wge(e, "va", acc_read)
            e.dma_start(x0_tok[:, :], io["x_in"][g * 128:(g + 1) * 128, :]).then_inc(p.inc("xo", 16), 16)

        ld_x0 = p.c["xo"]

        @blk.vector
        def _(e, ta_acc=ta_acc, ld_x0=ld_x0):
            p.wge(e, "ta", ta_acc)
            p.wge(e, "xo", ld_x0)
            p.wge(e, "os")
            e.tensor_tensor(xpre[:, 0:512], x0_tok[:, 0:512], acc[0][:, :], ALU.add).then_inc(p.inc("va", 1), 1)
            e.tensor_tensor(xpre[:, 512:1024], x0_tok[:, 512:1024], acc[1][:, :], ALU.add).then_inc(p.inc("va", 1), 1)

        acc_read = p.c["va"]
        emit_ln_store(p, blk, lnb, xpre, io["x_out"], g * 128)

    @blk.sync
    def _(e):
        p.wge(e, "os")
    return



def emit_l1_attn(p, io):
    nc = p.nc
    cs = io["consts"]
    blk = p.es.enter_context(nc.Block())
    C = 128
    TPB = S // C

    wk = [p.sb(f"l1a_wk{k}", [128, D], BF16) for k in range(KT)]
    wv = [p.sb(f"l1a_wv{k}", [128, D], BF16) for k in range(KT)]
    wr = [p.sb(f"l1a_wr{k}", [128, D], BF16) for k in range(KT)]
    wo = [p.sb(f"l1a_wo{k}", [128, D], BF16) for k in range(KT)]
    ones1 = p.sb("l1a_ones1", [1, 128], BF16)
    ones1f = p.sb("l1a_ones1f", [1, 128], F32)
    idn = p.sb("l1a_idn", [128, 128], BF16)
    idnf = p.sb("l1a_idnf", [128, 128], F32)
    dgrid = p.sb("l1a_dgrid", [128, 128], F32)
    mgrid = p.sb("l1a_mgrid", [128, 128], F32)
    ramp = p.sb("l1a_ramp", [1, 130], F32)
    rrcol = p.sb("l1a_rrcol", [128, 1], F32)

    STRIP = min(2048, S)
    NSTRIP = T // STRIP
    NTILE = STRIP // 128
    xf = [[p.sb(f"l1a_xf{b}_{k}", [128, STRIP], BF16) for k in range(KT)] for b in range(2)]

    lamv = p.sb("l1a_lamv", [16, 1], F32)
    lnl = p.sb("l1a_lnl", [16, 1], F32)
    lrow = p.sb("l1a_lrow", [1, 16], F32)
    lc = p.sb("l1a_lc", [128, 1], F32)
    rampl = p.sb("l1a_rampl", [1, 128], F32)
    lamtile = [p.sb(f"l1a_lt{t}", [128, 128], BF16) for t in range(KT)]
    ptile = [p.sb(f"l1a_pt{h}", [128, 128], BF16) for h in range(H)]
    lamrev = p.sb("l1a_lamrev", [128, H], F32)
    lam128 = p.sb("l1a_lam128", [128, H], F32)
    ucol = p.sb("l1a_ucol", [128, KT], F32)
    tmpP = p.sb("l1a_tmpP", [128, 128], F32)
    tcol = p.sb("l1a_tcol", [128, 1], F32)

    k_tok = p.sb("l1a_ktok", [128, D], BF16)
    v_tok = p.sb("l1a_vtok", [128, D], BF16)
    r_tok = p.sb("l1a_rtok", [128, D], BF16)
    kf = p.sb("l1a_kf", [128, KT * 128], BF16)
    rf = p.sb("l1a_rf", [128, KT * 128], BF16)
    khat = p.sb("l1a_khat", [128, 128], BF16)
    rtl = p.sb("l1a_rtl", [128, 128], BF16)
    abuf = [p.sb(f"l1a_ab{i}", [128, 128], BF16) for i in range(2)]
    att = p.sb("l1a_att", [128, KT * 128], BF16)
    s_f32 = p.sb("l1a_sf32", [128, 8 * 64], F32)
    s_bf = p.sb("l1a_sbf", [128, 8 * 64], BF16)
    x0_tok = p.sb("l1a_x0tok", [128, D], BF16)
    xpre = p.sb("l1a_xpre", [128, D], F32)
    lnb = (p.sb("l1a_stat", [128, 12], F32), p.sb("l1a_mv", [128, 2], F32),
           p.sb("l1a_mneg", [128, 1], F32), p.sb("l1a_rstd", [128, 1], F32),
           p.sb("l1a_xouttok", [128, D], BF16), p.sb("l1a_eps", [128, 1], F32))

    bank = [p.ps(f"l1a_bank{i}", [128, 512], F32) for i in range(6)]
    trp = p.ps("l1a_trp", [128, 128], BF16)

    for sname in ["ld", "xa0", "xa1", "xo", "os", "ta", "va", "sa", "ga"]:
        p.sem(sname)

    @blk.gpsimd
    def _(e):
        for wnm, wsc in [("Wk", "wksc"), ("Wv", "wvsc"), ("Wr", "wrsc"), ("Wo", "wosc")]:
            e.dma_start(io[wsc][:, :], io[wnm][:, :]).then_inc(p.inc("ga", 16), 16)

    ga_all = p.c["ga"]

    @blk.sync
    def _(e):
        e.dma_start(ones1[:1, :], cs["ones1"][:, :]).then_inc(p.inc("ld", 16), 16)
        e.dma_start(ones1f[:1, :], cs["ones1f"][:, :]).then_inc(p.inc("ld", 16), 16)
        e.dma_start(idn[:, :], cs["ident"][:, :]).then_inc(p.inc("ld", 16), 16)
        e.dma_start(idnf[:, :], cs["identf"][:, :]).then_inc(p.inc("ld", 16), 16)
        e.dma_start(dgrid[:, :], cs["dgrid"][:, :]).then_inc(p.inc("ld", 16), 16)
        e.dma_start(mgrid[:, :], cs["mgridf"][:, :]).then_inc(p.inc("ld", 16), 16)
        e.dma_start(ramp[:1, :], cs["ramp"][:, :]).then_inc(p.inc("ld", 16), 16)
        e.dma_start(rrcol[:, :], cs["rrcol"][:, :]).then_inc(p.inc("ld", 16), 16)
        e.dma_start(lamv[:, :], io["time_decay"][:, 0:1], allow_slow_non_contiguous=True).then_inc(p.inc("ld", 16), 16)
        for t in range(KT):
            e.dma_start(ucol[:, t:t + 1],
                        io["time_first"][2 * t:2 * t + 2, :].rearrange("a (b o) -> (a b) o", o=1)).then_inc(p.inc("ld", 16), 16)

    @blk.scalar
    def _(e):
        p.wge(e, "ga", ga_all)
        for k in range(KT):
            for wt, wsc in [(wk, "wksc"), (wv, "wvsc"), (wr, "wrsc"), (wo, "wosc")]:
                e.dma_start(wt[k][:, :], io[wsc][:, k * 128:(k + 1) * 128],
                            transpose=True).then_inc(p.inc("ld", 16), 16)

    ld_all = p.c["ld"]

    @blk.vector
    def _(e):
        e.memset(lnb[5][:, :], LN_EPS).then_inc(p.inc("va", 1), 1)

    @blk.scalar
    def _(e):
        p.wge(e, "ld", ld_all)
        e.activation(lnl[:, :], lamv[0:16, :], AF.Ln).then_inc(p.inc("sa", 1), 1)

    @blk.tensor
    def _(e):
        p.wge(e, "sa")
        e.transpose(bank[4][0:1, 0:16], lnl[0:16, :], idnf[0:16, 0:16]).then_inc(p.inc("ta", 1), 1)

    @blk.vector
    def _(e):
        p.wge(e, "ta")
        e.tensor_copy(lrow[:1, :], bank[4][0:1, 0:16]).then_inc(p.inc("va", 1), 1)

    for t in range(KT):
        for hi in range(2):
            h = 2 * t + hi
            @blk.vector
            def _(e, h=h):
                p.wge(e, "va")
                p.wge(e, "ta")
                e.tensor_scalar(rampl[:1, :], ramp[0:1, 1:129], lrow[0:1, h:h + 1],
                                None, ALU.mult).then_inc(p.inc("va", 1), 1)

            @blk.tensor
            def _(e, h=h, hi=hi):
                p.wge(e, "va")
                if not (h == 0):
                    p.wge(e, "sa")
                e.matmul(bank[4][hi * 64:(hi + 1) * 64, 0:128], ones1f[:1, 0:64], rampl[:1, :],
                         start=True, stop=True).then_inc(p.inc("ta", 1), 1)

        @blk.scalar
        def _(e, t=t):
            p.wge(e, "ta")
            e.activation(lamtile[t][:, :], bank[4][:, 0:128], AF.Exp).then_inc(p.inc("sa", 1), 1)

    for h in range(H):
        @blk.tensor
        def _(e, h=h):
            p.wge(e, "sa")
            p.wge(e, "va")
            e.matmul(bank[4][:, 128:129], ones1f[:1, :], lrow[0:1, h:h + 1],
                     start=True, stop=True).then_inc(p.inc("ta", 1), 1)

        @blk.vector
        def _(e, h=h):
            p.wge(e, "ta")
            e.tensor_copy(lc[:, :], bank[4][:, 128:129]).then_inc(p.inc("va", 1), 1)

        @blk.vector
        def _(e, h=h):
            p.wge(e, "va")
            p.bar(e, "va", e.tensor_scalar(tmpP[:, :], dgrid[:, :], lc[:, :], None, ALU.mult))
            p.bar(e, "va", e.tensor_scalar(tcol[:, :], rrcol[:, :], lc[:, :], None, ALU.mult))
            e.tensor_scalar(lam128[:, h:h + 1], lc[:, :], 128.0, None, ALU.mult).then_inc(p.inc("va", 1), 1)

        @blk.scalar
        def _(e, h=h):
            p.wge(e, "va")
            p.bar(e, "sa", e.activation(tmpP[:, :], tmpP[:, :], AF.Exp))
            p.bar(e, "sa", e.activation(lamrev[:, h:h + 1], tcol[:, :], AF.Exp))
            e.activation(lam128[:, h:h + 1], lam128[:, h:h + 1], AF.Exp).then_inc(p.inc("sa", 1), 1)

        @blk.vector
        def _(e, h=h):
            p.wge(e, "sa")
            e.tensor_tensor(ptile[h][:, :], tmpP[:, :], mgrid[:, :], ALU.mult).then_inc(p.inc("va", 1), 1)

    va_setup = p.c["va"]
    sa_setup = p.c["sa"]

    strip_loaded = {}
    strip_consumed = {}
    proj_evac_for = {}
    proj_evac = 0
    amask_done = {}
    abuf_used = {}
    khat_used = {}
    tm_state_prev = 0
    y_free = {}
    trp_free = ("va", 0)
    rtl_used = 0
    tok_used_ta = 0
    tok_used_va = 0
    kf_used = 0
    att_used = 0
    wo_free = 0

    for b in range(BPC):
        @blk.vector
        def _(e, b=b):
            p.wge(e, "ta")
            e.memset(s_f32[:, :], 0.0).then_inc(p.inc("va", 1), 1)
            e.memset(s_bf[:, :], 0.0).then_inc(p.inc("va", 1), 1)

        va_zero = p.c["va"]

        for cc in range(TPB):
            g = b * TPB + cc
            s_, ti = divmod(g, NTILE)
            if ti == 0:
                sb_ = s_ % 2
                @blk.scalar
                def _(e, s_=s_, sb_=sb_):
                    if s_ >= 2:
                        p.wge(e, "ta", strip_consumed[s_ - 2])
                    for k in range(KT):
                        e.dma_start(xf[sb_][k][:, :],
                                    io["x_in"][s_ * STRIP:(s_ + 1) * STRIP, k * 128:(k + 1) * 128],
                                    transpose=True).then_inc(p.inc(f"xa{s_ % 2}", 16), 16)
                strip_loaded[s_] = (f"xa{s_ % 2}", p.c[f"xa{s_ % 2}"])

            xs = xf[s_ % 2]
            tsl = slice(ti * 128, (ti + 1) * 128)

            for wi, (wt, dst) in enumerate([(wk, k_tok), (wv, v_tok), (wr, r_tok)]):
                for n in range(2):
                    bi = (wi * 2 + n) % 2
                    @blk.tensor
                    def _(e, g=g, s_=s_, tsl=tsl, wt=wt, n=n, bi=bi, wi=wi, va_zero=va_zero):
                        if g == 0 and wi == 0 and n == 0:
                            p.wge(e, "va", va_setup)
                            p.wge(e, "sa", sa_setup)
                        if wi == 0 and n == 0:
                            p.wge(e, "va", va_zero)
                            p.wge(e, strip_loaded[s_][0], strip_loaded[s_][1])
                        p.wge(e, "sa", proj_evac_for.get(bi, 0))
                        p.wge(e, "va", amask_done.get(bi, 0))
                        inst = None
                        for k in range(KT):
                            inst = e.matmul(bank[bi][:, :], xs[k][:, tsl],
                                            wt[k][:, n * 512:(n + 1) * 512],
                                            start=(k == 0), stop=(k == KT - 1))
                        inst.then_inc(p.inc("ta", 1), 1)

                    ta_pr = p.c["ta"]

                    @blk.scalar
                    def _(e, dst=dst, n=n, bi=bi, ta_pr=ta_pr, tu=(tok_used_ta, tok_used_va)):
                        p.wge(e, "ta", ta_pr)
                        p.wge(e, "ta", tu[0])
                        p.wge(e, "va", tu[1])
                        e.activation(dst[:, n * 512:(n + 1) * 512], bank[bi][:, :],
                                     AF.Copy).then_inc(p.inc("sa", 1), 1)

                    proj_evac_for[bi] = p.c["sa"]
            proj_evac = p.c["sa"]

            for t in range(KT):
                @blk.tensor
                def _(e, t=t, tf=trp_free, pe=proj_evac, ku=kf_used):
                    p.wge(e, "sa", pe)
                    p.wge(e, tf[0], tf[1])
                    e.transpose(trp[:, :], k_tok[:, t * 128:(t + 1) * 128], idn[:, :]).then_inc(p.inc("ta", 1), 1)

                ta_t1 = p.c["ta"]

                @blk.vector
                def _(e, t=t, ta_t1=ta_t1, ku=kf_used):
                    p.wge(e, "ta", ta_t1)
                    p.wge(e, "ta", ku)
                    e.tensor_copy(kf[:, t * 128:(t + 1) * 128], trp[:, :]).then_inc(p.inc("va", 1), 1)

                trp_free = ("va", p.c["va"])

                @blk.tensor
                def _(e, t=t, tf=trp_free):
                    p.wge(e, tf[0], tf[1])
                    e.transpose(trp[:, :], r_tok[:, t * 128:(t + 1) * 128], idn[:, :]).then_inc(p.inc("ta", 1), 1)

                ta_t2 = p.c["ta"]

                @blk.scalar
                def _(e, t=t, ta_t2=ta_t2):
                    p.wge(e, "ta", ta_t2)
                    p.wge(e, "va")
                    e.activation(tmpP[:, :], trp[:, :], AF.Sigmoid).then_inc(p.inc("sa", 1), 1)

                trp_free = ("sa", p.c["sa"])
                sa_sig = p.c["sa"]

                @blk.vector
                def _(e, t=t, sa_sig=sa_sig, ku=kf_used):
                    p.wge(e, "sa", sa_sig)
                    p.wge(e, "ta", ku)
                    e.tensor_scalar(rf[:, t * 128:(t + 1) * 128], tmpP[:, :], ucol[:, t:t + 1],
                                    None, ALU.mult).then_inc(p.inc("va", 1), 1)

            va_fm = p.c["va"]

            for pp in range(KT):
                @blk.vector
                def _(e, pp=pp, va_fm=va_fm, ru=rtl_used):
                    p.wge(e, "va", va_fm)
                    p.wge(e, "ta", ru)
                    e.tensor_tensor(rtl[:, :], rf[:, pp * 128:(pp + 1) * 128],
                                    lamtile[pp][:, :], ALU.mult).then_inc(p.inc("va", 1), 1)

                va_rtl = p.c["va"]
                ybank = bank[2 + pp % 2]

                for hi in range(2):
                    h = 2 * pp + hi
                    abi = h % 2

                    @blk.tensor
                    def _(e, pp=pp, hi=hi, h=h, abi=abi, va_fm=va_fm):
                        p.wge(e, "va", va_fm)
                        p.wge(e, "va", amask_done.get(abi, 0))
                        e.matmul(bank[abi][:, 0:128], kf[hi * 64:(hi + 1) * 64, pp * 128:(pp + 1) * 128],
                                 rf[hi * 64:(hi + 1) * 64, pp * 128:(pp + 1) * 128],
                                 start=True, stop=True).then_inc(p.inc("ta", 1), 1)

                    ta_ar = p.c["ta"]

                    @blk.vector
                    def _(e, h=h, abi=abi, ta_ar=ta_ar, au=abuf_used.get(abi, 0)):
                        p.wge(e, "ta", ta_ar)
                        p.wge(e, "ta", au)
                        e.tensor_tensor(abuf[abi][:, :], bank[abi][:, 0:128], ptile[h][:, :],
                                        ALU.mult).then_inc(p.inc("va", 1), 1)

                    amask_done[abi] = p.c["va"]
                    va_am = p.c["va"]

                    @blk.vector
                    def _(e, h=h, hi=hi, pe=proj_evac, ku=khat_used.get(hi, 0)):
                        p.wge(e, "sa", pe)
                        p.wge(e, "ta", ku)
                        e.tensor_scalar(khat[:, hi * 64:(hi + 1) * 64], k_tok[:, h * 64:(h + 1) * 64],
                                        lamrev[:, h:h + 1], None, ALU.mult).then_inc(p.inc("va", 1), 1)

                    va_kh = p.c["va"]

                    @blk.tensor
                    def _(e, pp=pp, hi=hi, h=h, abi=abi, va_am=va_am, va_rtl=va_rtl, va_kh=va_kh,
                          yf=y_free.get(pp % 2, 0), tf=tm_state_prev):
                        p.wge(e, "va", va_am)
                        p.wge(e, "va", va_rtl)
                        p.wge(e, "va", va_kh)
                        p.wge(e, "sa", yf)
                        p.wge(e, "va", tf)
                        e.matmul(ybank[hi * 64:(hi + 1) * 64, 0:128],
                                 s_bf[hi * 64:(hi + 1) * 64, pp * 64:(pp + 1) * 64],
                                 rtl[hi * 64:(hi + 1) * 64, :], start=True, stop=False)
                        e.matmul(ybank[hi * 64:(hi + 1) * 64, 0:128],
                                 v_tok[:, h * 64:(h + 1) * 64], abuf[abi][:, :],
                                 start=False, stop=True).then_inc(p.inc("ta", 1), 1)
                        e.matmul(bank[4][hi * 64:(hi + 1) * 64, (pp % 2) * 64:(pp % 2) * 64 + 64],
                                 khat[:, hi * 64:(hi + 1) * 64], v_tok[:, h * 64:(h + 1) * 64],
                                 start=True, stop=True, skip_group_check=True).then_inc(p.inc("ta", 1), 1)

                    abuf_used[abi] = p.c["ta"]
                    rtl_used = p.c["ta"]
                    khat_used[hi] = p.c["ta"]
                    ta_tm = p.c["ta"]

                    @blk.vector
                    def _(e, pp=pp, hi=hi, h=h, ta_tm=ta_tm):
                        p.wge(e, "ta", ta_tm)
                        r0, r1 = hi * 64, (hi + 1) * 64
                        c0, c1 = pp * 64, (pp + 1) * 64
                        p.bar(e, "va", e.scalar_tensor_tensor(
                            s_f32[r0:r1, c0:c1], s_f32[r0:r1, c0:c1], lam128[r0:r1, h:h + 1],
                            bank[4][r0:r1, (pp % 2) * 64:(pp % 2) * 64 + 64], ALU.mult, ALU.add))
                        e.tensor_copy(s_bf[r0:r1, c0:c1], s_f32[r0:r1, c0:c1]).then_inc(p.inc("va", 1), 1)

                    tm_state_prev = p.c["va"]

                @blk.scalar
                def _(e, pp=pp, au=att_used):
                    p.wge(e, "ta")
                    p.wge(e, "ta", au)
                    e.activation(att[:, pp * 128:(pp + 1) * 128], ybank[:, 0:128],
                                 AF.Copy).then_inc(p.inc("sa", 1), 1)

                y_free[pp % 2] = p.c["sa"]

            sa_att = p.c["sa"]

            @blk.sync
            def _(e, g=g, wf=wo_free):
                if wf:
                    p.wge(e, "va", wf)
                e.dma_start(x0_tok[:, :], io["x_in"][g * 128:(g + 1) * 128, :]).then_inc(p.inc("xo", 16), 16)

            xo_now = p.c["xo"]

            for n in range(2):
                @blk.tensor
                def _(e, n=n, sa_att=sa_att, wf=wo_free, vnow=p.c["va"]):
                    p.wge(e, "sa", sa_att)
                    if n == 0:
                        p.wge(e, "va", wf)
                    else:
                        p.wge(e, "va", vnow)
                    inst = None
                    for k in range(KT):
                        inst = e.matmul(bank[5][:, :], att[:, k * 128:(k + 1) * 128],
                                        wo[k][:, n * 512:(n + 1) * 512],
                                        start=(k == 0), stop=(k == KT - 1))
                    inst.then_inc(p.inc("ta", 1), 1)

                ta_wo = p.c["ta"]

                @blk.vector
                def _(e, n=n, ta_wo=ta_wo, xo_now=xo_now):
                    p.wge(e, "ta", ta_wo)
                    p.wge(e, "xo", xo_now)
                    p.wge(e, "os")
                    e.tensor_tensor(xpre[:, n * 512:(n + 1) * 512], x0_tok[:, n * 512:(n + 1) * 512],
                                    bank[5][:, :], ALU.add).then_inc(p.inc("va", 1), 1)

            wo_free = p.c["va"]
            tok_used_ta = p.c["ta"]
            tok_used_va = p.c["va"]
            kf_used = p.c["ta"]
            att_used = p.c["ta"]
            if ti == NTILE - 1:
                strip_consumed[s_] = p.c["ta"]
            emit_ln_store(p, blk, lnb, xpre, io["x_out"], g * 128)

    @blk.sync
    def _(e):
        p.wge(e, "os")
    return



def emit_l1_ffn(p, io):
    nc = p.nc
    cs = io["consts"]
    G3 = 3072
    NB = S // 8
    h0tok = p.sb("l1f_h0tok", [2, D], F32)

    for sname in ["ld", "xa0", "xa1", "xo0", "xo1", "xc", "xm", "ta", "va", "sa", "ga", "so0", "so1", "os",
                  "gq0", "gq1", "gh", "gt", "gv", "gs"]:
        p.sem(sname)

    esA = contextlib.ExitStack()
    with esA:
        sbA = lambda n, shp, dt=F32: esA.enter_context(nc.sbuf_tensor(n, shp, dt))
        psA = lambda n, shp, dt=F32: esA.enter_context(nc.psum_tensor(n, shp, dt))
        blk = esA.enter_context(nc.Block())
        wih = [sbA(f"l1f_wih{k}", [128, G3], BF16) for k in range(KT)]
        ctxw = [sbA(f"l1f_ctxw{k}", [128, D], BF16) for k in range(KT)]
        STRIP = min(2048, S)
        NSTRIP = T // STRIP
        NTILE = STRIP // 128
        xf = [[sbA(f"l1f_xfA{bb}_{k}", [128, STRIP], BF16) for k in range(KT)] for bb in range(2)]
        xw_tok = [sbA(f"l1f_xwtok{i}", [128, G3], BF16) for i in range(2)]
        x0A = [sbA(f"l1f_x0A{i}", [128, D], BF16) for i in range(2)]
        bsel = sbA("l1f_bsel", [128, 4], BF16)
        hm = sbA("l1f_hm", [2, D], BF16)
        hmT = sbA("l1f_hmT", [128, KT * 2], BF16)
        ctxbrow = sbA("l1f_ctxbrow", [1, D], BF16)
        onesA = sbA("l1f_onesA", [1, 128], BF16)
        idnA = sbA("l1f_idnA", [128, 128], BF16)
        bank = [psA(f"l1f_bankA{i}", [128, 512], F32) for i in range(4)]
        pmean = psA("l1f_pmean", [2, D], F32)
        ptrA = psA("l1f_ptrA", [128, 2], BF16)

        @blk.gpsimd
        def _(e):
            e.dma_start(io["wihsc"][:, :], io["gru_w_ih"][:, :]).then_inc(p.inc("ga", 16), 16)
            e.dma_start(io["whhsc"][:, :], io["gru_w_hh"][:, :]).then_inc(p.inc("ga", 16), 16)
            e.dma_start(io["ctxsc"][:, :], io["ctx_w"][:, :]).then_inc(p.inc("ga", 16), 16)
            e.dma_start(ctxbrow[:1, :], io["ctx_b"][None, :]).then_inc(p.inc("ga", 16), 16)

        ga_all = p.c["ga"]

        @blk.sync
        def _(e):
            e.dma_start(onesA[:1, :], cs["ones1"][:, :]).then_inc(p.inc("ld", 16), 16)
            e.dma_start(idnA[:, :], cs["ident"][:, :]).then_inc(p.inc("ld", 16), 16)
            e.dma_start(bsel[:, :], cs["bsel"][:, :]).then_inc(p.inc("ld", 16), 16)

        @blk.scalar
        def _(e):
            p.wge(e, "ga", ga_all)
            for k in range(KT):
                e.dma_start(wih[k][:, :], io["wihsc"][:, k * 128:(k + 1) * 128],
                            transpose=True).then_inc(p.inc("ld", 16), 16)
                e.dma_start(ctxw[k][:, :], io["ctxsc"][:, k * 128:(k + 1) * 128],
                            transpose=True).then_inc(p.inc("ld", 16), 16)

        ld_all = p.c["ld"]

        strip_loaded = {}
        strip_consumed = {}
        xw_stored = {}
        xw_evac = {}
        x0_used = {}
        for g in range(T // 128):
            s_, ti = divmod(g, NTILE)
            b = g // (S // 128)
            par = g % 2
            if ti == 0:
                sb_ = s_ % 2
                @blk.scalar
                def _(e, s_=s_, sb_=sb_):
                    if s_ >= 2:
                        p.wge(e, "ta", strip_consumed[s_ - 2])
                    for k in range(KT):
                        e.dma_start(xf[sb_][k][:, :],
                                    io["x_in"][s_ * STRIP:(s_ + 1) * STRIP, k * 128:(k + 1) * 128],
                                    transpose=True).then_inc(p.inc(f"xa{s_ % 2}", 16), 16)
                strip_loaded[s_] = (f"xa{s_ % 2}", p.c[f"xa{s_ % 2}"])

            xs = xf[s_ % 2]
            tsl = slice(ti * 128, (ti + 1) * 128)

            @blk.sync
            def _(e, g=g, par=par):
                if g >= 2:
                    p.wge(e, "ta", x0_used.get(g - 2, 0))
                e.dma_start(x0A[par][:, :], io["x_in"][g * 128:(g + 1) * 128, :]).then_inc(p.inc(f"xo{par}", 16), 16)

            xo_now = (f"xo{par}", p.c[f"xo{par}"])

            for n6 in range(6):
                bi = n6 % 3
                @blk.tensor
                def _(e, g=g, s_=s_, tsl=tsl, n6=n6, bi=bi):
                    if g == 0 and n6 == 0:
                        p.wge(e, "ld", ld_all)
                    if n6 == 0:
                        p.wge(e, strip_loaded[s_][0], strip_loaded[s_][1])
                    p.wge(e, "sa", xw_evac.get(bi, 0))
                    inst = None
                    for k in range(KT):
                        inst = e.matmul(bank[bi][:, :], xs[k][:, tsl],
                                        wih[k][:, n6 * 512:(n6 + 1) * 512],
                                        start=(k == 0), stop=(k == KT - 1))
                    inst.then_inc(p.inc("ta", 1), 1)

                ta_xw = p.c["ta"]

                @blk.scalar
                def _(e, g=g, n6=n6, bi=bi, par=par, ta_xw=ta_xw):
                    p.wge(e, "ta", ta_xw)
                    if g >= 2 and n6 == 0:
                        p.wge(e, xw_stored[g - 2][0], xw_stored[g - 2][1])
                    e.activation(xw_tok[par][:, n6 * 512:(n6 + 1) * 512], bank[bi][:, :],
                                 AF.Copy).then_inc(p.inc("sa", 1), 1)

                xw_evac[bi] = p.c["sa"]

            sa_xwtok = p.c["sa"]

            @blk.tensor
            def _(e, g=g, b=b, par=par, xo_now=xo_now, first=(g == 0), last=(g == T // 128 - 1)):
                p.wge(e, xo_now[0], xo_now[1])
                inst = None
                for n in range(2):
                    inst = e.matmul(pmean[:, n * 512:(n + 1) * 512], bsel[:, 2 * b:2 * b + 2],
                                    x0A[par][:, n * 512:(n + 1) * 512],
                                    start=first, stop=last, skip_group_check=True)
                inst.then_inc(p.inc("ta", 1), 1)

            x0_used[g] = p.c["ta"]

            @blk.sync
            def _(e, g=g, b=b, par=par, sa_xwtok=sa_xwtok):
                p.wge(e, "sa", sa_xwtok)
                s0 = (g % (S // 128)) * 128
                e.dma_start(io["xwsc"].rearrange("(s bb) c -> s bb c", bb=BPC)[s0:s0 + 128, b, :],
                            xw_tok[par][:, :]).then_inc(p.inc(f"so{par}", 16), 16)

            xw_stored[g] = (f"so{par}", p.c[f"so{par}"])
            if ti == NTILE - 1:
                strip_consumed[s_] = p.c["ta"]

        @blk.vector
        def _(e):
            p.wge(e, "ta")
            e.tensor_scalar(hm[:, :], pmean[:, :], 1.0 / S, None, ALU.mult).then_inc(p.inc("va", 1), 1)

        for k in range(KT):
            @blk.tensor
            def _(e, k=k):
                p.wge(e, "va")
                e.transpose(ptrA[:, :], hm[:, k * 128:(k + 1) * 128], idnA[0:2, 0:2]).then_inc(p.inc("ta", 1), 1)

            @blk.vector
            def _(e, k=k):
                p.wge(e, "ta")
                e.tensor_copy(hmT[:, k * 2:(k + 1) * 2], ptrA[:, :]).then_inc(p.inc("va", 1), 1)

        for n in range(2):
            @blk.tensor
            def _(e, n=n):
                p.wge(e, "va")
                for k in range(KT):
                    e.matmul(bank[3][0:2, :], hmT[:, k * 2:(k + 1) * 2],
                             ctxw[k][:, n * 512:(n + 1) * 512],
                             start=(k == 0), stop=False, skip_group_check=True)
                e.matmul(bank[3][0:2, :], onesA[:1, 0:2],
                         ctxbrow[:1, n * 512:(n + 1) * 512],
                         start=False, stop=True, skip_group_check=True).then_inc(p.inc("ta", 1), 1)

            @blk.vector
            def _(e, n=n):
                p.wge(e, "ta")
                e.tensor_copy(h0tok[:, n * 512:(n + 1) * 512], bank[3][0:2, :]).then_inc(p.inc("va", 1), 1)

    emit_l1_ffn_scan(p, io, h0tok)
    emit_l1_ffn_tail(p, io)
    return


def emit_l1_ffn_scan(p, io, h0tok):
    nc = p.nc
    cs = io["consts"]
    G3 = 3072
    NB = S // 8
    es = contextlib.ExitStack()
    with es:
        sb = lambda n, shp, dt=F32: es.enter_context(nc.sbuf_tensor(n, shp, dt))
        ps = lambda n, shp, dt=F32: es.enter_context(nc.psum_tensor(n, shp, dt))
        blk = es.enter_context(nc.Block())
        whh = [sb(f"l1f_whh{k}", [128, G3], BF16) for k in range(KT)]
        xwt = sb("l1f_xwt", [2, 16 * G3], BF16)
        hr = sb("l1f_hr", [2, 8 * D], BF16)
        hT = sb("l1f_hT", [128, 16], BF16)
        htok = sb("l1f_htok", [2, 2 * D], F32)
        rz = sb("l1f_rz", [2, 2048], F32)
        rz2 = sb("l1f_rz2", [2, 1024], F32)
        t3b = sb("l1f_t3", [2, D], F32)
        t4b = sb("l1f_t4", [2, D], F32)
        nnb = sb("l1f_nn", [2, D], F32)
        ddb = sb("l1f_dd", [2, D], F32)
        mmb = sb("l1f_mm", [2, D], F32)
        idnB = sb("l1f_idnB", [128, 128], BF16)
        h0bf = sb("l1f_h0bf", [2, D], BF16)
        phw = [ps(f"l1f_phw{i}", [2, 512], F32) for i in range(6)]
        ptrBig = ps("l1f_ptrBig", [128, 16], BF16)

        GVB, GTB, GSB, GHB = 512, 256, 256, 0
        NIT2 = NB // 2

        @blk.sync
        def _(e):
            e.dma_start(idnB[:, :], cs["ident"][:, :]).then_inc(p.inc("ld", 16), 16)

        @blk.scalar
        def _(e):
            p.wge(e, "ga")
            for k in range(KT):
                e.dma_start(whh[k][:, :], io["whhsc"][:, k * 128:(k + 1) * 128],
                            transpose=True).then_inc(p.inc("ld", 16), 16)

        ld_all = p.c["ld"]

        @blk.vector
        def _(e):
            p.wge(e, "va")
            p.bar(e, "gv", e.tensor_copy(htok[:, D:2 * D], h0tok[:, :]))
            e.tensor_copy(h0bf[:, :], h0tok[:, :]).then_inc(p.inc("gv", 1), 1)

        @blk.tensor
        def _(e):
            p.wge(e, "ld", ld_all)
            p.wge(e, "gv")
            for k in range(KT):
                e.transpose(ptrBig[:, 2 * k:2 * k + 2], h0bf[:, k * 128:(k + 1) * 128],
                            idnB[0:2, 0:2]).then_inc(p.inc("gt", 1), 1)

        @blk.vector
        def _(e):
            p.wge(e, "gt")
            e.tensor_copy(hT[:, :], ptrBig[:, :]).then_inc(p.inc("gv", 1), 1)

        @blk.vector
        def _(e):
            p.wge(e, "gv")
            e.sem_inc(p.sems["gv"], GVB - p.c["gv"])
        p.c["gv"] = GVB

        @blk.scalar
        def _(e):
            p.wge(e, "gv", GVB)
            p.wge(e, "gs")
            e.sem_inc(p.sems["gs"], GSB - p.c["gs"])
        p.c["gs"] = GSB

        @blk.tensor
        def _(e):
            p.wge(e, "gt")
            e.sem_inc(p.sems["gt"], GTB - p.c["gt"])
        p.c["gt"] = GTB



        xwsc_h = io["xwsc_h"]
        hsc_h = io["hsc_h"]

        @blk.gpsimd
        def _(e):
            p.wge(e, "so0")
            p.wge(e, "so1")
            r1 = e.alloc_register("xwoff")
            r2 = e.alloc_register("hoff")
            for kb in range(2):
                e.dma_start(bass.AP(xwt, kb * 8 * G3, [[16 * G3, 2], [G3, 8], [1, G3]]),
                            bass.AP(xwsc_h, kb * 8 * 2 * G3, [[G3, 2], [2 * G3, 8], [1, G3]])).then_inc(p.inc(f"gq{kb}", 16), 16)
            with e.Fori(1, NIT2) as m:
                for sub in range(2):
                    e.wait_ge(p.sems["gs"], (m * 128 + sub * 64 - 64) + GSB)
                    e.reg_mul(r2, m, 2 * 8 * 2 * D)
                    e.reg_add(r2, r2, (sub - 2) * 8 * 2 * D)
                    e.dma_start(bass.AP(hsc_h, r2, [[D, 2], [2 * D, 8], [1, D]]),
                                bass.AP(hr, 0, [[8 * D, 2], [D, 8], [1, D]])).then_inc(p.inc("gh", 16), 16)
                    e.wait_ge(p.sems["gv"], (m * 112 + sub * 56 - 112) + 56 + GVB)
                    e.reg_mul(r1, m, 2 * 8 * 2 * G3)
                    e.reg_add(r1, r1, sub * 8 * 2 * G3)
                    e.dma_start(bass.AP(xwt, sub * 8 * G3, [[16 * G3, 2], [G3, 8], [1, G3]]),
                                bass.AP(xwsc_h, r1, [[G3, 2], [2 * G3, 8], [1, G3]])).then_inc(p.inc(f"gq{sub}", 16), 16)
            for kk in [NB - 2, NB - 1]:
                e.wait_ge(p.sems["gs"], kk * 64 + 64 + GSB)
                e.dma_start(bass.AP(hsc_h, kk * 8 * 2 * D, [[D, 2], [2 * D, 8], [1, D]]),
                            bass.AP(hr, 0, [[8 * D, 2], [D, 8], [1, D]])).then_inc(p.inc("gh", 16), 16)
            e.free_register(r1)
            e.free_register(r2)
            e.wait_ge(p.sems["gh"], NB * 16)

        p.c["gq0"] = 16 * (NB // 2)
        p.c["gq1"] = 16 * (NB // 2)
        p.c["gh"] = 16 * NB


        @blk.tensor
        def _(e):
            rw = e.alloc_register("t_rw")
            def wge_rt(sem, coef, const, it2):
                e.reg_mul(rw, it2, coef)
                e.reg_add(rw, rw, const)
                e.wait_ge(sem, rw)
            with e.Fori(0, NIT2) as it2:
                for jj in range(16):
                    kbslot = jj // 8
                    wge_rt(p.sems["gv"], 112, 7 * jj + GVB, it2)
                    if jj % 8 == 0:
                        wge_rt(p.sems[f"gq{jj // 8}"], 16, 16, it2)
                    wge_rt(p.sems["gs"], 128, 8 * jj + GSB, it2)
                    for n6 in range(6):
                        inst = None
                        for k in range(KT):
                            inst = e.matmul(phw[n6][:, :], hT[:, 2 * k:2 * k + 2],
                                            whh[k][:, n6 * 512:(n6 + 1) * 512],
                                            start=(k == 0), stop=(k == KT - 1 and n6 >= 4))
                        if n6 < 4:
                            inst = e.matmul(phw[n6][:, :], idnB[0:2, 0:2],
                                            bass.AP(xwt, kbslot * 8 * G3 + (jj % 8) * G3 + n6 * 512,
                                                    [[16 * G3, 2], [1, 512]]),
                                            start=False, stop=True)
                        if n6 == 3:
                            inst.then_inc(p.sems["gt"], 1)
                    inst.then_inc(p.sems["gt"], 1)
                    wge_rt(p.sems["gs"], 128, 8 * jj + 8 + GSB, it2)
                    for k in range(KT):
                        e.transpose(ptrBig[:, 2 * k:2 * k + 2],
                                    bass.AP(hr, (jj % 8) * D + k * 128, [[8 * D, 2], [1, 128]]),
                                    idnB[0:2, 0:2]).then_inc(p.sems["gt"], 1)
            e.free_register(rw)

        p.c["gt"] = GTB + 10 * S

        @blk.scalar
        def _(e):
            rw = e.alloc_register("s_rw")
            def wge_rt(sem, coef, const, it2):
                e.reg_mul(rw, it2, coef)
                e.reg_add(rw, rw, const)
                e.wait_ge(sem, rw)
            with e.Fori(0, NIT2) as it2:
                for jj in range(16):
                    wge_rt(p.sems["gt"], 160, 10 * jj + 1 + GTB, it2)
                    for c in range(4):
                        e.activation(rz[:, c * 512:(c + 1) * 512], phw[c][:, :],
                                     AF.Sigmoid).then_inc(p.sems["gs"], 1)
                    for c in range(2):
                        e.activation(rz2[:, c * 512:(c + 1) * 512], phw[2 + c][:, :],
                                     AF.Sigmoid, scale=-1.0).then_inc(p.sems["gs"], 1)
                    wge_rt(p.sems["gv"], 112, 7 * jj + 3 + GVB, it2)
                    e.activation(nnb[:, :], t4b[:, :], AF.Tanh).then_inc(p.sems["gs"], 1)
                    wge_rt(p.sems["gv"], 112, 7 * jj + 6 + GVB, it2)
                    wge_rt(p.sems["gh"], 32, 16 * (jj // 8), it2)
                    e.activation(bass.AP(hr, (jj % 8) * D, [[8 * D, 2], [1, D]]),
                                 htok[:, (jj % 2) * D:(jj % 2) * D + D],
                                 AF.Copy).then_inc(p.sems["gs"], 1)
            e.free_register(rw)

        p.c["gs"] = GSB + 8 * S

        @blk.vector
        def _(e):
            rw = e.alloc_register("v_rw")
            def wge_rt(sem, coef, const, it2):
                e.reg_mul(rw, it2, coef)
                e.reg_add(rw, rw, const)
                e.wait_ge(sem, rw)
            with e.Fori(0, NIT2) as it2:
                for jj in range(16):
                    kbslot = jj // 8
                    cur = (jj % 2) * D
                    prv = ((jj + 1) % 2) * D
                    wge_rt(p.sems["gs"], 128, 8 * jj + 2 + GSB, it2)
                    wge_rt(p.sems["gt"], 160, 10 * jj + 2 + GTB, it2)
                    e.tensor_tensor(t3b[:, 0:512], rz[:, 0:512], phw[4][:, :], ALU.mult).then_inc(p.sems["gv"], 1)
                    e.tensor_tensor(t3b[:, 512:1024], rz[:, 512:1024], phw[5][:, :], ALU.mult).then_inc(p.sems["gv"], 1)
                    wge_rt(p.sems["gv"], 112, 7 * jj + 2 + GVB, it2)
                    e.tensor_tensor(t4b[:, 0:1024],
                                    bass.AP(xwt, kbslot * 8 * G3 + (jj % 8) * G3 + 2048, [[16 * G3, 2], [1, 1024]]),
                                    t3b[:, 0:1024], ALU.add).then_inc(p.sems["gv"], 1)
                    wge_rt(p.sems["gs"], 128, 8 * jj + 4 + GSB, it2)
                    e.tensor_tensor(ddb[:, :], htok[:, prv:prv + D], rz[:, 1024:2048], ALU.mult).then_inc(p.sems["gv"], 1)
                    wge_rt(p.sems["gs"], 128, 8 * jj + 7 + GSB, it2)
                    e.tensor_tensor(mmb[:, :], nnb[:, :], rz2[:, :], ALU.mult).then_inc(p.sems["gv"], 1)
                    wge_rt(p.sems["gv"], 112, 7 * jj + 5 + GVB, it2)
                    e.tensor_tensor(htok[:, cur:cur + D], mmb[:, :], ddb[:, :], ALU.add).then_inc(p.sems["gv"], 1)
                    wge_rt(p.sems["gt"], 160, 10 * jj + 10 + GTB, it2)
                    e.tensor_copy(hT[:, :], ptrBig[:, :]).then_inc(p.sems["gv"], 1)
            e.free_register(rw)

        p.c["gv"] = GVB + 7 * S
    return

def emit_l1_ffn_tail(p, io):
    nc = p.nc
    cs = io["consts"]
    E, FF = 8, 2048
    NTPC = T // 128

    esC = contextlib.ExitStack()
    with esC:
        sb = lambda n, shp, dt=F32: esC.enter_context(nc.sbuf_tensor(n, shp, dt))
        ps = lambda n, shp, dt=F32: esC.enter_context(nc.psum_tensor(n, shp, dt))
        blk = esC.enter_context(nc.Block())
        exw = [sb(f"l1f_exw{k}", [128, D], BF16) for k in range(KT)]
        STRIP = min(2048, S)
        NSTRIP = T // STRIP
        NTILE = STRIP // 128
        xf = [[sb(f"l1f_xfC{bb}_{k}", [128, STRIP], BF16) for k in range(KT)] for bb in range(2)]
        ex_tok = sb("l1f_extok", [128, D], BF16)
        rt_tok = [sb(f"l1f_rttok{i}", [128, D], BF16) for i in range(2)]
        junk = sb("l1f_junk", [128, 128], F32)
        dot = sb("l1f_dot", [128, 8], F32)
        ssb = sb("l1f_ss", [128, 8], F32)
        seb = sb("l1f_se", [128, 8], F32)
        gn = sb("l1f_gn", [128, 64], F32)
        inv = sb("l1f_inv", [128, 8], F32)
        nr = sb("l1f_nr", [128, 8], F32)
        nrn = sb("l1f_nrn", [128, 8], F32)
        n2 = sb("l1f_n2", [128, 8], F32)
        q8 = sb("l1f_q8", [128, 8], F32)
        pentile = sb("l1f_pentile", [128, 1], F32)
        penacc = sb("l1f_penacc", [128, 1], F32)
        dist = sb("l1f_dist", [128, 8], F32)
        dldp = [sb(f"l1f_dld{i}", [128, 8], F32) for i in range(2)]
        tmp8 = sb("l1f_tmp8", [128, 8], F32)
        tmp1 = sb("l1f_tmp1c", [128, 1], F32)
        onesCf = sb("l1f_onesCf", [1, 128], F32)
        onescol = sb("l1f_onescolC", [128, 1], F32)
        pone = ps("l1f_pone", [128, 512], F32)
        pex = [ps(f"l1f_pex{i}", [128, 512], F32) for i in range(2)]

        @blk.gpsimd
        def _(e):
            e.dma_start(io["exprsc"][:, :], io["expr_w"][:, :]).then_inc(p.inc("ga", 16), 16)

        ga_now = p.c["ga"]

        @blk.sync
        def _(e):
            e.dma_start(onesCf[:1, :], cs["ones1f"][:, :]).then_inc(p.inc("ld", 16), 16)
            e.dma_start(onescol[:, :], cs["onescolf"][:, :]).then_inc(p.inc("ld", 16), 16)

        @blk.scalar
        def _(e):
            p.wge(e, "ga", ga_now)
            for k in range(KT):
                e.dma_start(exw[k][:, :], io["exprsc"][:, k * 128:(k + 1) * 128],
                            transpose=True).then_inc(p.inc("ld", 16), 16)

        ld_all = p.c["ld"]

        @blk.vector
        def _(e):
            e.memset(penacc[:, :], 0.0).then_inc(p.inc("va", 1), 1)

        va_pen = p.c["va"]

        strip_loaded = {}
        strip_consumed = {}
        rt_used = {}
        ex_evac = {}
        extok_used = 0
        dist_stored = 0
        for g in range(NTPC):
            s_, ti = divmod(g, NTILE)
            par = g % 2
            if ti == 0:
                sb_ = s_ % 2
                @blk.scalar
                def _(e, s_=s_, sb_=sb_):
                    if s_ >= 2:
                        p.wge(e, "ta", strip_consumed[s_ - 2])
                    for k in range(KT):
                        e.dma_start(xf[sb_][k][:, :],
                                    io["x_in"][s_ * STRIP:(s_ + 1) * STRIP, k * 128:(k + 1) * 128],
                                    transpose=True).then_inc(p.inc(f"xa{s_ % 2}", 16), 16)
                strip_loaded[s_] = (f"xa{s_ % 2}", p.c[f"xa{s_ % 2}"])

            xs = xf[s_ % 2]
            tsl = slice(ti * 128, (ti + 1) * 128)
            b = g // (S // 128)
            s0 = (g % (S // 128)) * 128

            @blk.sync
            def _(e, g=g, par=par, b=b, s0=s0):
                if g >= 2:
                    p.wge(e, "va", rt_used.get(g - 2, 0))
                e.dma_start(rt_tok[par][:, :],
                            io["hsc"].rearrange("(s bb) c -> s bb c", bb=BPC)[s0:s0 + 128, b, :]).then_inc(p.inc(f"xo{par}", 16), 16)

            xo_now = (f"xo{par}", p.c[f"xo{par}"])

            for n in range(2):
                @blk.tensor
                def _(e, g=g, s_=s_, tsl=tsl, n=n):
                    if g == 0 and n == 0:
                        p.wge(e, "ld", ld_all)
                        p.wge(e, "va", va_pen)
                    if n == 0:
                        p.wge(e, strip_loaded[s_][0], strip_loaded[s_][1])
                    p.wge(e, "sa", ex_evac.get(n, 0))
                    inst = None
                    for k in range(KT):
                        inst = e.matmul(pex[n][:, :], xs[k][:, tsl], exw[k][:, n * 512:(n + 1) * 512],
                                        start=(k == 0), stop=(k == KT - 1))
                    inst.then_inc(p.inc("ta", 1), 1)

                ta_ex = p.c["ta"]

                @blk.scalar
                def _(e, n=n, ta_ex=ta_ex):
                    p.wge(e, "ta", ta_ex)
                    p.wge(e, "va", extok_used)
                    e.activation(ex_tok[:, n * 512:(n + 1) * 512], pex[n][:, :],
                                 AF.Copy).then_inc(p.inc("sa", 1), 1)

                ex_evac[n] = p.c["sa"]

            sa_ex = p.c["sa"]

            @blk.vector
            def _(e, g=g, par=par, sa_ex=sa_ex, xo_now=xo_now):
                p.wge(e, "sa", sa_ex)
                p.wge(e, xo_now[0], xo_now[1])
                for ee in range(E):
                    sl = slice(ee * 128, (ee + 1) * 128)
                    p.bar(e, "va", e.tensor_tensor(junk[:, :], ex_tok[:, sl], rt_tok[par][:, sl], ALU.mult))
                    p.bar(e, "va", e.reduce_sum(dot[:, ee:ee + 1], junk[:, :], axis=mybir.AxisListType.X))
                    p.bar(e, "va", e.tensor_tensor(junk[:, :], rt_tok[par][:, sl], rt_tok[par][:, sl], ALU.mult))
                    p.bar(e, "va", e.reduce_sum(ssb[:, ee:ee + 1], junk[:, :], axis=mybir.AxisListType.X))
                    p.bar(e, "va", e.tensor_tensor(junk[:, :], ex_tok[:, sl], ex_tok[:, sl], ALU.mult))
                    p.bar(e, "va", e.reduce_sum(seb[:, ee:ee + 1], junk[:, :], axis=mybir.AxisListType.X))
                p.bar(e, "va", e.memset(gn[:, :], 0.0))
                for ee in range(E):
                    for ff in range(ee + 1, E):
                        p.bar(e, "va", e.tensor_tensor(junk[:, :], rt_tok[par][:, ee * 128:(ee + 1) * 128],
                                                       rt_tok[par][:, ff * 128:(ff + 1) * 128], ALU.mult))
                        p.bar(e, "va", e.reduce_sum(gn[:, ee * 8 + ff:ee * 8 + ff + 1], junk[:, :],
                                                    axis=mybir.AxisListType.X))

            extok_used = p.c["va"]
            rt_used[g] = p.c["va"]
            va_ttr = p.c["va"]

            @blk.scalar
            def _(e, va_ttr=va_ttr):
                p.wge(e, "va", va_ttr)
                e.activation(nr[:, :], ssb[:, :], AF.Sqrt).then_inc(p.inc("sa", 1), 1)
                e.activation(seb[:, :], seb[:, :], AF.Sqrt).then_inc(p.inc("sa", 1), 1)

            @blk.vector
            def _(e, ds=dist_stored):
                p.wge(e, "sa")
                p.wge(e, "so0", ds)
                p.bar(e, "va", e.tensor_scalar(inv[:, :], nr[:, :], 1e-12, None, ALU.max))
                p.bar(e, "va", e.reciprocal(inv[:, :], inv[:, :]))
                for ee in range(E):
                    for ff in range(ee + 1, E):
                        c0 = ee * 8 + ff
                        p.bar(e, "va", e.tensor_scalar(gn[:, c0:c0 + 1], gn[:, c0:c0 + 1],
                                                       inv[:, ee:ee + 1], inv[:, ff:ff + 1],
                                                       ALU.mult, ALU.mult))
                        e.tensor_copy(gn[:, ff * 8 + ee:ff * 8 + ee + 1], gn[:, c0:c0 + 1]).then_inc(p.inc("va", 1), 1)
                p.wge(e, "va")
                p.bar(e, "va", e.tensor_tensor(gn[:, :], gn[:, :], gn[:, :], ALU.mult))
                for ee in range(E):
                    p.bar(e, "va", e.reduce_sum(n2[:, ee:ee + 1], gn[:, ee * 8:(ee + 1) * 8],
                                                axis=mybir.AxisListType.X))
                p.bar(e, "va", e.tensor_scalar(q8[:, :], n2[:, :], 1e-24, None, ALU.max))
                p.bar(e, "va", e.reciprocal(q8[:, :], q8[:, :]))
                p.bar(e, "va", e.tensor_tensor(q8[:, :], q8[:, :], n2[:, :], ALU.mult))
                p.bar(e, "va", e.reduce_sum(pentile[:, :], q8[:, :], axis=mybir.AxisListType.X))
                p.bar(e, "va", e.tensor_tensor(penacc[:, :], penacc[:, :], pentile[:, :], ALU.add))
                p.bar(e, "va", e.tensor_tensor(dot[:, :], dot[:, :], inv[:, :], ALU.mult))
                p.bar(e, "va", e.tensor_tensor(nrn[:, :], nr[:, :], inv[:, :], ALU.mult))
                p.bar(e, "va", e.tensor_scalar(tmp8[:, :], seb[:, :], 1e-8, None, ALU.max))
                p.bar(e, "va", e.tensor_scalar(nrn[:, :], nrn[:, :], 1e-8, None, ALU.max))
                p.bar(e, "va", e.tensor_tensor(tmp8[:, :], tmp8[:, :], nrn[:, :], ALU.mult))
                p.bar(e, "va", e.reciprocal(tmp8[:, :], tmp8[:, :]))
                p.bar(e, "va", e.tensor_tensor(dot[:, :], dot[:, :], tmp8[:, :], ALU.mult))
                e.tensor_scalar(dist[:, :], dot[:, :], -1.0, 1.0, ALU.mult, ALU.add).then_inc(p.inc("va", 1), 1)

            va_dist = p.c["va"]

            @blk.sync
            def _(e, g=g, va_dist=va_dist):
                p.wge(e, "va", va_dist)
                e.dma_start(io["distsc"][g * 128:(g + 1) * 128, :], dist[:, :]).then_inc(p.inc("so0", 16), 16)

            dist_stored = p.c["so0"]
            if ti == NTILE - 1:
                strip_consumed[s_] = p.c["ta"]

        @blk.tensor
        def _(e):
            p.wge(e, "va")
            e.matmul(pone[0:1, 0:1], penacc[:, :], onescol[:, :], start=True, stop=True,
                     skip_group_check=True).then_inc(p.inc("ta", 1), 1)

        @blk.vector
        def _(e):
            p.wge(e, "ta")
            e.tensor_copy(tmp1[0:1, 0:1], pone[0:1, 0:1]).then_inc(p.inc("va", 1), 1)

        @blk.sync
        def _(e):
            p.wge(e, "va")
            e.dma_start(io["penpart"][:, :], tmp1[0:1, 0:1]).then_inc(p.inc("so1", 16), 16)
            p.wge(e, "so0")
            p.wge(e, "so1")
    return


def emit_l1_ffn_tail2(p, io):
    nc = p.nc
    cs = io["consts"]
    E = 8
    NTPC = T // 128
    esC = contextlib.ExitStack()
    with esC:
        sb = lambda n, shp, dt=F32: esC.enter_context(nc.sbuf_tensor(n, shp, dt))
        ps = lambda n, shp, dt=F32: esC.enter_context(nc.psum_tensor(n, shp, dt))
        blk = esC.enter_context(nc.Block())
        dist = sb("l1t_dist", [128, 8], F32)
        dldp = [sb(f"l1t_dld{i}", [128, 8], F32) for i in range(2)]
        tmp8 = sb("l1t_tmp8", [128, 8], F32)
        q8 = sb("l1t_q8", [128, 8], F32)
        nrn = sb("l1t_nrn", [128, 8], F32)
        n2 = sb("l1t_n2", [128, 8], F32)
        tmp1 = sb("l1t_tmp1c", [128, 1], F32)
        pentile = sb("l1t_pentile", [128, 1], F32)
        pens = sb("l1t_pens", [1, 8], F32)
        onesCf = sb("l1t_onesCf", [1, 128], F32)
        pone = ps("l1t_pone", [128, 512], F32)

        for sname in ["ld", "xo0", "xo1", "xc", "xm", "so1", "ta", "va", "sa", "ga",
                      "xa0", "xa1", "os"]:
            p.sem(sname)

        @blk.sync
        def _(e):
            e.dma_start(onesCf[:1, :], cs["ones1f"][:, :]).then_inc(p.inc("ld", 16), 16)
            if "pens" in io:
                e.dma_start(pens[:1, :], io["pens"][:, :]).then_inc(p.inc("ld", 16), 16)
            else:
                e.dma_start(pens[:1, 0:1], io["penpart"][:, :]).then_inc(p.inc("ld", 16), 16)

        @blk.vector
        def _(e):
            p.wge(e, "ld")
            if "pens" in io:
                p.bar(e, "va", e.reduce_sum(tmp1[0:1, 0:1], pens[0:1, :], axis=mybir.AxisListType.X))
                e.tensor_scalar(tmp1[0:1, 0:1], tmp1[0:1, 0:1], 1.0 / (NCORES * T), 1.0,
                                ALU.mult, ALU.add).then_inc(p.inc("va", 1), 1)
            else:
                e.tensor_scalar(tmp1[0:1, 0:1], pens[0:1, 0:1], 1.0 / T, 1.0,
                                ALU.mult, ALU.add).then_inc(p.inc("va", 1), 1)

        @blk.tensor
        def _(e):
            p.wge(e, "va")
            e.matmul(pone[:, 0:1], onesCf[:1, :], tmp1[0:1, 0:1], start=True, stop=True,
                     skip_group_check=True).then_inc(p.inc("ta", 1), 1)

        @blk.vector
        def _(e):
            p.wge(e, "ta")
            e.tensor_copy(pentile[:, :], pone[:, 0:1]).then_inc(p.inc("va", 1), 1)

        va_pen = p.c["va"]
        rt_used = {}

        for g in range(NTPC):
            par = g % 2

            @blk.sync
            def _(e, g=g, par=par):
                if g >= 2:
                    p.wge(e, "va", rt_used.get(("c2", g - 2), 0))
                e.dma_start(dldp[par][:, :], io["distsc"][g * 128:(g + 1) * 128, :]).then_inc(p.inc(f"xo{par}", 16), 16)

            xo_now = (f"xo{par}", p.c[f"xo{par}"])

            @blk.vector
            def _(e, g=g, par=par, xo_now=xo_now, va_pen=va_pen):
                p.wge(e, xo_now[0], xo_now[1])
                p.wge(e, "va", va_pen)
                p.wge(e, "so1")
                p.bar(e, "va", e.tensor_scalar(dist[:, :], dldp[par][:, :], pentile[:, :], None, ALU.mult))
                p.bar(e, "va", e.reduce_max(tmp1[:, :], dist[:, :], axis=mybir.AxisListType.X))
                p.bar(e, "va", e.tensor_scalar(tmp8[:, :], dist[:, :], tmp1[:, :], None, ALU.is_ge))
                p.bar(e, "va", e.scalar_tensor_tensor(q8[:, :], tmp8[:, :], -1e30, dist[:, :],
                                                      ALU.mult, ALU.add))
                p.bar(e, "va", e.reduce_max(n2[:, 0:1], q8[:, :], axis=mybir.AxisListType.X))
                p.bar(e, "va", e.tensor_scalar(nrn[:, :], q8[:, :], n2[:, 0:1], None, ALU.is_ge))
                e.tensor_tensor(n2[:, 1:2], n2[:, 0:1], tmp1[:, :], ALU.subtract).then_inc(p.inc("va", 1), 1)

            va_t2 = p.c["va"]

            @blk.scalar
            def _(e, va_t2=va_t2):
                p.wge(e, "va", va_t2)
                e.activation(n2[:, 2:3], n2[:, 1:2], AF.Exp).then_inc(p.inc("sa", 1), 1)

            sa_e2 = p.c["sa"]

            @blk.vector
            def _(e, g=g, sa_e2=sa_e2):
                p.wge(e, "sa", sa_e2)
                p.bar(e, "va", e.tensor_scalar(n2[:, 3:4], n2[:, 2:3], 1.0, None, ALU.add))
                p.bar(e, "va", e.reciprocal(n2[:, 3:4], n2[:, 3:4]))
                p.bar(e, "va", e.tensor_tensor(n2[:, 4:5], n2[:, 2:3], n2[:, 3:4], ALU.mult))
                p.bar(e, "va", e.tensor_scalar(tmp8[:, :], tmp8[:, :], n2[:, 3:4], None, ALU.mult))
                p.bar(e, "va", e.tensor_scalar(nrn[:, :], nrn[:, :], n2[:, 4:5], None, ALU.mult))
                e.tensor_tensor(dist[:, :], tmp8[:, :], nrn[:, :], ALU.add).then_inc(p.inc("va", 1), 1)

            va_cw = p.c["va"]
            rt_used[("c2", g)] = p.c["va"]

            @blk.sync
            def _(e, g=g, va_cw=va_cw):
                p.wge(e, "va", va_cw)
                e.dma_start(io["cwsc"][g * 128:(g + 1) * 128, :], dist[:, :]).then_inc(p.inc("so1", 16), 16)

        @blk.sync
        def _(e):
            p.wge(e, "so1")

    emit_l1_ffn_moe(p, io)
    return


def emit_l1_ffn_moe(p, io):
    nc = p.nc
    cs = io["consts"]
    E, FF = 8, 2048
    CH = min(512, T // 4)
    NTT = CH // 128
    QT = (T // 4) // 128
    es = contextlib.ExitStack()
    with es:
        sb = lambda n, shp, dt=F32: es.enter_context(nc.sbuf_tensor(n, shp, dt))
        ps = lambda n, shp, dt=F32: es.enter_context(nc.psum_tensor(n, shp, dt))
        blk = es.enter_context(nc.Block())
        acc = sb("l1f_acc", [128, 16 * D], BF16)
        gw_sb = [sb(f"l1f_gw_{k}", [128, FF], BF16) for k in range(KT)]
        uw_sb = [sb(f"l1f_uw_{k}", [128, FF], BF16) for k in range(KT)]
        dw_sb = [sb(f"l1f_dw_{k}", [128, D], BF16) for k in range(16)]
        xfm = [[sb(f"l1f_xfm{bb}_{k}", [128, CH], BF16) for k in range(KT)] for bb in range(2)]
        zt = sb("l1f_zt", [128, 16 * CH], BF16)
        st_ = sb("l1f_st", [128, CH], F32)
        cwt = sb("l1f_cwt", [128, NTT * 8], F32)
        x0m = sb("l1f_x0m", [128, D], BF16)
        xpre = sb("l1f_xpreM", [128, D], F32)
        lnb = (sb("l1f_statM", [128, 12], F32), sb("l1f_mvM", [128, 2], F32),
               sb("l1f_mnegM", [128, 1], F32), sb("l1f_rstdM", [128, 1], F32),
               sb("l1f_xoutM", [128, D], F32), sb("l1f_epsM", [128, 1], F32))

        gp = [ps(f"l1f_gp{i}", [128, CH], F32) for i in range(2)]
        upp = [ps(f"l1f_up{i}", [128, CH], F32) for i in range(2)]
        dn = [ps(f"l1f_dn{i}", [128, 512], F32) for i in range(2)]

        @blk.vector
        def _(e):
            e.memset(lnb[5][:, :], LN_EPS).then_inc(p.inc("va", 1), 1)

        @blk.gpsimd
        def _(e):
            for ei in range(E):
                e.dma_start(io["gwsc"][ei * FF:(ei + 1) * FF, :], io["gate_w"][ei, :, :]).then_inc(p.inc("ga", 16), 16)
                e.dma_start(io["uwsc"][ei * FF:(ei + 1) * FF, :], io["up_w"][ei, :, :]).then_inc(p.inc("ga", 16), 16)
                e.dma_start(io["dwsc"][ei * D:(ei + 1) * D, :], io["down_w"][ei, :, :]).then_inc(p.inc("ga", 16), 16)
            e.dma_start(io["gwsc"][E * FF:(E + 1) * FF, :], io["shared_gate_w"][:, :]).then_inc(p.inc("ga", 16), 16)
            e.dma_start(io["uwsc"][E * FF:(E + 1) * FF, :], io["shared_up_w"][:, :]).then_inc(p.inc("ga", 16), 16)
            e.dma_start(io["dwsc"][E * D:(E + 1) * D, :], io["shared_down_w"][:, :]).then_inc(p.inc("ga", 16), 16)

        ga_all = p.c["ga"]

        load_done = {}
        wbuf_free = {}

        def emit_wload(idx):
            ei = idx % (E + 1)
            @blk.scalar
            def _(e, ei=ei, idx=idx):
                p.wge(e, "ga", ga_all)
                p.wge(e, "ta", wbuf_free.get(idx - 1, 0))
                for k in range(KT):
                    e.dma_start(gw_sb[k][:, :], io["gwsc"][ei * FF:(ei + 1) * FF, k * 128:(k + 1) * 128],
                                transpose=True).then_inc(p.inc("ld", 16), 16)
                    e.dma_start(uw_sb[k][:, :], io["uwsc"][ei * FF:(ei + 1) * FF, k * 128:(k + 1) * 128],
                                transpose=True).then_inc(p.inc("ld", 16), 16)
                for k in range(16):
                    e.dma_start(dw_sb[k][:, :],
                                io["dwsc"][ei * D:(ei + 1) * D, k * 128:(k + 1) * 128],
                                transpose=True).then_inc(p.inc("ld", 16), 16)
            load_done[idx] = p.c["ld"]

        NCHH = (T // 4) // CH
        st_free = 0
        strip_used = {}
        cw_used = {}
        zt_used = {}
        gp_free = {}
        up_free = {}
        dn_free = {}
        acc_ln_read = 0

        emit_wload(0)

        qctr = 0
        NQTR = 4
        for half in range(NQTR):
            tok0h = half * (T // 4)
            for ei in range(E + 1):
                idx = half * (E + 1) + ei
                for c in range(NCHH):
                    tok0 = tok0h + c * CH
                    spar = qctr % 2

                    @blk.scalar
                    def _(e, tok0=tok0, spar=spar, q=qctr):
                        if q >= 2:
                            p.wge(e, "ta", strip_used.get(q - 2, 0))
                        for k in range(KT):
                            e.dma_start(xfm[spar][k][:, :],
                                        io["x_in"][tok0:tok0 + CH, k * 128:(k + 1) * 128],
                                        transpose=True).then_inc(p.inc(f"xa{spar}", 16), 16)

                    xa_now = (f"xa{spar}", p.c[f"xa{spar}"])

                    if ei < E:
                        @blk.sync
                        def _(e, tok0=tok0, q=qctr):
                            if q >= 1:
                                p.wge(e, "va", cw_used.get(q - 1, 0))
                            e.dma_start(cwt[:, :].rearrange("q (tt c) -> q tt c", tt=NTT),
                                        io["cwsc"][tok0:tok0 + CH, :].rearrange("(tt q) c -> q tt c", q=128)).then_inc(p.inc("xc", 16), 16)
                    xo_now = ("xc", p.c["xc"])

                    for m in range(16):
                        mp = m % 2
                        @blk.tensor
                        def _(e, m=m, mp=mp, spar=spar, idx=idx, q=qctr):
                            if m == 0:
                                p.wge(e, "ld", load_done[idx])
                                p.wge(e, xa_now[0], xa_now[1])
                            p.wge(e, "sa", gp_free.get(mp, 0))
                            p.wge(e, "va", up_free.get(mp, 0))
                            inst = None
                            for k in range(KT):
                                e.matmul(gp[mp][:, :], gw_sb[k][:, m * 128:(m + 1) * 128],
                                         xfm[spar][k][:, :], start=(k == 0), stop=(k == KT - 1))
                            for k in range(KT):
                                inst = e.matmul(upp[mp][:, :], uw_sb[k][:, m * 128:(m + 1) * 128],
                                                xfm[spar][k][:, :], start=(k == 0), stop=(k == KT - 1))
                            inst.then_inc(p.inc("ta", 1), 1)

                        ta_gu = p.c["ta"]

                        @blk.scalar
                        def _(e, m=m, mp=mp, ta_gu=ta_gu):
                            p.wge(e, "ta", ta_gu)
                            p.wge(e, "va", st_free)
                            e.activation(st_[:, :], gp[mp][:, :], AF.Silu).then_inc(p.inc("sa", 1), 1)

                        gp_free[mp] = p.c["sa"]
                        sa_silu = p.c["sa"]

                        @blk.vector
                        def _(e, m=m, mp=mp, sa_silu=sa_silu, q=qctr):
                            p.wge(e, "sa", sa_silu)
                            p.wge(e, "ta", zt_used.get(q - 1, 0) if m == 0 else 0)
                            e.tensor_tensor(zt[:, m * CH:(m + 1) * CH], st_[:, :], upp[mp][:, :],
                                            ALU.mult).then_inc(p.inc("va", 1), 1)

                        up_free[mp] = p.c["va"]
                        st_free = p.c["va"]

                    va_z = p.c["va"]

                    for tt in range(NTT):
                        t_loc = c * NTT + tt
                        for n in range(2):
                            dp = (tt * 2 + n) % 2
                            @blk.tensor
                            def _(e, tt=tt, n=n, dp=dp, idx=idx, va_z=va_z):
                                p.wge(e, "va", va_z)
                                p.wge(e, "va", dn_free.get(dp, 0))
                                inst = None
                                for k2 in range(16):
                                    inst = e.matmul(dn[dp][:, :], zt[:, k2 * CH + tt * 128: k2 * CH + (tt + 1) * 128],
                                                    dw_sb[k2][:, n * 512:(n + 1) * 512],
                                                    start=(k2 == 0), stop=(k2 == 15))
                                inst.then_inc(p.inc("ta", 1), 1)

                            ta_dn = p.c["ta"]

                            @blk.vector
                            def _(e, tt=tt, n=n, dp=dp, ei=ei, t_loc=t_loc, ta_dn=ta_dn, xo_now=xo_now, half=half, aln=acc_ln_read):
                                p.wge(e, "ta", ta_dn)
                                if ei < E:
                                    p.wge(e, xo_now[0], xo_now[1])
                                if ei == 0:
                                    p.wge(e, "va", aln)
                                asl = acc[:, t_loc * D + n * 512: t_loc * D + (n + 1) * 512]
                                if ei == 0:
                                    e.tensor_scalar(asl, dn[dp][:, :], cwt[:, tt * 8 + ei:tt * 8 + ei + 1],
                                                    None, ALU.mult).then_inc(p.inc("va", 1), 1)
                                elif ei < E:
                                    e.scalar_tensor_tensor(asl, dn[dp][:, :], cwt[:, tt * 8 + ei:tt * 8 + ei + 1],
                                                           asl, ALU.mult, ALU.add).then_inc(p.inc("va", 1), 1)
                                else:
                                    e.tensor_tensor(asl, asl, dn[dp][:, :], ALU.add).then_inc(p.inc("va", 1), 1)

                            dn_free[dp] = p.c["va"]

                    cw_used[qctr] = p.c["va"]
                    strip_used[qctr] = p.c["ta"]
                    zt_used[qctr] = p.c["ta"]
                    qctr += 1

                wbuf_free[idx] = p.c["ta"]
                if idx + 1 <= NQTR * (E + 1) - 1:
                    emit_wload(idx + 1)

            va_moe = p.c["va"]
            x0_free = 0
            for tl in range(QT):
                g = half * QT + tl
                @blk.sync
                def _(e, g=g, x0_free=x0_free):
                    if x0_free:
                        p.wge(e, "va", x0_free)
                    e.dma_start(x0m[:, :], io["x_in"][g * 128:(g + 1) * 128, :]).then_inc(p.inc("xm", 16), 16)

                xo_now = p.c["xm"]

                @blk.vector
                def _(e, tl=tl, xo_now=xo_now, va_moe=va_moe):
                    p.wge(e, "va", va_moe)
                    p.wge(e, "xm", xo_now)
                    p.wge(e, "os")
                    e.tensor_tensor(xpre[:, 0:512], x0m[:, 0:512], acc[:, tl * D:tl * D + 512],
                                    ALU.add).then_inc(p.inc("va", 1), 1)
                    e.tensor_tensor(xpre[:, 512:1024], x0m[:, 512:1024], acc[:, tl * D + 512:tl * D + 1024],
                                    ALU.add).then_inc(p.inc("va", 1), 1)

                x0_free = p.c["va"]
                emit_ln_store(p, blk, lnb, xpre, io["x_out"], g * 128)

            acc_ln_read = p.c["va"]

        @blk.sync
        def _(e):
            p.wge(e, "os")
    return



def emit_phase_barrier(nc, p):
    snap = [(n, v) for n, v in p.c.items() if v > 0]
    with nc.Block() as blk:
        for engname in ["tensor", "vector", "scalar", "sync", "gpsimd"]:
            @getattr(blk, engname)
            def _(e, snap=snap):
                for n, v in snap:
                    e.wait_ge(p.sems[n], v)


def build_fused_a_program():
    nc = bass.Bass("TRN2", target_bir_lowering=False)
    G3 = 3072
    consts = declare_consts(nc)

    def dt(name, shape, dtype=F32, kind=None):
        h = nc.dram_tensor(name, shape, dtype, kind=kind) if kind else nc.dram_tensor(name, shape, dtype)
        return h

    io1 = {"consts": consts}
    io1["x_in"] = dt("x_in", [T, D], F32, "ExternalInput").ap()
    for w in ["Wq", "Wk", "Wv", "Wo"]:
        io1[w] = dt("a0_" + w, [D, D], F32, "ExternalInput").ap()
    for b in ["bq", "bk", "bv", "bo"]:
        io1[b] = dt("a0_" + b, [D], F32, "ExternalInput").ap()
    io1["x_out"] = dt("x1sc", [T, D], BF16).ap()
    io1["xbf"] = dt("a0_xbf", [T, D], BF16).ap()
    io1["qsc"] = dt("a0_qsc", [T, D], BF16).ap()
    io1["wsc"] = dt("a0_wsc", [4 * D, D], BF16).ap()

    io2 = {"consts": consts, "x_in": io1["x_out"]}
    io2["gate_W"] = dt("f0_gate_W", [D, 8], F32, "ExternalInput").ap()
    io2["W1"] = dt("f0_W1", [8, D, 512], F32, "ExternalInput").ap()
    io2["b1"] = dt("f0_b1", [8, 512], F32, "ExternalInput").ap()
    io2["W2"] = dt("f0_W2", [8, 512, D], F32, "ExternalInput").ap()
    io2["b2"] = dt("f0_b2", [8, D], F32, "ExternalInput").ap()
    io2["Ws1"] = dt("f0_Ws1", [D, 512], F32, "ExternalInput").ap()
    io2["bs1"] = dt("f0_bs1", [512], F32, "ExternalInput").ap()
    io2["Ws2"] = dt("f0_Ws2", [512, D], F32, "ExternalInput").ap()
    io2["bs2"] = dt("f0_bs2", [D], F32, "ExternalInput").ap()
    io2["route_scale"] = dt("f0_route_scale", [1], F32, "ExternalInput").ap()
    io2["x_out"] = dt("x2sc", [T, D], BF16).ap()

    io3 = {"consts": consts, "x_in": io2["x_out"]}
    for w in ["Wk", "Wv", "Wr", "Wo"]:
        io3[w] = dt("a1_" + w, [D, D], F32, "ExternalInput").ap()
    io3["time_decay"] = dt("a1_time_decay", [16, 64], F32, "ExternalInput").ap()
    io3["time_first"] = dt("a1_time_first", [16, 64], F32, "ExternalInput").ap()
    io3["x_out"] = dt("x3_out", [T, D], BF16).ap()
    for wsc in ["wksc", "wvsc", "wrsc", "wosc"]:
        io3[wsc] = dt("a1_" + wsc, [D, D], BF16).ap()

    io4 = {"consts": consts, "x_in": io3["x_out"]}
    io4["gru_w_ih"] = dt("f1_gru_w_ih", [G3, D], F32, "ExternalInput").ap()
    io4["gru_w_hh"] = dt("f1_gru_w_hh", [G3, D], F32, "ExternalInput").ap()
    io4["expr_w"] = dt("f1_expr_w", [D, D], F32, "ExternalInput").ap()
    io4["ctx_w"] = dt("f1_ctx_w", [D, D], F32, "ExternalInput").ap()
    io4["ctx_b"] = dt("f1_ctx_b", [D], F32, "ExternalInput").ap()
    io4["distsc"] = dt("distsc", [T, 8], F32).ap()
    io4["penpart"] = dt("penpart", [1, 1], F32).ap()
    io4["wihsc"] = dt("f1_wihsc", [G3, D], BF16).ap()
    io4["whhsc"] = dt("f1_whhsc", [G3, D], BF16).ap()
    io4["ctxsc"] = dt("f1_ctxsc", [D, D], BF16).ap()
    io4["exprsc"] = dt("f1_exprsc", [D, D], BF16).ap()
    xwsc_h = nc.dram_tensor("f1_xwsc", [S * BPC, G3], BF16)
    io4["xwsc"] = xwsc_h.ap()
    io4["xwsc_h"] = xwsc_h
    hsc_h = nc.dram_tensor("f1_hsc", [S * BPC, D], BF16)
    io4["hsc"] = hsc_h.ap()
    io4["hsc_h"] = hsc_h

    io5 = {"consts": consts, "x_in": io3["x_out"],
           "distsc": io4["distsc"], "penpart": io4["penpart"]}
    io5["gate_w"] = dt("f1_gate_w", [8, 2048, D], F32, "ExternalInput").ap()
    io5["up_w"] = dt("f1_up_w", [8, 2048, D], F32, "ExternalInput").ap()
    io5["down_w"] = dt("f1_down_w", [8, D, 2048], F32, "ExternalInput").ap()
    io5["shared_gate_w"] = dt("f1_shared_gate_w", [2048, D], F32, "ExternalInput").ap()
    io5["shared_up_w"] = dt("f1_shared_up_w", [2048, D], F32, "ExternalInput").ap()
    io5["shared_down_w"] = dt("f1_shared_down_w", [D, 2048], F32, "ExternalInput").ap()
    io5["x_out"] = dt("x_out", [T, D], F32, "ExternalOutput").ap()
    io5["cwsc"] = dt("f1_cwsc", [T, 8], F32).ap()
    io5["gwsc"] = dt("f1_gwsc", [9 * 2048, D], BF16).ap()
    io5["uwsc"] = dt("f1_uwsc", [9 * 2048, D], BF16).ap()
    io5["dwsc"] = dt("f1_dwsc", [9 * D, 2048], BF16).ap()

    es = contextlib.ExitStack()
    with es:
        p = Prog(nc, es)
        for emit, io_ in [(emit_l0_attn, io1), (emit_l0_ffn, io2),
                          (emit_l1_attn, io3), (emit_l1_ffn, io4),
                          (emit_l1_ffn_tail2, io5)]:
            with contextlib.ExitStack() as eph:
                p.es = eph
                emit(p, io_)
            if emit is not emit_l1_ffn_tail2:
                emit_phase_barrier(nc, p)
        p.es = es
    return nc


def build_l0_attn_program():
    nc = bass.Bass("TRN2", target_bir_lowering=False)
    io = {}
    io["x_in"] = nc.dram_tensor("x_in", [T, D], F32, kind="ExternalInput").ap()
    for w in ["Wq", "Wk", "Wv", "Wo"]:
        io[w] = nc.dram_tensor(w, [D, D], F32, kind="ExternalInput").ap()
    for b in ["bq", "bk", "bv", "bo"]:
        io[b] = nc.dram_tensor(b, [D], F32, kind="ExternalInput").ap()
    io["x_out"] = nc.dram_tensor("x_out", [T, D], BF16, kind="ExternalOutput").ap()
    io["xbf"] = nc.dram_tensor("xbf", [T, D], BF16).ap()
    io["qsc"] = nc.dram_tensor("qsc", [T, D], BF16).ap()
    io["wsc"] = nc.dram_tensor("wsc", [4 * D, D], BF16).ap()
    io["consts"] = declare_consts(nc)
    es = contextlib.ExitStack()
    with es:
        p = Prog(nc, es)
        emit_l0_attn(p, io)
    return nc


def build_l1_attn_program():
    nc = bass.Bass("TRN2", target_bir_lowering=False)
    io = {}
    io["x_in"] = nc.dram_tensor("x_in", [T, D], BF16, kind="ExternalInput").ap()
    for w in ["Wk", "Wv", "Wr", "Wo"]:
        io[w] = nc.dram_tensor(w, [D, D], F32, kind="ExternalInput").ap()
    io["time_decay"] = nc.dram_tensor("time_decay", [16, 64], F32, kind="ExternalInput").ap()
    io["time_first"] = nc.dram_tensor("time_first", [16, 64], F32, kind="ExternalInput").ap()
    io["x_out"] = nc.dram_tensor("x_out", [T, D], BF16, kind="ExternalOutput").ap()
    for wsc in ["wksc", "wvsc", "wrsc", "wosc"]:
        io[wsc] = nc.dram_tensor(wsc, [D, D], BF16).ap()
    io["consts"] = declare_consts(nc)
    es = contextlib.ExitStack()
    with es:
        p = Prog(nc, es)
        emit_l1_attn(p, io)
    return nc


def build_l1_ffn_a_program():
    nc = bass.Bass("TRN2", target_bir_lowering=False)
    G3 = 3072
    io = {}
    io["x_in"] = nc.dram_tensor("x_in", [T, D], BF16, kind="ExternalInput").ap()
    io["gru_w_ih"] = nc.dram_tensor("gru_w_ih", [G3, D], F32, kind="ExternalInput").ap()
    io["gru_w_hh"] = nc.dram_tensor("gru_w_hh", [G3, D], F32, kind="ExternalInput").ap()
    io["expr_w"] = nc.dram_tensor("expr_w", [D, D], F32, kind="ExternalInput").ap()
    io["ctx_w"] = nc.dram_tensor("ctx_w", [D, D], F32, kind="ExternalInput").ap()
    io["ctx_b"] = nc.dram_tensor("ctx_b", [D], F32, kind="ExternalInput").ap()
    io["distsc"] = nc.dram_tensor("distsc", [T, 8], F32, kind="ExternalOutput").ap()
    io["penpart"] = nc.dram_tensor("penpart", [1, 1], F32, kind="ExternalOutput").ap()
    G3 = 3072
    io["wihsc"] = nc.dram_tensor("wihsc", [G3, D], BF16).ap()
    io["whhsc"] = nc.dram_tensor("whhsc", [G3, D], BF16).ap()
    io["ctxsc"] = nc.dram_tensor("ctxsc", [D, D], BF16).ap()
    io["exprsc"] = nc.dram_tensor("exprsc", [D, D], BF16).ap()
    xwsc_h = nc.dram_tensor("xwsc", [S * BPC, G3], BF16)
    io["xwsc"] = xwsc_h.ap()
    io["xwsc_h"] = xwsc_h
    hsc_h = nc.dram_tensor("hsc", [S * BPC, D], BF16)
    io["hsc"] = hsc_h.ap()
    io["hsc_h"] = hsc_h
    io["consts"] = declare_consts(nc)
    es = contextlib.ExitStack()
    with es:
        p = Prog(nc, es)
        emit_l1_ffn(p, io)
    return nc


def build_l1_ffn_b_program():
    nc = bass.Bass("TRN2", target_bir_lowering=False)
    io = {}
    io["x_in"] = nc.dram_tensor("x_in", [T, D], BF16, kind="ExternalInput").ap()
    io["distsc"] = nc.dram_tensor("distsc", [T, 8], F32, kind="ExternalInput").ap()
    io["pens"] = nc.dram_tensor("pens", [1, 8], F32, kind="ExternalInput").ap()
    io["gate_w"] = nc.dram_tensor("gate_w", [8, 2048, D], F32, kind="ExternalInput").ap()
    io["up_w"] = nc.dram_tensor("up_w", [8, 2048, D], F32, kind="ExternalInput").ap()
    io["down_w"] = nc.dram_tensor("down_w", [8, D, 2048], F32, kind="ExternalInput").ap()
    io["shared_gate_w"] = nc.dram_tensor("shared_gate_w", [2048, D], F32, kind="ExternalInput").ap()
    io["shared_up_w"] = nc.dram_tensor("shared_up_w", [2048, D], F32, kind="ExternalInput").ap()
    io["shared_down_w"] = nc.dram_tensor("shared_down_w", [D, 2048], F32, kind="ExternalInput").ap()
    io["x_out"] = nc.dram_tensor("x_out", [T, D], F32, kind="ExternalOutput").ap()
    io["cwsc"] = nc.dram_tensor("cwsc", [T, 8], F32).ap()
    io["gwsc"] = nc.dram_tensor("gwsc", [9 * 2048, D], BF16).ap()
    io["uwsc"] = nc.dram_tensor("uwsc", [9 * 2048, D], BF16).ap()
    io["dwsc"] = nc.dram_tensor("dwsc", [9 * D, 2048], BF16).ap()
    io["consts"] = declare_consts(nc)
    es = contextlib.ExitStack()
    with es:
        p = Prog(nc, es)
        emit_l1_ffn_tail2(p, io)
    return nc


_CACHED = {}


def _get_prog(name, builder):
    if name not in _CACHED:
        _CACHED[name] = builder()
    return _CACHED[name]


def kernel(**inputs):
    assert S == 4096 and T == 8192, "kernel() requires full dims"
    consts = host_consts()
    x = np.ascontiguousarray(np.asarray(inputs["L0_attn_x"], dtype=np.float32))
    shards = [np.ascontiguousarray(x[2 * c:2 * c + 2].reshape(T, D)) for c in range(NCORES)]

    def amap(names, src_prefix, dst_prefix):
        return {dst_prefix + k: np.ascontiguousarray(np.asarray(inputs[src_prefix + k], dtype=np.float32))
                for k in names}

    ncA = _get_prog("fusedall", build_fused_a_program)
    wA = {}
    wA.update(amap(["Wq", "bq", "Wk", "bk", "Wv", "bv", "Wo", "bo"], "L0_attn_", "a0_"))
    wA.update(amap(["gate_W", "W1", "b1", "W2", "b2", "Ws1", "bs1", "Ws2", "bs2"], "L0_ffn_", "f0_"))
    wA["f0_route_scale"] = np.asarray(inputs["L0_ffn_route_scale"], dtype=np.float32).reshape(1)
    wA.update(amap(["Wk", "Wv", "Wr", "Wo", "time_decay", "time_first"], "L1_attn_", "a1_"))
    wA.update(amap(["gru_w_ih", "gru_w_hh", "expr_w", "ctx_w", "ctx_b",
                    "gate_w", "up_w", "down_w",
                    "shared_gate_w", "shared_up_w", "shared_down_w"], "L1_ffn_", "f1_"))
    in_maps = []
    for c in range(NCORES):
        m = {"x_in": shards[c]}
        m.update(wA)
        m.update(consts)
        in_maps.append(m)
    res = run_bass_kernel_spmd(ncA, in_maps, core_ids=list(range(NCORES)))
    outs = [res.results[c]["x_out"] for c in range(NCORES)]
    return np.concatenate([o.reshape(BPC, S, D) for o in outs], axis=0)


def build_l0_ffn_program():
    nc = bass.Bass("TRN2", target_bir_lowering=False)
    io = {}
    io["x_in"] = nc.dram_tensor("x_in", [T, D], BF16, kind="ExternalInput").ap()
    io["gate_W"] = nc.dram_tensor("gate_W", [D, 8], F32, kind="ExternalInput").ap()
    io["W1"] = nc.dram_tensor("W1", [8, D, 512], F32, kind="ExternalInput").ap()
    io["b1"] = nc.dram_tensor("b1", [8, 512], F32, kind="ExternalInput").ap()
    io["W2"] = nc.dram_tensor("W2", [8, 512, D], F32, kind="ExternalInput").ap()
    io["b2"] = nc.dram_tensor("b2", [8, D], F32, kind="ExternalInput").ap()
    io["Ws1"] = nc.dram_tensor("Ws1", [D, 512], F32, kind="ExternalInput").ap()
    io["bs1"] = nc.dram_tensor("bs1", [512], F32, kind="ExternalInput").ap()
    io["Ws2"] = nc.dram_tensor("Ws2", [512, D], F32, kind="ExternalInput").ap()
    io["bs2"] = nc.dram_tensor("bs2", [D], F32, kind="ExternalInput").ap()
    io["route_scale"] = nc.dram_tensor("route_scale", [1], F32, kind="ExternalInput").ap()
    io["x_out"] = nc.dram_tensor("x_out", [T, D], BF16, kind="ExternalOutput").ap()
    io["consts"] = declare_consts(nc)
    es = contextlib.ExitStack()
    with es:
        p = Prog(nc, es)
        emit_l0_ffn(p, io)
    return nc
